# version 16
# baseline (speedup 1.0000x reference)
import numpy as np
import ml_dtypes
import concourse.bass as bass
import concourse.mybir as mybir
from concourse.alu_op_type import AluOpType as OP
from concourse.bass_utils import run_bass_kernel_spmd

F32 = mybir.dt.float32
BF16 = mybir.dt.bfloat16
AF = mybir.ActivationFunctionType

P = 128
CC = 16
C = 2048
NCORES = 8
HALO = 128
EPS = 1e-5
INTER = 8192

ENGINES = ("tensor", "vector", "scalar", "gpsimd", "sync")
OWN_SEM = {"tensor": "P", "vector": "V", "scalar": "A", "gpsimd": "G", "sync": None}


def segs(n, cap=512):
    out = []
    o = 0
    while o < n:
        s = min(cap, n - o)
        out.append((o, s))
        o += s
    return out


class Res:

    def __init__(self, name):
        self.name = name
        self.writer = None
        self.readers = []


class Sched:

    def __init__(self, nc):
        self.nc = nc
        self.ops = []
        self.last_eng = {}
        self.barrier_idx = None

    def barrier(self):
        deps = [i for i in self.last_eng.values()]
        idx = self.op("vector", lambda e: e.engine_nop(), waits_ops=deps)
        self.barrier_idx = idx
        return idx

    def op(self, eng, fns, reads=(), writes=(), waits_ops=(), dma=False,
           dma_sem=None):
        if not isinstance(fns, (list, tuple)):
            fns = [fns]
        idx = len(self.ops)
        deps = set(waits_ops)
        rnames = [r.name for r in reads]
        wnames = [r.name for r in writes]
        for r in reads:
            if r.writer is not None:
                deps.add(r.writer)
            r.readers.append(idx)
        for r in writes:
            if r.writer is not None:
                deps.add(r.writer)
            deps.update(r.readers)
            r.writer = idx
            r.readers = []
        if dma:
            sem = dma_sem or ("dG" if eng == "gpsimd" else "dS")
        else:
            sem = OWN_SEM[eng]
        self.ops.append(dict(eng=eng, fns=list(fns), sem=sem, dma=dma,
                             deps=deps, idx=idx, rnames=rnames, wnames=wnames))
        self.last_eng[eng] = idx
        return idx

    def emit(self, sems):
        ops = self.ops
        needed = set()
        for o in ops:
            own = OWN_SEM[o["eng"]]
            for d in o["deps"]:
                dop = ops[d]
                if (not dop["dma"]) and dop["sem"] is not None and dop["sem"] == own:
                    continue
                needed.add(d)
        inc_count = {}
        op_val = {}
        for o in ops:
            s = o["sem"]
            if o["dma"]:
                inc_count[s] = inc_count.get(s, 0) + 16
                op_val[o["idx"]] = (s, inc_count[s])
            elif o["idx"] in needed and s is not None:
                inc_count[s] = inc_count.get(s, 0) + 1
                op_val[o["idx"]] = (s, inc_count[s])
            else:
                op_val[o["idx"]] = None
        per_eng = {e: [] for e in ENGINES}
        hw = {e: {} for e in ENGINES}
        for o in ops:
            own = OWN_SEM[o["eng"]]
            wmap = {}
            for d in sorted(o["deps"]):
                dop = ops[d]
                if (not dop["dma"]) and dop["sem"] is not None and dop["sem"] == own:
                    continue
                v = op_val[d]
                if v is None:
                    raise RuntimeError("dep without inc")
                s, val = v
                if wmap.get(s, 0) < val:
                    wmap[s] = val
            waits = []
            for s, val in sorted(wmap.items()):
                if hw[o["eng"]].get(s, 0) >= val:
                    continue
                hw[o["eng"]][s] = val
                waits.append((s, val))
            inc = op_val[o["idx"]]
            myinc = None
            if o["dma"]:
                myinc = (inc[0], 16)
            elif inc is not None:
                myinc = (inc[0], 1)
            per_eng[o["eng"]].append((waits, o["fns"], myinc, o))
        self._final_counts = dict(inc_count)

        audit = getattr(self, "audit", None)

        def runner(items):
            def go(e):
                for waits, fns, myinc, o in items:
                    insts = []
                    for s, val in waits:
                        e.wait_ge(sems[s], val)
                    last = None
                    for fn in fns:
                        last = fn(e)
                        insts.append(last)
                    if myinc is not None:
                        last.then_inc(sems[myinc[0]], myinc[1])
                    if audit is not None:
                        o["_insts"] = insts
            return go

        with self.nc.Block() as blk:
            for eng in ENGINES:
                getattr(blk, eng)(runner(per_eng[eng]))


def build_program(TOWN=1024, dbg=False):
    TH = TOWN + HALO
    SW = TH // 2
    BW = SW + 1
    assert TH % 2 == 0 and SW > HALO
    OW0 = SW - HALO

    nc = bass.Bass("TRN2", target_bir_lowering=False,
                   detect_race_conditions=False)

    d_xh0 = nc.dram_tensor("x_h0", [P, CC, SW], F32, kind="ExternalInput")
    d_xh1 = nc.dram_tensor("x_h1", [P, CC, SW + 1], F32, kind="ExternalInput")
    d_xres = nc.dram_tensor("x_res", [P, CC, TOWN], F32, kind="ExternalInput")
    d_w = {}
    for nm in ("wk", "wv", "wr", "wg", "wo"):
        d_w[nm] = nc.dram_tensor(nm, [8, P, CC, 256], F32, kind="ExternalInput")
    d_wgu_g = nc.dram_tensor("wgu_g", [32, P, CC, 256], F32, kind="ExternalInput")
    d_wgu_u = nc.dram_tensor("wgu_u", [32, P, CC, 256], F32, kind="ExternalInput")
    d_wd = nc.dram_tensor("wd", [8, CC, P, 8, P], F32, kind="ExternalInput")
    d_vec = {}
    for nm in ("tmk", "tmv", "tmr", "tmg", "td", "tf", "lng", "lnb"):
        d_vec[nm] = nc.dram_tensor(nm, [P, CC], F32, kind="ExternalInput")
    d_mask = nc.dram_tensor("mask", [P, 1], F32, kind="ExternalInput")
    d_out = nc.dram_tensor("out", [P, CC, TOWN], F32, kind="ExternalOutput")

    from contextlib import ExitStack
    stack = ExitStack()
    sems = {}
    for s in ("dG", "dS", "dW", "P", "V", "A", "G"):
        sems[s] = stack.enter_context(nc.semaphore(s))

    sc = Sched(nc)

    def sb(name, shape, dt):
        return stack.enter_context(nc.sbuf_tensor(name, shape, dt))

    vecs = {nm: sb("sb_" + nm, [P, CC], F32) for nm in d_vec}
    maskt = sb("maskt", [P, 1], F32)
    wfm = sb("wfm", [P, CC], F32)
    euf = sb("euf", [P, CC], F32)
    s2f = sb("s2f", [P, CC], F32)
    e1f = sb("e1f", [P, CC], F32)
    onesm = sb("onesm", [P, 1], BF16)
    onesr = sb("onesr", [1, P], F32)
    stA = sb("stA", [P, CC], F32)
    stB = sb("stB", [P, CC], F32)
    ag = sb("ag", [P, CC * TOWN], BF16)
    SL = [sb("SL0", [P, CC * 256], BF16), sb("SL1", [P, CC * 256], BF16)]

    R = {}

    def res(name):
        if name not in R:
            R[name] = Res(name)
            R[name].writer = sc.barrier_idx
        return R[name]

    def fence():
        idx = sc.barrier()
        for r in R.values():
            r.writer = idx
            r.readers = []
        return idx

    def cres(buf, cc):
        return res("%s_%d" % (buf, cc))

    def all_res(buf):
        return [cres(buf, i) for i in range(CC)]

    def dma(eng, dst_ap, src_ap, reads=(), writes=(), waits=(), dma_sem=None):
        return sc.op(eng, lambda e, d=dst_ap, s=src_ap: e.dma_start(d, s),
                     reads=reads, writes=writes, waits_ops=waits, dma=True,
                     dma_sem=dma_sem)

    def act(out_ap, in_ap, func, bias=0.0, scale=1.0, reads=(), writes=(), waits=()):
        return sc.op("scalar",
                     lambda e, o=out_ap, i=in_ap, f=func, b=bias, s=scale:
                     e.activation(o, i, f, bias=b, scale=s),
                     reads=reads, writes=writes, waits_ops=waits)

    def vop(fn, reads=(), writes=(), waits=()):
        return sc.op("vector", fn, reads=reads, writes=writes, waits_ops=waits)

    slab_seq = [0]
    slab_pos = {}

    def load_slab_one(dram, key=None):
        k = key if key is not None else id(dram)
        s = slab_pos.get(k, 0)
        slab_pos[k] = s + 1
        par = slab_seq[0] % 2
        slab_seq[0] += 1
        t = dma("gpsimd",
                SL[par][:, :].rearrange("p (c w) -> p c w", w=256),
                dram.ap()[s],
                writes=[res("SL%d" % par)])
        return t, par

    for nm in d_vec:
        dma("sync", vecs[nm][:, :], d_vec[nm].ap(), writes=[res("vec_" + nm)])
    dma("sync", maskt[:, :], d_mask.ap(), writes=[res("maskt")])
    vop(lambda e: e.memset(onesm[:, :], 1.0 / 2048.0), writes=[res("onesm")])
    vop(lambda e: e.memset(onesr[:, :], 1.0), writes=[res("onesr")])
    act(e1f[:, :], vecs["td"][:, :], AF.Exp,
        reads=[res("vec_td")], writes=[res("e1f")])
    act(euf[:, :], vecs["tf"][:, :], AF.Exp,
        reads=[res("vec_tf")], writes=[res("euf")])
    vop(lambda e: e.memset(wfm[:, 0:1], 1.0), writes=[res("wfm")])
    for fn in (AF.Square, AF.Sigmoid, AF.Silu, AF.Ln, AF.Exp):
        act(wfm[:, 0:1], wfm[:, 0:1], fn, writes=[res("wfm")])
    act(wfm[:, :], e1f[:, :], AF.Exp, scale=-1.0,
        reads=[res("e1f")], writes=[res("wfm")])
    vop(lambda e: e.tensor_mul(s2f[:, :], euf[:, :], wfm[:, :]),
        reads=[res("euf"), res("wfm")], writes=[res("s2f")])
    vop(lambda e: e.tensor_scalar(s2f[:, :], s2f[:, :], -1.0, 1.0,
                                  OP.mult, OP.add), writes=[res("s2f")])
    fence()

    with ExitStack() as hstack:
        def hsb(name, shape, dt):
            return hstack.enter_context(nc.sbuf_tensor(name, shape, dt))

        def hps(name, shape, dt=F32):
            return hstack.enter_context(nc.psum_tensor(name, shape, dt))

        Lb = hsb("Lb", [P, CC * BW], BF16)
        Eb = hsb("Eb", [P, CC * BW], BF16)
        Vb = hsb("Vb", [P, CC * BW], BF16)
        M1 = hsb("M1", [P, CC * BW], BF16)
        M2 = hsb("M2", [P, CC * BW], BF16)
        R1 = hsb("R1", [P, CC * BW], BF16)
        R3 = hsb("R3", [P, CC * BW], BF16)
        tmix = hsb("tmix", [P, BW], BF16)
        tf1 = [hsb("tf1a", [P, BW], F32), hsb("tf1b", [P, BW], F32)]
        tf2 = [hsb("tf2a", [P, BW], F32), hsb("tf2b", [P, BW], F32)]
        tf3 = hsb("tf3a", [P, BW], F32)
        mub_h = hsb("mub_h", [P, BW], F32)
        rsb_h = hsb("rsb_h", [P, BW], F32)
        hmu_row = hsb("hmu_row", [1, BW], F32)
        hs2_row = hsb("hs2_row", [1, BW], F32)
        hrs_row = hsb("hrs_row", [1, BW], F32)
        pp = [hps("pp0", [P, BW], F32), hps("pp1", [P, BW], F32),
              hps("pp2", [P, BW], F32)]
        ps_mu = pp[0]
        ps_s2 = pp[1]

        def chunk(buf, cc, lo, n, stride=BW):
            return buf[:, cc * stride + lo: cc * stride + lo + n]

        def halo_view(buf):
            return buf[:, :].rearrange("p (c w) -> p c w", w=BW)[:, :, 0:HALO]

        def col_view(buf, col):
            return buf[:, :].rearrange("p (c w) -> p c w", w=BW)[:, :, col:col + 1]

        if dbg:
            for buf, bres in ((Lb, "Lb"), (Eb, "Eb"), (Vb, "Vb"),
                              (M1, "M1"), (M2, "M2"), (R1, "R1"), (R3, "R3")):
                vop(lambda e, buf=buf: e.memset(buf[:, :], 0.0),
                    writes=[res(bres)] + all_res(bres))

        def ln_stats_apply(ln_w, xbf_res, sq_res, xbuf, sqbuf, outbuf):
            xv3 = xbuf[:, :].rearrange("p (c w) -> p c w", w=BW)[:, :, 0:ln_w]
            sv3 = sqbuf[:, :].rearrange("p (c w) -> p c w", w=BW)[:, :, 0:ln_w]
            act(sv3, xv3, AF.Square, reads=all_res(xbf_res),
                writes=all_res(sq_res))
            for stat, (srcb, sres, psum, pres) in enumerate(
                    ((xbuf, xbf_res, ps_mu, "ps_mu"),
                     (sqbuf, sq_res, ps_s2, "ps_s2"))):
                for (o, n) in segs(ln_w, 512):
                    fns = []
                    for cci in range(CC):
                        fns.append(lambda e, srcb=srcb, psum=psum, o=o, n=n, cci=cci:
                                   e.matmul(psum[0:1, o:o + n], onesm[:, 0:1],
                                            chunk(srcb, cci, o, n),
                                            start=(cci == 0), stop=(cci == CC - 1)))
                    sc.op("tensor", fns, reads=all_res(sres) + [res("onesm")],
                          writes=[res("pp0" if pres == "ps_mu" else "pp1")])
            vop(lambda e: e.tensor_copy(hmu_row[0:1, 0:ln_w], ps_mu[0:1, 0:ln_w]),
                reads=[res("pp0")], writes=[res("hmu_row")])
            vop(lambda e: e.tensor_copy(hs2_row[0:1, 0:ln_w], ps_s2[0:1, 0:ln_w]),
                reads=[res("pp1")], writes=[res("hs2_row")])
            vop(lambda e: e.tensor_mul(hrs_row[0:1, 0:ln_w], hmu_row[0:1, 0:ln_w],
                                       hmu_row[0:1, 0:ln_w]),
                reads=[res("hmu_row")], writes=[res("hrs_row")])
            vop(lambda e: e.tensor_sub(hs2_row[0:1, 0:ln_w], hs2_row[0:1, 0:ln_w],
                                       hrs_row[0:1, 0:ln_w]),
                reads=[res("hrs_row")], writes=[res("hs2_row")])
            vop(lambda e: e.tensor_single_scalar(
                hs2_row[0:1, 0:ln_w], hs2_row[0:1, 0:ln_w], EPS, OP.add),
                writes=[res("hs2_row")])
            act(hs2_row[0:1, 0:ln_w], hs2_row[0:1, 0:ln_w], AF.Ln,
                writes=[res("hs2_row")])
            act(hrs_row[0:1, 0:ln_w], hs2_row[0:1, 0:ln_w], AF.Exp, scale=-0.5,
                reads=[res("hs2_row")], writes=[res("hrs_row")])
            for row, rres, dstb, dres, psel in (
                    (hmu_row, "hmu_row", mub_h, "mub_h", 0),
                    (hrs_row, "hrs_row", rsb_h, "rsb_h", 1)):
                fns = []
                for (o, n) in segs(ln_w, 512):
                    fns.append(lambda e, row=row, o=o, n=n, psel=psel:
                               e.matmul(pp[psel][0:128, o:o + n], onesr[0:1, 0:P],
                                        row[0:1, o:o + n], start=True, stop=True))
                sc.op("tensor", fns, reads=[res(rres), res("onesr")],
                      writes=[res("pp%d" % psel)])
                vop(lambda e, dstb=dstb, psel=psel, ln_w=ln_w:
                    e.tensor_copy(dstb[:, 0:ln_w], pp[psel][:, 0:ln_w]),
                    reads=[res("pp%d" % psel)], writes=[res(dres)])
            for cci in range(CC):
                vop(lambda e, cci=cci, ln_w=ln_w: e.tensor_sub(
                    tf1[0][:, 0:ln_w], chunk(xbuf, cci, 0, ln_w),
                    mub_h[:, 0:ln_w]),
                    reads=[cres(xbf_res, cci), res("mub_h")],
                    writes=[res("tf1_0")])
                vop(lambda e, cci=cci, ln_w=ln_w: e.tensor_mul(
                    chunk(outbuf, cci, 0, ln_w), tf1[0][:, 0:ln_w],
                    rsb_h[:, 0:ln_w]),
                    reads=[res("tf1_0"), res("rsb_h")],
                    writes=[cres("Lb", cci)])

        def mix_into(dstbuf, dres, tmname, lnoff, lo, n, h):
            for cci in range(CC):
                if h == 0 and lo == 0:
                    vop(lambda e: e.memset(tmix[:, 0:1], 0.0),
                        writes=[res("tmix")])
                    vop(lambda e, cci=cci, n=n: e.tensor_sub(
                        tmix[:, 1:n], chunk(Lb, cci, 0, n - 1),
                        chunk(Lb, cci, 1, n - 1)),
                        reads=[res("Lb")], writes=[res("tmix")])
                else:
                    off = lnoff + lo
                    vop(lambda e, cci=cci, off=off, n=n: e.tensor_sub(
                        tmix[:, 0:n], chunk(Lb, cci, off - 1, n),
                        chunk(Lb, cci, off, n)),
                        reads=[res("Lb")], writes=[res("tmix")])
                vop(lambda e, cci=cci, tmname=tmname, dstbuf=dstbuf,
                    off=lnoff + lo, n=n:
                    e.scalar_tensor_tensor(
                        chunk(dstbuf, cci, 0, n), tmix[:, 0:n],
                        vecs[tmname][:, cci:cci + 1],
                        chunk(Lb, cci, off, n), OP.mult, OP.add),
                    reads=[res("tmix"), res("vec_" + tmname), cres("Lb", cci)],
                    writes=[cres(dres, cci)])

        def projection(wname, mixbuf, mres, width, evac_fn):
            for s in range(8):
                t_slab, par = load_slab_one(d_w[wname], key=wname)
                for osub in range(2):
                    oc = s * 2 + osub
                    psel = oc % 3
                    fns = []
                    for cci in range(CC):
                        for (o, n) in segs(width, 512):
                            fns.append(lambda e, par=par, cci=cci, osub=osub,
                                       o=o, n=n, psel=psel, mixbuf=mixbuf:
                                       e.matmul(pp[psel][:, o:o + n],
                                                SL[par][:, cci * 256 + osub * P:
                                                        cci * 256 + osub * P + P],
                                                chunk(mixbuf, cci, o, n),
                                                start=(cci == 0),
                                                stop=(cci == CC - 1)))
                    sc.op("tensor", fns,
                          reads=[res("SL%d" % par)] + all_res(mres),
                          writes=[res("pp%d" % psel)], waits_ops=(t_slab,))
                    evac_fn(oc, psel)

        for h in (0, 1):
            for wnm in ("wk", "wv", "wr", "wg"):
                slab_pos.pop(wnm, None)
            ln_w = SW if h == 0 else SW + 1
            lnoff = 0 if h == 0 else 1
            lo = HALO if h == 0 else 0
            now = SW - lo
            gcol0 = 0 if h == 0 else OW0
            d_x = d_xh0 if h == 0 else d_xh1

            t_xbf = dma("gpsimd",
                        Eb[:, :].rearrange("p (c w) -> p c w", w=BW)[:, :, 0:ln_w],
                        d_x.ap(), writes=all_res("Eb"))
            ln_stats_apply(ln_w, "Eb", "Vb", Eb, Vb, Lb)

            mix_into(M1, "M1", "tmk", lnoff, 0, SW, h)
            mix_into(R1, "R1", "tmr", lnoff, lo, now, h if lo == 0 else 1)
            mix_into(M2, "M2", "tmv", lnoff, 0, SW, h)

            projection("wk", M1, "M1", SW,
                       lambda oc, psel: act(chunk(Eb, oc, 0, SW),
                                            pp[psel][:, 0:SW], AF.Exp,
                                            reads=[res("pp%d" % psel)],
                                            writes=[cres("Eb", oc)]))
            projection("wr", R1, "R1", now,
                       lambda oc, psel: act(chunk(R3, oc, 0, now),
                                            pp[psel][:, 0:now], AF.Sigmoid,
                                            reads=[res("pp%d" % psel)],
                                            writes=[cres("R3", oc)]))

            if h == 0:
                for cci in range(CC):
                    vop(lambda e, cci=cci: e.tensor_single_scalar(
                        Eb[:, cci * BW: cci * BW + HALO],
                        Eb[:, cci * BW: cci * BW + HALO],
                        maskt[:, 0:1], OP.mult),
                        reads=[res("maskt")], writes=[cres("Eb", cci)])

            gmix_pend = []

            def gmix_one(cci):
                if h == 0 and lo == 0:
                    pass
                n = now
                off = lnoff + lo
                vop(lambda e, cci=cci, off=off, n=n: e.tensor_sub(
                    tmix[:, 0:n], chunk(Lb, cci, off - 1, n),
                    chunk(Lb, cci, off, n)),
                    reads=[cres("Lb", cci)], writes=[res("tmix")])
                vop(lambda e, cci=cci, off=off, n=n:
                    e.scalar_tensor_tensor(
                        chunk(R1, cci, 0, n), tmix[:, 0:n],
                        vecs["tmg"][:, cci:cci + 1],
                        chunk(Lb, cci, off, n), OP.mult, OP.add),
                    reads=[res("tmix"), res("vec_tmg"), cres("Lb", cci)],
                    writes=[cres("R1", cci)])

            def v_evac(oc, psel):
                sc.op("scalar", lambda e, oc=oc, psel=psel:
                      e.copy(chunk(Vb, oc, 0, SW), pp[psel][:, 0:SW]),
                      reads=[res("pp%d" % psel)], writes=[cres("Vb", oc)])
                cci = oc
                sc.op("gpsimd", lambda e, cci=cci: e.tensor_mul(
                    chunk(M1, cci, 0, SW), chunk(Eb, cci, 0, SW),
                    chunk(Vb, cci, 0, SW)),
                    reads=[cres("Eb", cci), cres("Vb", cci)],
                    writes=[cres("M1", cci)])
                init_a = 0.0 if h == 0 else stA[:, cci:cci + 1]
                init_b = 0.0 if h == 0 else stB[:, cci:cci + 1]
                wb = wfm[:, cci:cci + 1].broadcast_to([P, SW])
                vop(lambda e, cci=cci, init_a=init_a, wb=wb:
                    e.tensor_tensor_scan(
                    chunk(Vb, cci, 0, SW), wb,
                    chunk(M1, cci, 0, SW), init_a, OP.mult, OP.add),
                    reads=[res("wfm"), cres("M1", cci), res("stA")],
                    writes=[cres("Vb", cci)])
                gmix_one(cci)
                vop(lambda e, cci=cci, init_b=init_b, wb=wb:
                    e.tensor_tensor_scan(
                    chunk(Lb, cci, 0, SW), wb,
                    chunk(Eb, cci, 0, SW), init_b, OP.mult, OP.add),
                    reads=[res("wfm"), cres("Eb", cci), res("stB")],
                    writes=[cres("Lb", cci)])

            projection("wv", M2, "M2", SW, v_evac)

            def epilogue(cci):
                par = cci % 2
                if lo == 0:
                    vop(lambda e, cci=cci, par=par: e.scalar_tensor_tensor(
                        tf1[par][:, 0:1], M1[:, cci * BW: cci * BW + 1],
                        euf[:, cci:cci + 1], stA[:, cci:cci + 1],
                        OP.mult, OP.add),
                        reads=[cres("M1", cci), res("euf"), res("stA")],
                        writes=[res("tf1_%d" % par)])
                    vop(lambda e, cci=cci, par=par: e.scalar_tensor_tensor(
                        tf2[par][:, 0:1], Eb[:, cci * BW: cci * BW + 1],
                        euf[:, cci:cci + 1], stB[:, cci:cci + 1],
                        OP.mult, OP.add),
                        reads=[cres("Eb", cci), res("euf"), res("stB")],
                        writes=[res("tf2_%d" % par)])
                    o1, n1 = 1, now - 1
                else:
                    o1, n1 = 0, now
                vop(lambda e, cci=cci, par=par, lo=lo, o1=o1, n1=n1:
                    e.scalar_tensor_tensor(
                    tf1[par][:, o1:o1 + n1], chunk(M1, cci, lo + o1, n1),
                    euf[:, cci:cci + 1], chunk(Vb, cci, lo + o1 - 1, n1),
                    OP.mult, OP.add),
                    reads=[cres("M1", cci), cres("Vb", cci), res("euf")],
                    writes=[res("tf1_%d" % par)])
                vop(lambda e, cci=cci, par=par, lo=lo, o1=o1, n1=n1:
                    e.scalar_tensor_tensor(
                    tf2[par][:, o1:o1 + n1], chunk(Eb, cci, lo + o1, n1),
                    euf[:, cci:cci + 1], chunk(Lb, cci, lo + o1 - 1, n1),
                    OP.mult, OP.add),
                    reads=[cres("Eb", cci), cres("Lb", cci), res("euf")],
                    writes=[res("tf2_%d" % par)])
                act(tf3[:, 0:now], tf2[par][:, 0:now], AF.Ln,
                    reads=[res("tf2_%d" % par)], writes=[res("tf3")])
                act(tf2[par][:, 0:now], tf3[:, 0:now], AF.Exp, scale=-1.0,
                    reads=[res("tf3")], writes=[res("tf2_%d" % par)])

            def wkv_mul(cci):
                par = cci % 2
                vop(lambda e, cci=cci, par=par, lo=lo, now=now: e.tensor_mul(
                    chunk(M1, cci, lo, now), tf1[par][:, 0:now],
                    tf2[par][:, 0:now]),
                    reads=[res("tf1_%d" % par), res("tf2_%d" % par)],
                    writes=[cres("M1", cci)])

            if h == 0:
                vop(lambda e: e.tensor_copy(
                    stA[:, :],
                    Vb[:, :].rearrange("p (c w) -> p c w", w=BW)[:, :, SW - 1:SW]),
                    reads=all_res("Vb"), writes=[res("stA")])
                vop(lambda e: e.tensor_copy(
                    stB[:, :],
                    Lb[:, :].rearrange("p (c w) -> p c w", w=BW)[:, :, SW - 1:SW]),
                    reads=all_res("Lb"), writes=[res("stB")])
            def g_evac(oc, psel):
                sc.op("scalar", lambda e, oc=oc, psel=psel, now=now:
                      e.copy(chunk(M2, oc, 0, now), pp[psel][:, 0:now]),
                      reads=[res("pp%d" % psel)], writes=[cres("M2", oc)])

            projection("wg", R1, "R1", now, g_evac)

            pend = []
            for cci in range(CC):
                epilogue(cci)
                for f in pend:
                    f()
                pend = [lambda cci=cci: wkv_mul(cci)]
            for f in pend:
                f()

            for cci in range(CC):
                tgb, tgr = (tmix, "tmix") if cci % 2 == 0 else (tf3, "tf3")
                act(tgb[:, 0:now], chunk(M2, cci, 0, now), AF.Sigmoid,
                    reads=[cres("M2", cci)], writes=[res(tgr)])
                tpar = cci % 2
                vop(lambda e, cci=cci, tgb=tgb, lo=lo, now=now, tpar=tpar:
                    e.tensor_mul(
                    tf1[tpar][:, 0:now], tgb[:, 0:now],
                    chunk(M1, cci, lo, now)),
                    reads=[res(tgr), cres("M1", cci)],
                    writes=[res("tf1_%d" % tpar)])
                vop(lambda e, cci=cci, tpar=tpar, gcol0=gcol0, now=now:
                    e.tensor_mul(
                    ag[:, cci * TOWN + gcol0: cci * TOWN + gcol0 + now],
                    tf1[tpar][:, 0:now], chunk(R3, cci, 0, now)),
                    reads=[res("tf1_%d" % tpar), cres("R3", cci)],
                    writes=[res("ag")])

        if dbg:
            for nm, buf, wdt in (("dbg_ag", ag, CC * TOWN),
                                 ("dbg_M1", M1, CC * BW),
                                 ("dbg_R3", R3, CC * BW),
                                 ("dbg_Lb", Lb, CC * BW),
                                 ("dbg_R1", R1, CC * BW),
                                 ("dbg_M2", M2, CC * BW)):
                dd = nc.dram_tensor(nm, [P, wdt], BF16, kind="ExternalOutput")
                bn = nm.replace("dbg_", "") if nm != "dbg_ag" else "ag"
                dma("sync", dd.ap(), buf[:, :],
                    reads=([res("ag")] if bn == "ag" else all_res(bn)))
    fence()

    with ExitStack() as pstack:
        def psb(name, shape, dt):
            return pstack.enter_context(nc.sbuf_tensor(name, shape, dt))

        acc = psb("acc", [P, CC * TOWN], F32)
        tp1 = psb("tp1", [P, TOWN], F32)
        tbf = psb("tbf", [P, TOWN], BF16)
        mu_b = psb("mu_b", [P, TOWN], F32)
        rs_b = psb("rs_b", [P, TOWN], F32)
        mu_row = psb("mu_row2", [1, TOWN], F32)
        s2_row = psb("s2_row2", [1, TOWN], F32)
        rs_row = psb("rs_row2", [1, TOWN], F32)
        wstack = ExitStack()
        attn_o = wstack.enter_context(nc.sbuf_tensor("attn_o", [P, CC * TOWN], BF16))
        qpsum = ExitStack()
        po = [qpsum.enter_context(nc.psum_tensor("po0", [P, TOWN], F32)),
              qpsum.enter_context(nc.psum_tensor("po1", [P, TOWN], F32))]
        ps2_mu = qpsum.enter_context(nc.psum_tensor("ps2_mu", [1, TOWN], F32))
        ps2_s2 = qpsum.enter_context(nc.psum_tensor("ps2_s2", [1, TOWN], F32))

        dma("sync", acc[:, :].rearrange("p (c w) -> p c w", w=TOWN),
            d_xres.ap(), writes=[res("acc")])

        def stream_stats(oc, srcbuf, srcres, start):
            vop(lambda e, oc=oc, srcbuf=srcbuf: e.tensor_copy(
                tbf[:, :], srcbuf[:, oc * TOWN:(oc + 1) * TOWN]),
                reads=[res(srcres)], writes=[res("tbf")])
            for stepi in range(2):
                psum, pres = ((ps2_mu, "ps2_mu"), (ps2_s2, "ps2_s2"))[stepi]
                if stepi == 1:
                    act(tbf[:, :], tbf[:, :], AF.Square, writes=[res("tbf")])
                fns = []
                for (o, n) in segs(TOWN, 512):
                    fns.append(lambda e, psum=psum, o=o, n=n:
                               e.matmul(psum[0:1, o:o + n], onesm[:, 0:1],
                                        tbf[:, o:o + n],
                                        start=start, stop=(oc == CC - 1)))
                sc.op("tensor", fns, reads=[res("tbf"), res("onesm")],
                      writes=[] if not start else [res(pres)],
                      waits_ops=(R[pres].writer,) if (start and R[pres].writer
                                                      is not None) else ())
                if not start:
                    R[pres].writer = sc.ops[-1]["idx"]

        def finalize_rows_and_bcast():
            vop(lambda e: e.tensor_copy(mu_row[0:1, 0:TOWN], ps2_mu[0:1, 0:TOWN]),
                reads=[res("ps2_mu")], writes=[res("mu_row")])
            vop(lambda e: e.tensor_copy(s2_row[0:1, 0:TOWN], ps2_s2[0:1, 0:TOWN]),
                reads=[res("ps2_s2")], writes=[res("s2_row")])
            vop(lambda e: e.tensor_mul(rs_row[0:1, 0:TOWN], mu_row[0:1, 0:TOWN],
                                       mu_row[0:1, 0:TOWN]),
                reads=[res("mu_row")], writes=[res("rs_row")])
            vop(lambda e: e.tensor_sub(s2_row[0:1, 0:TOWN], s2_row[0:1, 0:TOWN],
                                       rs_row[0:1, 0:TOWN]),
                reads=[res("rs_row")], writes=[res("s2_row")])
            vop(lambda e: e.tensor_single_scalar(
                s2_row[0:1, 0:TOWN], s2_row[0:1, 0:TOWN], EPS, OP.add),
                writes=[res("s2_row")])
            act(s2_row[0:1, 0:TOWN], s2_row[0:1, 0:TOWN], AF.Ln,
                writes=[res("s2_row")])
            act(rs_row[0:1, 0:TOWN], s2_row[0:1, 0:TOWN], AF.Exp, scale=-0.5,
                reads=[res("s2_row")], writes=[res("rs_row")])
            for row, rres, dstb, dres, psel in (
                    (mu_row, "mu_row", mu_b, "mu_b", 0),
                    (rs_row, "rs_row", rs_b, "rs_b", 1)):
                fns = []
                for (o, n) in segs(TOWN, 512):
                    fns.append(lambda e, row=row, o=o, n=n, psel=psel:
                               e.matmul(po[psel][0:128, o:o + n], onesr[0:1, 0:P],
                                        row[0:1, o:o + n], start=True, stop=True))
                sc.op("tensor", fns, reads=[res(rres), res("onesr")],
                      writes=[res("po%d" % psel)])
                vop(lambda e, dstb=dstb, psel=psel:
                    e.tensor_copy(dstb[:, 0:TOWN], po[psel][:, 0:TOWN]),
                    reads=[res("po%d" % psel)], writes=[res(dres)])

        pend_stats = []
        for s in range(8):
            t_slab, par = load_slab_one(d_w["wo"], key="wo")
            for osub in range(2):
                oc = s * 2 + osub
                psel = oc % 2
                fns = []
                for cci in range(CC):
                    for (o, n) in segs(TOWN, 512):
                        fns.append(lambda e, par=par, cci=cci, osub=osub, o=o,
                                   n=n, psel=psel:
                                   e.matmul(po[psel][:, o:o + n],
                                            SL[par][:, cci * 256 + osub * P:
                                                    cci * 256 + osub * P + P],
                                            ag[:, cci * TOWN + o:
                                               cci * TOWN + o + n],
                                            start=(cci == 0),
                                            stop=(cci == CC - 1)))
                sc.op("tensor", fns, reads=[res("SL%d" % par), res("ag")],
                      writes=[res("po%d" % psel)], waits_ops=(t_slab,))
                sc.op("scalar", lambda e, oc=oc, psel=psel:
                      e.copy(attn_o[:, oc * TOWN:(oc + 1) * TOWN],
                             po[psel][:, 0:TOWN]),
                      reads=[res("po%d" % psel)], writes=[res("attn_o")])
                if len(pend_stats) >= 2:
                    pend_stats.pop(0)()
                pend_stats.append(lambda oc=oc: stream_stats(
                    oc, attn_o, "attn_o", start=(oc == 0)))
        for f in pend_stats:
            f()
        finalize_rows_and_bcast()
        for cci in range(CC):
            sc.op("gpsimd", lambda e, cci=cci: e.tensor_sub(
                tp1[:, :], attn_o[:, cci * TOWN:(cci + 1) * TOWN], mu_b[:, :]),
                reads=[res("attn_o"), res("mu_b")], writes=[res("tp1")])
            vop(lambda e: e.tensor_mul(tp1[:, :], tp1[:, :], rs_b[:, :]),
                reads=[res("rs_b")], writes=[res("tp1")])
            vop(lambda e, cci=cci: e.tensor_scalar(
                tp1[:, :], tp1[:, :], vecs["lng"][:, cci:cci + 1],
                vecs["lnb"][:, cci:cci + 1], OP.mult, OP.add),
                reads=[res("vec_lng"), res("vec_lnb")], writes=[res("tp1")])
            vop(lambda e, cci=cci: e.tensor_add(
                acc[:, cci * TOWN:(cci + 1) * TOWN],
                acc[:, cci * TOWN:(cci + 1) * TOWN], tp1[:, :]),
                reads=[res("tp1")], writes=[res("acc")])
            stream_stats(cci, acc, "acc", start=(cci == 0))

        if dbg:
            dd = nc.dram_tensor("dbg_attn_o", [P, CC * TOWN], BF16,
                                kind="ExternalOutput")
            dma("sync", dd.ap(), attn_o[:, :], reads=[res("attn_o")])
        wstack.close()
        fence()
        ln2 = psb("ln2", [P, CC * TOWN], BF16)
        finalize_rows_and_bcast()
        for cci in range(CC):
            sc.op("gpsimd", lambda e, cci=cci: e.tensor_sub(
                tp1[:, :], acc[:, cci * TOWN:(cci + 1) * TOWN], mu_b[:, :]),
                reads=[res("acc"), res("mu_b")], writes=[res("tp1")])
            vop(lambda e, cci=cci: e.tensor_mul(
                ln2[:, cci * TOWN:(cci + 1) * TOWN], tp1[:, :], rs_b[:, :]),
                reads=[res("tp1"), res("rs_b")], writes=[res("ln2")])
        if dbg:
            dd = nc.dram_tensor("dbg_ln2", [P, CC * TOWN], BF16,
                                kind="ExternalOutput")
            dma("sync", dd.ap(), ln2[:, :], reads=[res("ln2")])
        qpsum.close()

        with ExitStack() as fstack:
            def fsb(name, shape, dt):
                return fstack.enter_context(nc.sbuf_tensor(name, shape, dt))

            def fps(name, shape, dt=F32):
                return fstack.enter_context(nc.psum_tensor(name, shape, dt))

            hs = [ag[:, 0:8 * TOWN], ag[:, 8 * TOWN:16 * TOWN]]
            wdb = [fsb("wdb0", [P, 8 * P], BF16), fsb("wdb1", [P, 8 * P], BF16)]
            th2 = [fsb("th0", [P, TOWN], BF16), fsb("th1", [P, TOWN], BF16)]
            pg = fps("pg", [P, TOWN], F32)
            pu = fps("pu", [P, TOWN], F32)
            pd = [fps("pd0", [P, TOWN], F32), fps("pd1", [P, TOWN], F32)]
            def emit_down(s, hc):
                hpar = s % 2
                wpar = hc % 2
                dma("gpsimd",
                    wdb[wpar][:, :].rearrange("p (i w) -> p i w", w=P),
                    d_wd.ap()[s][hc], writes=[res("wdb%d" % wpar)])
                fns = []
                for ic in range(8):
                    for (o, n) in segs(TOWN, 512):
                        fns.append(lambda e, wpar=wpar, ic=ic, o=o, n=n,
                                   hpar=hpar:
                                   e.matmul(pd[wpar][:, o:o + n],
                                            wdb[wpar][:, ic * P:(ic + 1) * P],
                                            hs[hpar][:, ic * TOWN + o:
                                                     ic * TOWN + o + n],
                                            start=(ic == 0), stop=(ic == 7)))
                sc.op("tensor", fns,
                      reads=[res("wdb%d" % wpar), res("hs%d" % hpar)],
                      writes=[res("pd%d" % wpar)])
                vop(lambda e, hc=hc, wpar=wpar: e.tensor_add(
                    acc[:, hc * TOWN:(hc + 1) * TOWN],
                    acc[:, hc * TOWN:(hc + 1) * TOWN], pd[wpar][:, :]),
                    reads=[res("pd%d" % wpar)], writes=[res("acc")])

            down_pend = []
            for s in range(8):
                hpar = s % 2
                for ic in range(8):
                    gic = 8 * s + ic
                    slab = gic // 2
                    within = gic % 2
                    if within == 0:
                        tg_, parg = load_slab_one(d_wgu_g, key="g")
                        tu_, paru = load_slab_one(d_wgu_u, key="u")
                        sc._ffn_cur = {"g": (tg_, parg), "u": (tu_, paru)}
                    cur = sc._ffn_cur
                    for which in ("g", "u"):
                        t_slab, par = cur[which]
                        psum, pres = (pg, "pg") if which == "g" else (pu, "pu")
                        fns = []
                        for cci in range(CC):
                            for (o, n) in segs(TOWN, 512):
                                fns.append(lambda e, par=par, cci=cci,
                                           within=within, o=o, n=n, psum=psum:
                                           e.matmul(psum[:, o:o + n],
                                                    SL[par][:, cci * 256
                                                            + within * P:
                                                            cci * 256
                                                            + within * P + P],
                                                    ln2[:, cci * TOWN + o:
                                                        cci * TOWN + o + n],
                                                    start=(cci == 0),
                                                    stop=(cci == CC - 1)))
                        sc.op("tensor", fns,
                              reads=[res("SL%d" % par), res("ln2")],
                              writes=[res(pres)], waits_ops=(t_slab,))
                    tpar = ic % 2
                    act(th2[tpar][:, :], pg[:, :], AF.Silu,
                        reads=[res("pg")], writes=[res("th_%d" % tpar)])
                    vop(lambda e, hpar=hpar, ic=ic, tpar=tpar: e.tensor_mul(
                        hs[hpar][:, ic * TOWN:(ic + 1) * TOWN],
                        th2[tpar][:, :], pu[:, :]),
                        reads=[res("th_%d" % tpar), res("pu")],
                        writes=[res("hs%d" % hpar)])
                    for _ in range(2):
                        if down_pend:
                            emit_down(*down_pend.pop(0))
                down_pend.extend((s, hc) for hc in range(CC))
            for s_hc in down_pend:
                emit_down(*s_hc)

        t_out = dma("sync", d_out.ap(),
                    acc[:, :].rearrange("p (c w) -> p c w", w=TOWN),
                    reads=[res("acc")])
        sc.op("sync", lambda e: e.nop(), waits_ops=(t_out,))

    sc.emit(sems)
    nc.finalize()
    stack.close()
    return nc



def _swizzle_w(wT, oslab):
    cin, o = wT.shape
    return np.ascontiguousarray(
        wT.reshape(CC, P, o // oslab, oslab).transpose(2, 1, 0, 3))


def _vec_fm(v):
    return np.ascontiguousarray(v.reshape(CC, P).T)


def _x_fm(xs):
    t = xs.shape[0]
    return np.ascontiguousarray(xs.T.reshape(CC, P, t).transpose(1, 0, 2))


def kernel(L0_attn_x, L0_attn_time_decay, L0_attn_time_first, L0_attn_tm_k,
           L0_attn_tm_v, L0_attn_tm_r, L0_attn_tm_g, L0_attn_Wr, L0_attn_Wk,
           L0_attn_Wv, L0_attn_Wg, L0_attn_Wo, L0_attn_ln_g, L0_attn_ln_b,
           L0_ffn_w_gate_up, L0_ffn_w_down):
    TOWN = 1024
    TH = TOWN + HALO
    SW = TH // 2

    x = np.asarray(L0_attn_x, dtype=np.float32)[0]
    T = x.shape[0]
    assert T == NCORES * TOWN

    wk = _swizzle_w(np.ascontiguousarray(np.asarray(L0_attn_Wk, np.float32).T), 256)
    wv = _swizzle_w(np.ascontiguousarray(np.asarray(L0_attn_Wv, np.float32).T), 256)
    wr = _swizzle_w(np.ascontiguousarray(np.asarray(L0_attn_Wr, np.float32).T), 256)
    wg = _swizzle_w(np.ascontiguousarray(np.asarray(L0_attn_Wg, np.float32).T), 256)
    wo = _swizzle_w(np.ascontiguousarray(np.asarray(L0_attn_Wo, np.float32).T), 256)
    wguT = np.ascontiguousarray(np.asarray(L0_ffn_w_gate_up, np.float32).T)
    wgu_g = _swizzle_w(np.ascontiguousarray(wguT[:, 0:INTER]), 256)
    wgu_u = _swizzle_w(np.ascontiguousarray(wguT[:, INTER:]), 256)
    wdT = np.ascontiguousarray(np.asarray(L0_ffn_w_down, np.float32).T)
    wd = np.ascontiguousarray(
        wdT.reshape(8, 8, P, CC, P).transpose(0, 3, 2, 1, 4))

    vec_common = {
        "tmk": _vec_fm(np.asarray(L0_attn_tm_k, np.float32)),
        "tmv": _vec_fm(np.asarray(L0_attn_tm_v, np.float32)),
        "tmr": _vec_fm(np.asarray(L0_attn_tm_r, np.float32)),
        "tmg": _vec_fm(np.asarray(L0_attn_tm_g, np.float32)),
        "td": _vec_fm(np.asarray(L0_attn_time_decay, np.float32)),
        "tf": _vec_fm(np.asarray(L0_attn_time_first, np.float32)),
        "lng": _vec_fm(np.asarray(L0_attn_ln_g, np.float32)),
        "lnb": _vec_fm(np.asarray(L0_attn_ln_b, np.float32)),
    }
    w_common = {"wk": wk, "wv": wv, "wr": wr, "wg": wg, "wo": wo,
                "wgu_g": wgu_g, "wgu_u": wgu_u, "wd": wd}

    in_maps = []
    for c in range(NCORES):
        base = c * TOWN
        if c == 0:
            halo_rows = np.concatenate([x[0:HALO - 1], x[1:2]], axis=0)
            mask = np.zeros((P, 1), np.float32)
        else:
            halo_rows = x[base - HALO: base]
            mask = np.ones((P, 1), np.float32)
        xa = np.concatenate([halo_rows, x[base: base + TOWN]], axis=0)
        m = {
            "x_h0": _x_fm(xa[0:SW]),
            "x_h1": _x_fm(xa[SW - 1: TH]),
            "x_res": _x_fm(xa[HALO: TH]),
            "mask": mask,
        }
        m.update(vec_common)
        m.update(w_common)
        in_maps.append(m)

    nc = build_program(TOWN)
    r = run_bass_kernel_spmd(nc, in_maps, core_ids=list(range(NCORES)))
    outs = []
    for c in range(NCORES):
        o = np.asarray(r.results[c]["out"])
        outs.append(o.transpose(1, 0, 2).reshape(C, TOWN).T)
    full = np.concatenate(outs, axis=0)[None]
    return full.astype(np.float32)


# version 17
# speedup vs baseline: 1.0806x; 1.0806x over previous
import numpy as np
import ml_dtypes
import concourse.bass as bass
import concourse.mybir as mybir
from concourse.alu_op_type import AluOpType as OP
from concourse.bass_utils import run_bass_kernel_spmd

F32 = mybir.dt.float32
BF16 = mybir.dt.bfloat16
AF = mybir.ActivationFunctionType

P = 128
CC = 16
C = 2048
NCORES = 8
HALO = 128
EPS = 1e-5
INTER = 8192

ENGINES = ("tensor", "vector", "scalar", "gpsimd", "sync")
OWN_SEM = {"tensor": "P", "vector": "V", "scalar": "A", "gpsimd": "G", "sync": None}


def segs(n, cap=512):
    out = []
    o = 0
    while o < n:
        s = min(cap, n - o)
        out.append((o, s))
        o += s
    return out


class Res:

    def __init__(self, name):
        self.name = name
        self.writer = None
        self.readers = []


class Sched:

    def __init__(self, nc):
        self.nc = nc
        self.ops = []
        self.last_eng = {}
        self.barrier_idx = None

    def barrier(self):
        deps = [i for i in self.last_eng.values()]
        idx = self.op("vector", lambda e: e.engine_nop(), waits_ops=deps)
        self.barrier_idx = idx
        return idx

    def op(self, eng, fns, reads=(), writes=(), waits_ops=(), dma=False,
           dma_sem=None):
        if not isinstance(fns, (list, tuple)):
            fns = [fns]
        idx = len(self.ops)
        deps = set(waits_ops)
        rnames = [r.name for r in reads]
        wnames = [r.name for r in writes]
        for r in reads:
            if r.writer is not None:
                deps.add(r.writer)
            r.readers.append(idx)
        for r in writes:
            if r.writer is not None:
                deps.add(r.writer)
            deps.update(r.readers)
            r.writer = idx
            r.readers = []
        if dma:
            sem = dma_sem or ("dG" if eng == "gpsimd" else "dS")
        else:
            sem = OWN_SEM[eng]
        self.ops.append(dict(eng=eng, fns=list(fns), sem=sem, dma=dma,
                             deps=deps, idx=idx, rnames=rnames, wnames=wnames))
        self.last_eng[eng] = idx
        return idx

    def emit(self, sems):
        ops = self.ops
        needed = set()
        for o in ops:
            own = OWN_SEM[o["eng"]]
            for d in o["deps"]:
                dop = ops[d]
                if (not dop["dma"]) and dop["sem"] is not None and dop["sem"] == own:
                    continue
                needed.add(d)
        inc_count = {}
        op_val = {}
        for o in ops:
            s = o["sem"]
            if o["dma"]:
                inc_count[s] = inc_count.get(s, 0) + 16
                op_val[o["idx"]] = (s, inc_count[s])
            elif o["idx"] in needed and s is not None:
                inc_count[s] = inc_count.get(s, 0) + 1
                op_val[o["idx"]] = (s, inc_count[s])
            else:
                op_val[o["idx"]] = None
        per_eng = {e: [] for e in ENGINES}
        hw = {e: {} for e in ENGINES}
        for o in ops:
            own = OWN_SEM[o["eng"]]
            wmap = {}
            for d in sorted(o["deps"]):
                dop = ops[d]
                if (not dop["dma"]) and dop["sem"] is not None and dop["sem"] == own:
                    continue
                v = op_val[d]
                if v is None:
                    raise RuntimeError("dep without inc")
                s, val = v
                if wmap.get(s, 0) < val:
                    wmap[s] = val
            waits = []
            for s, val in sorted(wmap.items()):
                if hw[o["eng"]].get(s, 0) >= val:
                    continue
                hw[o["eng"]][s] = val
                waits.append((s, val))
            inc = op_val[o["idx"]]
            myinc = None
            if o["dma"]:
                myinc = (inc[0], 16)
            elif inc is not None:
                myinc = (inc[0], 1)
            per_eng[o["eng"]].append((waits, o["fns"], myinc, o))
        self._final_counts = dict(inc_count)

        audit = getattr(self, "audit", None)

        def runner(items):
            def go(e):
                for waits, fns, myinc, o in items:
                    insts = []
                    for s, val in waits:
                        e.wait_ge(sems[s], val)
                    last = None
                    for fn in fns:
                        last = fn(e)
                        insts.append(last)
                    if myinc is not None:
                        last.then_inc(sems[myinc[0]], myinc[1])
                    if audit is not None:
                        o["_insts"] = insts
            return go

        with self.nc.Block() as blk:
            for eng in ENGINES:
                getattr(blk, eng)(runner(per_eng[eng]))


def build_program(TOWN=1024, dbg=False):
    TH = TOWN + HALO
    SW = TH // 2
    BW = SW + 1
    assert TH % 2 == 0 and SW > HALO
    OW0 = SW - HALO

    nc = bass.Bass("TRN2", target_bir_lowering=False,
                   detect_race_conditions=False)

    d_xh0 = nc.dram_tensor("x_h0", [P, CC, SW], F32, kind="ExternalInput")
    d_xh1 = nc.dram_tensor("x_h1", [P, CC, SW + 1], F32, kind="ExternalInput")
    d_xres = nc.dram_tensor("x_res", [P, CC, TOWN], F32, kind="ExternalInput")
    d_w = {}
    for nm in ("wk", "wv", "wr", "wg", "wo"):
        d_w[nm] = nc.dram_tensor(nm, [8, P, CC, 256], F32, kind="ExternalInput")
    d_wgu_g = nc.dram_tensor("wgu_g", [32, P, CC, 256], F32, kind="ExternalInput")
    d_wgu_u = nc.dram_tensor("wgu_u", [32, P, CC, 256], F32, kind="ExternalInput")
    d_wd = nc.dram_tensor("wd", [8, CC, P, 8, P], F32, kind="ExternalInput")
    d_vec = {}
    for nm in ("tmk", "tmv", "tmr", "tmg", "td", "tf", "lng", "lnb"):
        d_vec[nm] = nc.dram_tensor(nm, [P, CC], F32, kind="ExternalInput")
    d_mask = nc.dram_tensor("mask", [P, 1], F32, kind="ExternalInput")
    d_out = nc.dram_tensor("out", [P, CC, TOWN], F32, kind="ExternalOutput")

    from contextlib import ExitStack
    stack = ExitStack()
    sems = {}
    for s in ("dG", "dS", "dW", "P", "V", "A", "G"):
        sems[s] = stack.enter_context(nc.semaphore(s))

    sc = Sched(nc)

    def sb(name, shape, dt):
        return stack.enter_context(nc.sbuf_tensor(name, shape, dt))

    vecs = {nm: sb("sb_" + nm, [P, CC], F32) for nm in d_vec}
    maskt = sb("maskt", [P, 1], F32)
    wfm = sb("wfm", [P, CC], F32)
    euf = sb("euf", [P, CC], F32)
    s2f = sb("s2f", [P, CC], F32)
    e1f = sb("e1f", [P, CC], F32)
    onesm = sb("onesm", [P, 1], BF16)
    onesr = sb("onesr", [1, P], F32)
    stA = sb("stA", [P, CC], F32)
    stB = sb("stB", [P, CC], F32)
    ag = sb("ag", [P, CC * TOWN], BF16)
    SL = [sb("SL0", [P, CC * 256], BF16), sb("SL1", [P, CC * 256], BF16)]

    R = {}

    def res(name):
        if name not in R:
            R[name] = Res(name)
            R[name].writer = sc.barrier_idx
        return R[name]

    def fence():
        idx = sc.barrier()
        for r in R.values():
            r.writer = idx
            r.readers = []
        return idx

    def cres(buf, cc):
        return res("%s_%d" % (buf, cc))

    def all_res(buf):
        return [cres(buf, i) for i in range(CC)]

    def dma(eng, dst_ap, src_ap, reads=(), writes=(), waits=(), dma_sem=None):
        return sc.op(eng, lambda e, d=dst_ap, s=src_ap: e.dma_start(d, s),
                     reads=reads, writes=writes, waits_ops=waits, dma=True,
                     dma_sem=dma_sem)

    def act(out_ap, in_ap, func, bias=0.0, scale=1.0, reads=(), writes=(), waits=()):
        return sc.op("scalar",
                     lambda e, o=out_ap, i=in_ap, f=func, b=bias, s=scale:
                     e.activation(o, i, f, bias=b, scale=s),
                     reads=reads, writes=writes, waits_ops=waits)

    def vop(fn, reads=(), writes=(), waits=()):
        return sc.op("vector", fn, reads=reads, writes=writes, waits_ops=waits)

    slab_seq = [0]
    slab_pos = {}

    def load_slab_one(dram, key=None):
        k = key if key is not None else id(dram)
        s = slab_pos.get(k, 0)
        slab_pos[k] = s + 1
        par = slab_seq[0] % 2
        slab_seq[0] += 1
        t = dma("gpsimd",
                SL[par][:, :].rearrange("p (c w) -> p c w", w=256),
                dram.ap()[s],
                writes=[res("SL%d" % par)])
        return t, par

    for nm in d_vec:
        dma("sync", vecs[nm][:, :], d_vec[nm].ap(), writes=[res("vec_" + nm)])
    dma("sync", maskt[:, :], d_mask.ap(), writes=[res("maskt")])
    vop(lambda e: e.memset(onesm[:, :], 1.0 / 2048.0), writes=[res("onesm")])
    vop(lambda e: e.memset(onesr[:, :], 1.0), writes=[res("onesr")])
    act(e1f[:, :], vecs["td"][:, :], AF.Exp,
        reads=[res("vec_td")], writes=[res("e1f")])
    act(euf[:, :], vecs["tf"][:, :], AF.Exp,
        reads=[res("vec_tf")], writes=[res("euf")])
    vop(lambda e: e.memset(wfm[:, 0:1], 1.0), writes=[res("wfm")])
    for fn in (AF.Square, AF.Sigmoid, AF.Silu, AF.Ln, AF.Exp):
        act(wfm[:, 0:1], wfm[:, 0:1], fn, writes=[res("wfm")])
    act(wfm[:, :], e1f[:, :], AF.Exp, scale=-1.0,
        reads=[res("e1f")], writes=[res("wfm")])
    vop(lambda e: e.tensor_mul(s2f[:, :], euf[:, :], wfm[:, :]),
        reads=[res("euf"), res("wfm")], writes=[res("s2f")])
    vop(lambda e: e.tensor_scalar(s2f[:, :], s2f[:, :], -1.0, 1.0,
                                  OP.mult, OP.add), writes=[res("s2f")])
    fence()

    with ExitStack() as hstack:
        def hsb(name, shape, dt):
            return hstack.enter_context(nc.sbuf_tensor(name, shape, dt))

        def hps(name, shape, dt=F32):
            return hstack.enter_context(nc.psum_tensor(name, shape, dt))

        Lb = hsb("Lb", [P, CC * BW], BF16)
        Eb = hsb("Eb", [P, CC * BW], BF16)
        Vb = hsb("Vb", [P, CC * BW], BF16)
        M1 = hsb("M1", [P, CC * BW], BF16)
        M2 = hsb("M2", [P, CC * BW], BF16)
        R1 = hsb("R1", [P, CC * BW], BF16)
        R3 = hsb("R3", [P, CC * BW], BF16)
        tmix = hsb("tmix", [P, BW], BF16)
        tf1 = [hsb("tf1a", [P, BW], F32), hsb("tf1b", [P, BW], F32)]
        tf2 = [hsb("tf2a", [P, BW], F32), hsb("tf2b", [P, BW], F32)]
        tf3 = hsb("tf3a", [P, BW], F32)
        mub_h = hsb("mub_h", [P, BW], F32)
        rsb_h = hsb("rsb_h", [P, BW], F32)
        hmu_row = hsb("hmu_row", [1, BW], F32)
        hs2_row = hsb("hs2_row", [1, BW], F32)
        hrs_row = hsb("hrs_row", [1, BW], F32)
        pp = [hps("pp0", [P, BW], F32), hps("pp1", [P, BW], F32),
              hps("pp2", [P, BW], F32)]
        ps_mu = pp[0]
        ps_s2 = pp[1]

        def chunk(buf, cc, lo, n, stride=BW):
            return buf[:, cc * stride + lo: cc * stride + lo + n]

        def halo_view(buf):
            return buf[:, :].rearrange("p (c w) -> p c w", w=BW)[:, :, 0:HALO]

        def col_view(buf, col):
            return buf[:, :].rearrange("p (c w) -> p c w", w=BW)[:, :, col:col + 1]

        if dbg:
            for buf, bres in ((Lb, "Lb"), (Eb, "Eb"), (Vb, "Vb"),
                              (M1, "M1"), (M2, "M2"), (R1, "R1"), (R3, "R3")):
                vop(lambda e, buf=buf: e.memset(buf[:, :], 0.0),
                    writes=[res(bres)] + all_res(bres))

        def ln_stats_apply(ln_w, xbf_res, sq_res, xbuf, sqbuf, outbuf):
            xv3 = xbuf[:, :].rearrange("p (c w) -> p c w", w=BW)[:, :, 0:ln_w]
            sv3 = sqbuf[:, :].rearrange("p (c w) -> p c w", w=BW)[:, :, 0:ln_w]
            act(sv3, xv3, AF.Square, reads=all_res(xbf_res),
                writes=all_res(sq_res))
            for stat, (srcb, sres, psum, pres) in enumerate(
                    ((xbuf, xbf_res, ps_mu, "ps_mu"),
                     (sqbuf, sq_res, ps_s2, "ps_s2"))):
                for (o, n) in segs(ln_w, 512):
                    fns = []
                    for cci in range(CC):
                        fns.append(lambda e, srcb=srcb, psum=psum, o=o, n=n, cci=cci:
                                   e.matmul(psum[0:1, o:o + n], onesm[:, 0:1],
                                            chunk(srcb, cci, o, n),
                                            start=(cci == 0), stop=(cci == CC - 1)))
                    sc.op("tensor", fns, reads=all_res(sres) + [res("onesm")],
                          writes=[res("pp0" if pres == "ps_mu" else "pp1")])
            vop(lambda e: e.tensor_copy(hmu_row[0:1, 0:ln_w], ps_mu[0:1, 0:ln_w]),
                reads=[res("pp0")], writes=[res("hmu_row")])
            vop(lambda e: e.tensor_copy(hs2_row[0:1, 0:ln_w], ps_s2[0:1, 0:ln_w]),
                reads=[res("pp1")], writes=[res("hs2_row")])
            vop(lambda e: e.tensor_mul(hrs_row[0:1, 0:ln_w], hmu_row[0:1, 0:ln_w],
                                       hmu_row[0:1, 0:ln_w]),
                reads=[res("hmu_row")], writes=[res("hrs_row")])
            vop(lambda e: e.tensor_sub(hs2_row[0:1, 0:ln_w], hs2_row[0:1, 0:ln_w],
                                       hrs_row[0:1, 0:ln_w]),
                reads=[res("hrs_row")], writes=[res("hs2_row")])
            vop(lambda e: e.tensor_single_scalar(
                hs2_row[0:1, 0:ln_w], hs2_row[0:1, 0:ln_w], EPS, OP.add),
                writes=[res("hs2_row")])
            act(hs2_row[0:1, 0:ln_w], hs2_row[0:1, 0:ln_w], AF.Ln,
                writes=[res("hs2_row")])
            act(hrs_row[0:1, 0:ln_w], hs2_row[0:1, 0:ln_w], AF.Exp, scale=-0.5,
                reads=[res("hs2_row")], writes=[res("hrs_row")])
            for row, rres, dstb, dres, psel in (
                    (hmu_row, "hmu_row", mub_h, "mub_h", 0),
                    (hrs_row, "hrs_row", rsb_h, "rsb_h", 1)):
                fns = []
                for (o, n) in segs(ln_w, 512):
                    fns.append(lambda e, row=row, o=o, n=n, psel=psel:
                               e.matmul(pp[psel][0:128, o:o + n], onesr[0:1, 0:P],
                                        row[0:1, o:o + n], start=True, stop=True))
                sc.op("tensor", fns, reads=[res(rres), res("onesr")],
                      writes=[res("pp%d" % psel)])
                vop(lambda e, dstb=dstb, psel=psel, ln_w=ln_w:
                    e.tensor_copy(dstb[:, 0:ln_w], pp[psel][:, 0:ln_w]),
                    reads=[res("pp%d" % psel)], writes=[res(dres)])
            for cci in range(CC):
                vop(lambda e, cci=cci, ln_w=ln_w: e.tensor_sub(
                    tf1[0][:, 0:ln_w], chunk(xbuf, cci, 0, ln_w),
                    mub_h[:, 0:ln_w]),
                    reads=[cres(xbf_res, cci), res("mub_h")],
                    writes=[res("tf1_0")])
                vop(lambda e, cci=cci, ln_w=ln_w: e.tensor_mul(
                    chunk(outbuf, cci, 0, ln_w), tf1[0][:, 0:ln_w],
                    rsb_h[:, 0:ln_w]),
                    reads=[res("tf1_0"), res("rsb_h")],
                    writes=[cres("Lb", cci)])

        def mix_into(dstbuf, dres, tmname, lnoff, lo, n, h):
            for cci in range(CC):
                if h == 0 and lo == 0:
                    vop(lambda e: e.memset(tmix[:, 0:1], 0.0),
                        writes=[res("tmix")])
                    vop(lambda e, cci=cci, n=n: e.tensor_sub(
                        tmix[:, 1:n], chunk(Lb, cci, 0, n - 1),
                        chunk(Lb, cci, 1, n - 1)),
                        reads=[res("Lb")], writes=[res("tmix")])
                else:
                    off = lnoff + lo
                    vop(lambda e, cci=cci, off=off, n=n: e.tensor_sub(
                        tmix[:, 0:n], chunk(Lb, cci, off - 1, n),
                        chunk(Lb, cci, off, n)),
                        reads=[res("Lb")], writes=[res("tmix")])
                vop(lambda e, cci=cci, tmname=tmname, dstbuf=dstbuf,
                    off=lnoff + lo, n=n:
                    e.scalar_tensor_tensor(
                        chunk(dstbuf, cci, 0, n), tmix[:, 0:n],
                        vecs[tmname][:, cci:cci + 1],
                        chunk(Lb, cci, off, n), OP.mult, OP.add),
                    reads=[res("tmix"), res("vec_" + tmname), cres("Lb", cci)],
                    writes=[cres(dres, cci)])

        def projection(wname, mixbuf, mres, width, evac_fn):
            for s in range(8):
                t_slab, par = load_slab_one(d_w[wname], key=wname)
                for osub in range(2):
                    oc = s * 2 + osub
                    psel = oc % 3
                    fns = []
                    for cci in range(CC):
                        for (o, n) in segs(width, 512):
                            fns.append(lambda e, par=par, cci=cci, osub=osub,
                                       o=o, n=n, psel=psel, mixbuf=mixbuf:
                                       e.matmul(pp[psel][:, o:o + n],
                                                SL[par][:, cci * 256 + osub * P:
                                                        cci * 256 + osub * P + P],
                                                chunk(mixbuf, cci, o, n),
                                                start=(cci == 0),
                                                stop=(cci == CC - 1)))
                    sc.op("tensor", fns,
                          reads=[res("SL%d" % par)] + all_res(mres),
                          writes=[res("pp%d" % psel)], waits_ops=(t_slab,))
                    evac_fn(oc, psel)

        for h in (0, 1):
            for wnm in ("wk", "wv", "wr", "wg"):
                slab_pos.pop(wnm, None)
            ln_w = SW if h == 0 else SW + 1
            lnoff = 0 if h == 0 else 1
            lo = HALO if h == 0 else 0
            now = SW - lo
            gcol0 = 0 if h == 0 else OW0
            d_x = d_xh0 if h == 0 else d_xh1

            t_xbf = dma("gpsimd",
                        Eb[:, :].rearrange("p (c w) -> p c w", w=BW)[:, :, 0:ln_w],
                        d_x.ap(), writes=all_res("Eb"))
            ln_stats_apply(ln_w, "Eb", "Vb", Eb, Vb, Lb)

            mix_into(M1, "M1", "tmk", lnoff, 0, SW, h)
            mix_into(R1, "R1", "tmr", lnoff, lo, now, h if lo == 0 else 1)
            mix_into(M2, "M2", "tmv", lnoff, 0, SW, h)

            projection("wk", M1, "M1", SW,
                       lambda oc, psel: act(chunk(Eb, oc, 0, SW),
                                            pp[psel][:, 0:SW], AF.Exp,
                                            reads=[res("pp%d" % psel)],
                                            writes=[cres("Eb", oc)]))
            projection("wr", R1, "R1", now,
                       lambda oc, psel: act(chunk(R3, oc, 0, now),
                                            pp[psel][:, 0:now], AF.Sigmoid,
                                            reads=[res("pp%d" % psel)],
                                            writes=[cres("R3", oc)]))

            if h == 0:
                for cci in range(CC):
                    vop(lambda e, cci=cci: e.tensor_single_scalar(
                        Eb[:, cci * BW: cci * BW + HALO],
                        Eb[:, cci * BW: cci * BW + HALO],
                        maskt[:, 0:1], OP.mult),
                        reads=[res("maskt")], writes=[cres("Eb", cci)])

            gmix_pend = []

            def gmix_one(cci):
                if h == 0 and lo == 0:
                    pass
                n = now
                off = lnoff + lo
                vop(lambda e, cci=cci, off=off, n=n: e.tensor_sub(
                    tmix[:, 0:n], chunk(Lb, cci, off - 1, n),
                    chunk(Lb, cci, off, n)),
                    reads=[cres("Lb", cci)], writes=[res("tmix")])
                vop(lambda e, cci=cci, off=off, n=n:
                    e.scalar_tensor_tensor(
                        chunk(R1, cci, 0, n), tmix[:, 0:n],
                        vecs["tmg"][:, cci:cci + 1],
                        chunk(Lb, cci, off, n), OP.mult, OP.add),
                    reads=[res("tmix"), res("vec_tmg"), cres("Lb", cci)],
                    writes=[cres("R1", cci)])

            def v_evac(oc, psel):
                sc.op("scalar", lambda e, oc=oc, psel=psel:
                      e.copy(chunk(Vb, oc, 0, SW), pp[psel][:, 0:SW]),
                      reads=[res("pp%d" % psel)], writes=[cres("Vb", oc)])
                cci = oc
                vop(lambda e, cci=cci: e.tensor_mul(
                    chunk(M1, cci, 0, SW), chunk(Eb, cci, 0, SW),
                    chunk(Vb, cci, 0, SW)),
                    reads=[cres("Eb", cci), cres("Vb", cci)],
                    writes=[cres("M1", cci)])
                init_a = 0.0 if h == 0 else stA[:, cci:cci + 1]
                init_b = 0.0 if h == 0 else stB[:, cci:cci + 1]
                wb = wfm[:, cci:cci + 1].broadcast_to([P, SW])
                vop(lambda e, cci=cci, init_a=init_a, wb=wb:
                    e.tensor_tensor_scan(
                    chunk(Vb, cci, 0, SW), wb,
                    chunk(M1, cci, 0, SW), init_a, OP.mult, OP.add),
                    reads=[res("wfm"), cres("M1", cci), res("stA")],
                    writes=[cres("Vb", cci)])
                gmix_one(cci)
                vop(lambda e, cci=cci, init_b=init_b, wb=wb:
                    e.tensor_tensor_scan(
                    chunk(Lb, cci, 0, SW), wb,
                    chunk(Eb, cci, 0, SW), init_b, OP.mult, OP.add),
                    reads=[res("wfm"), cres("Eb", cci), res("stB")],
                    writes=[cres("Lb", cci)])

            projection("wv", M2, "M2", SW, v_evac)

            def epilogue(cci):
                par = cci % 2
                if lo == 0:
                    vop(lambda e, cci=cci, par=par: e.scalar_tensor_tensor(
                        tf1[par][:, 0:1], M1[:, cci * BW: cci * BW + 1],
                        euf[:, cci:cci + 1], stA[:, cci:cci + 1],
                        OP.mult, OP.add),
                        reads=[cres("M1", cci), res("euf"), res("stA")],
                        writes=[res("tf1_%d" % par)])
                    vop(lambda e, cci=cci, par=par: e.scalar_tensor_tensor(
                        tf2[par][:, 0:1], Eb[:, cci * BW: cci * BW + 1],
                        euf[:, cci:cci + 1], stB[:, cci:cci + 1],
                        OP.mult, OP.add),
                        reads=[cres("Eb", cci), res("euf"), res("stB")],
                        writes=[res("tf2_%d" % par)])
                    o1, n1 = 1, now - 1
                else:
                    o1, n1 = 0, now
                vop(lambda e, cci=cci, par=par, lo=lo, o1=o1, n1=n1:
                    e.scalar_tensor_tensor(
                    tf1[par][:, o1:o1 + n1], chunk(M1, cci, lo + o1, n1),
                    euf[:, cci:cci + 1], chunk(Vb, cci, lo + o1 - 1, n1),
                    OP.mult, OP.add),
                    reads=[cres("M1", cci), cres("Vb", cci), res("euf")],
                    writes=[res("tf1_%d" % par)])
                vop(lambda e, cci=cci, par=par, lo=lo, o1=o1, n1=n1:
                    e.scalar_tensor_tensor(
                    tf2[par][:, o1:o1 + n1], chunk(Eb, cci, lo + o1, n1),
                    euf[:, cci:cci + 1], chunk(Lb, cci, lo + o1 - 1, n1),
                    OP.mult, OP.add),
                    reads=[cres("Eb", cci), cres("Lb", cci), res("euf")],
                    writes=[res("tf2_%d" % par)])
                act(tf3[:, 0:now], tf2[par][:, 0:now], AF.Ln,
                    reads=[res("tf2_%d" % par)], writes=[res("tf3")])
                act(tf2[par][:, 0:now], tf3[:, 0:now], AF.Exp, scale=-1.0,
                    reads=[res("tf3")], writes=[res("tf2_%d" % par)])

            def wkv_mul(cci):
                par = cci % 2
                vop(lambda e, cci=cci, par=par, lo=lo, now=now: e.tensor_mul(
                    chunk(M1, cci, lo, now), tf1[par][:, 0:now],
                    tf2[par][:, 0:now]),
                    reads=[res("tf1_%d" % par), res("tf2_%d" % par)],
                    writes=[cres("M1", cci)])

            if h == 0:
                vop(lambda e: e.tensor_copy(
                    stA[:, :],
                    Vb[:, :].rearrange("p (c w) -> p c w", w=BW)[:, :, SW - 1:SW]),
                    reads=all_res("Vb"), writes=[res("stA")])
                vop(lambda e: e.tensor_copy(
                    stB[:, :],
                    Lb[:, :].rearrange("p (c w) -> p c w", w=BW)[:, :, SW - 1:SW]),
                    reads=all_res("Lb"), writes=[res("stB")])
            def g_evac(oc, psel):
                sc.op("scalar", lambda e, oc=oc, psel=psel, now=now:
                      e.copy(chunk(M2, oc, 0, now), pp[psel][:, 0:now]),
                      reads=[res("pp%d" % psel)], writes=[cres("M2", oc)])

            projection("wg", R1, "R1", now, g_evac)

            pend = []
            for cci in range(CC):
                epilogue(cci)
                for f in pend:
                    f()
                pend = [lambda cci=cci: wkv_mul(cci)]
            for f in pend:
                f()

            for cci in range(CC):
                tgb, tgr = (tmix, "tmix") if cci % 2 == 0 else (tf3, "tf3")
                act(tgb[:, 0:now], chunk(M2, cci, 0, now), AF.Sigmoid,
                    reads=[cres("M2", cci)], writes=[res(tgr)])
                tpar = cci % 2
                vop(lambda e, cci=cci, tgb=tgb, lo=lo, now=now, tpar=tpar:
                    e.tensor_mul(
                    tf1[tpar][:, 0:now], tgb[:, 0:now],
                    chunk(M1, cci, lo, now)),
                    reads=[res(tgr), cres("M1", cci)],
                    writes=[res("tf1_%d" % tpar)])
                vop(lambda e, cci=cci, tpar=tpar, gcol0=gcol0, now=now:
                    e.tensor_mul(
                    ag[:, cci * TOWN + gcol0: cci * TOWN + gcol0 + now],
                    tf1[tpar][:, 0:now], chunk(R3, cci, 0, now)),
                    reads=[res("tf1_%d" % tpar), cres("R3", cci)],
                    writes=[res("ag")])

        if dbg:
            for nm, buf, wdt in (("dbg_ag", ag, CC * TOWN),
                                 ("dbg_M1", M1, CC * BW),
                                 ("dbg_R3", R3, CC * BW),
                                 ("dbg_Lb", Lb, CC * BW),
                                 ("dbg_R1", R1, CC * BW),
                                 ("dbg_M2", M2, CC * BW)):
                dd = nc.dram_tensor(nm, [P, wdt], BF16, kind="ExternalOutput")
                bn = nm.replace("dbg_", "") if nm != "dbg_ag" else "ag"
                dma("sync", dd.ap(), buf[:, :],
                    reads=([res("ag")] if bn == "ag" else all_res(bn)))
    fence()

    with ExitStack() as pstack:
        def psb(name, shape, dt):
            return pstack.enter_context(nc.sbuf_tensor(name, shape, dt))

        acc = psb("acc", [P, CC * TOWN], F32)
        tp1 = psb("tp1", [P, TOWN], F32)
        tbf = psb("tbf", [P, TOWN], BF16)
        mu_b = psb("mu_b", [P, TOWN], F32)
        rs_b = psb("rs_b", [P, TOWN], F32)
        mu_row = psb("mu_row2", [1, TOWN], F32)
        s2_row = psb("s2_row2", [1, TOWN], F32)
        rs_row = psb("rs_row2", [1, TOWN], F32)
        wstack = ExitStack()
        attn_o = wstack.enter_context(nc.sbuf_tensor("attn_o", [P, CC * TOWN], BF16))
        qpsum = ExitStack()
        po = [qpsum.enter_context(nc.psum_tensor("po0", [P, TOWN], F32)),
              qpsum.enter_context(nc.psum_tensor("po1", [P, TOWN], F32))]
        ps2_mu = qpsum.enter_context(nc.psum_tensor("ps2_mu", [1, TOWN], F32))
        ps2_s2 = qpsum.enter_context(nc.psum_tensor("ps2_s2", [1, TOWN], F32))

        dma("sync", acc[:, :].rearrange("p (c w) -> p c w", w=TOWN),
            d_xres.ap(), writes=[res("acc")])

        def stream_stats(oc, srcbuf, srcres, start):
            vop(lambda e, oc=oc, srcbuf=srcbuf: e.tensor_copy(
                tbf[:, :], srcbuf[:, oc * TOWN:(oc + 1) * TOWN]),
                reads=[res(srcres)], writes=[res("tbf")])
            for stepi in range(2):
                psum, pres = ((ps2_mu, "ps2_mu"), (ps2_s2, "ps2_s2"))[stepi]
                if stepi == 1:
                    act(tbf[:, :], tbf[:, :], AF.Square, writes=[res("tbf")])
                fns = []
                for (o, n) in segs(TOWN, 512):
                    fns.append(lambda e, psum=psum, o=o, n=n:
                               e.matmul(psum[0:1, o:o + n], onesm[:, 0:1],
                                        tbf[:, o:o + n],
                                        start=start, stop=(oc == CC - 1)))
                sc.op("tensor", fns, reads=[res("tbf"), res("onesm")],
                      writes=[] if not start else [res(pres)],
                      waits_ops=(R[pres].writer,) if (start and R[pres].writer
                                                      is not None) else ())
                if not start:
                    R[pres].writer = sc.ops[-1]["idx"]

        def finalize_rows_and_bcast():
            vop(lambda e: e.tensor_copy(mu_row[0:1, 0:TOWN], ps2_mu[0:1, 0:TOWN]),
                reads=[res("ps2_mu")], writes=[res("mu_row")])
            vop(lambda e: e.tensor_copy(s2_row[0:1, 0:TOWN], ps2_s2[0:1, 0:TOWN]),
                reads=[res("ps2_s2")], writes=[res("s2_row")])
            vop(lambda e: e.tensor_mul(rs_row[0:1, 0:TOWN], mu_row[0:1, 0:TOWN],
                                       mu_row[0:1, 0:TOWN]),
                reads=[res("mu_row")], writes=[res("rs_row")])
            vop(lambda e: e.tensor_sub(s2_row[0:1, 0:TOWN], s2_row[0:1, 0:TOWN],
                                       rs_row[0:1, 0:TOWN]),
                reads=[res("rs_row")], writes=[res("s2_row")])
            vop(lambda e: e.tensor_single_scalar(
                s2_row[0:1, 0:TOWN], s2_row[0:1, 0:TOWN], EPS, OP.add),
                writes=[res("s2_row")])
            act(s2_row[0:1, 0:TOWN], s2_row[0:1, 0:TOWN], AF.Ln,
                writes=[res("s2_row")])
            act(rs_row[0:1, 0:TOWN], s2_row[0:1, 0:TOWN], AF.Exp, scale=-0.5,
                reads=[res("s2_row")], writes=[res("rs_row")])
            for row, rres, dstb, dres, psel in (
                    (mu_row, "mu_row", mu_b, "mu_b", 0),
                    (rs_row, "rs_row", rs_b, "rs_b", 1)):
                fns = []
                for (o, n) in segs(TOWN, 512):
                    fns.append(lambda e, row=row, o=o, n=n, psel=psel:
                               e.matmul(po[psel][0:128, o:o + n], onesr[0:1, 0:P],
                                        row[0:1, o:o + n], start=True, stop=True))
                sc.op("tensor", fns, reads=[res(rres), res("onesr")],
                      writes=[res("po%d" % psel)])
                vop(lambda e, dstb=dstb, psel=psel:
                    e.tensor_copy(dstb[:, 0:TOWN], po[psel][:, 0:TOWN]),
                    reads=[res("po%d" % psel)], writes=[res(dres)])

        pend_stats = []
        for s in range(8):
            t_slab, par = load_slab_one(d_w["wo"], key="wo")
            for osub in range(2):
                oc = s * 2 + osub
                psel = oc % 2
                fns = []
                for cci in range(CC):
                    for (o, n) in segs(TOWN, 512):
                        fns.append(lambda e, par=par, cci=cci, osub=osub, o=o,
                                   n=n, psel=psel:
                                   e.matmul(po[psel][:, o:o + n],
                                            SL[par][:, cci * 256 + osub * P:
                                                    cci * 256 + osub * P + P],
                                            ag[:, cci * TOWN + o:
                                               cci * TOWN + o + n],
                                            start=(cci == 0),
                                            stop=(cci == CC - 1)))
                sc.op("tensor", fns, reads=[res("SL%d" % par), res("ag")],
                      writes=[res("po%d" % psel)], waits_ops=(t_slab,))
                sc.op("scalar", lambda e, oc=oc, psel=psel:
                      e.copy(attn_o[:, oc * TOWN:(oc + 1) * TOWN],
                             po[psel][:, 0:TOWN]),
                      reads=[res("po%d" % psel)], writes=[res("attn_o")])
                if len(pend_stats) >= 2:
                    pend_stats.pop(0)()
                pend_stats.append(lambda oc=oc: stream_stats(
                    oc, attn_o, "attn_o", start=(oc == 0)))
        for f in pend_stats:
            f()
        finalize_rows_and_bcast()
        for cci in range(CC):
            sc.op("gpsimd", lambda e, cci=cci: e.tensor_sub(
                tp1[:, :], attn_o[:, cci * TOWN:(cci + 1) * TOWN], mu_b[:, :]),
                reads=[res("attn_o"), res("mu_b")], writes=[res("tp1")])
            vop(lambda e: e.tensor_mul(tp1[:, :], tp1[:, :], rs_b[:, :]),
                reads=[res("rs_b")], writes=[res("tp1")])
            vop(lambda e, cci=cci: e.tensor_scalar(
                tp1[:, :], tp1[:, :], vecs["lng"][:, cci:cci + 1],
                vecs["lnb"][:, cci:cci + 1], OP.mult, OP.add),
                reads=[res("vec_lng"), res("vec_lnb")], writes=[res("tp1")])
            vop(lambda e, cci=cci: e.tensor_add(
                acc[:, cci * TOWN:(cci + 1) * TOWN],
                acc[:, cci * TOWN:(cci + 1) * TOWN], tp1[:, :]),
                reads=[res("tp1")], writes=[res("acc")])
            stream_stats(cci, acc, "acc", start=(cci == 0))

        if dbg:
            dd = nc.dram_tensor("dbg_attn_o", [P, CC * TOWN], BF16,
                                kind="ExternalOutput")
            dma("sync", dd.ap(), attn_o[:, :], reads=[res("attn_o")])
        wstack.close()
        fence()
        ln2 = psb("ln2", [P, CC * TOWN], BF16)
        finalize_rows_and_bcast()
        for cci in range(CC):
            sc.op("gpsimd", lambda e, cci=cci: e.tensor_sub(
                tp1[:, :], acc[:, cci * TOWN:(cci + 1) * TOWN], mu_b[:, :]),
                reads=[res("acc"), res("mu_b")], writes=[res("tp1")])
            vop(lambda e, cci=cci: e.tensor_mul(
                ln2[:, cci * TOWN:(cci + 1) * TOWN], tp1[:, :], rs_b[:, :]),
                reads=[res("tp1"), res("rs_b")], writes=[res("ln2")])
        if dbg:
            dd = nc.dram_tensor("dbg_ln2", [P, CC * TOWN], BF16,
                                kind="ExternalOutput")
            dma("sync", dd.ap(), ln2[:, :], reads=[res("ln2")])
        qpsum.close()

        with ExitStack() as fstack:
            def fsb(name, shape, dt):
                return fstack.enter_context(nc.sbuf_tensor(name, shape, dt))

            def fps(name, shape, dt=F32):
                return fstack.enter_context(nc.psum_tensor(name, shape, dt))

            hs = [ag[:, 0:8 * TOWN], ag[:, 8 * TOWN:16 * TOWN]]
            wdb = [fsb("wdb0", [P, 8 * P], BF16), fsb("wdb1", [P, 8 * P], BF16)]
            th2 = [fsb("th0", [P, TOWN], BF16), fsb("th1", [P, TOWN], BF16)]
            pg = fps("pg", [P, TOWN], F32)
            pu = fps("pu", [P, TOWN], F32)
            pd = [fps("pd0", [P, TOWN], F32), fps("pd1", [P, TOWN], F32)]
            def emit_down(s, hc):
                hpar = s % 2
                wpar = hc % 2
                dma("gpsimd",
                    wdb[wpar][:, :].rearrange("p (i w) -> p i w", w=P),
                    d_wd.ap()[s][hc], writes=[res("wdb%d" % wpar)])
                fns = []
                for ic in range(8):
                    for (o, n) in segs(TOWN, 512):
                        fns.append(lambda e, wpar=wpar, ic=ic, o=o, n=n,
                                   hpar=hpar:
                                   e.matmul(pd[wpar][:, o:o + n],
                                            wdb[wpar][:, ic * P:(ic + 1) * P],
                                            hs[hpar][:, ic * TOWN + o:
                                                     ic * TOWN + o + n],
                                            start=(ic == 0), stop=(ic == 7)))
                sc.op("tensor", fns,
                      reads=[res("wdb%d" % wpar), res("hs%d" % hpar)],
                      writes=[res("pd%d" % wpar)])
                vop(lambda e, hc=hc, wpar=wpar: e.tensor_add(
                    acc[:, hc * TOWN:(hc + 1) * TOWN],
                    acc[:, hc * TOWN:(hc + 1) * TOWN], pd[wpar][:, :]),
                    reads=[res("pd%d" % wpar)], writes=[res("acc")])

            down_pend = []
            for s in range(8):
                hpar = s % 2
                for ic in range(8):
                    gic = 8 * s + ic
                    slab = gic // 2
                    within = gic % 2
                    if within == 0:
                        tg_, parg = load_slab_one(d_wgu_g, key="g")
                        tu_, paru = load_slab_one(d_wgu_u, key="u")
                        sc._ffn_cur = {"g": (tg_, parg), "u": (tu_, paru)}
                    cur = sc._ffn_cur
                    for which in ("g", "u"):
                        t_slab, par = cur[which]
                        psum, pres = (pg, "pg") if which == "g" else (pu, "pu")
                        fns = []
                        for cci in range(CC):
                            for (o, n) in segs(TOWN, 512):
                                fns.append(lambda e, par=par, cci=cci,
                                           within=within, o=o, n=n, psum=psum:
                                           e.matmul(psum[:, o:o + n],
                                                    SL[par][:, cci * 256
                                                            + within * P:
                                                            cci * 256
                                                            + within * P + P],
                                                    ln2[:, cci * TOWN + o:
                                                        cci * TOWN + o + n],
                                                    start=(cci == 0),
                                                    stop=(cci == CC - 1)))
                        sc.op("tensor", fns,
                              reads=[res("SL%d" % par), res("ln2")],
                              writes=[res(pres)], waits_ops=(t_slab,))
                    tpar = ic % 2
                    act(th2[tpar][:, :], pg[:, :], AF.Silu,
                        reads=[res("pg")], writes=[res("th_%d" % tpar)])
                    vop(lambda e, hpar=hpar, ic=ic, tpar=tpar: e.tensor_mul(
                        hs[hpar][:, ic * TOWN:(ic + 1) * TOWN],
                        th2[tpar][:, :], pu[:, :]),
                        reads=[res("th_%d" % tpar), res("pu")],
                        writes=[res("hs%d" % hpar)])
                    for _ in range(2):
                        if down_pend:
                            emit_down(*down_pend.pop(0))
                down_pend.extend((s, hc) for hc in range(CC))
            for s_hc in down_pend:
                emit_down(*s_hc)

        t_out = dma("sync", d_out.ap(),
                    acc[:, :].rearrange("p (c w) -> p c w", w=TOWN),
                    reads=[res("acc")])
        sc.op("sync", lambda e: e.nop(), waits_ops=(t_out,))

    sc.emit(sems)
    nc.finalize()
    stack.close()
    return nc



def _swizzle_w(wT, oslab):
    cin, o = wT.shape
    return np.ascontiguousarray(
        wT.reshape(CC, P, o // oslab, oslab).transpose(2, 1, 0, 3))


def _vec_fm(v):
    return np.ascontiguousarray(v.reshape(CC, P).T)


def _x_fm(xs):
    t = xs.shape[0]
    return np.ascontiguousarray(xs.T.reshape(CC, P, t).transpose(1, 0, 2))


def kernel(L0_attn_x, L0_attn_time_decay, L0_attn_time_first, L0_attn_tm_k,
           L0_attn_tm_v, L0_attn_tm_r, L0_attn_tm_g, L0_attn_Wr, L0_attn_Wk,
           L0_attn_Wv, L0_attn_Wg, L0_attn_Wo, L0_attn_ln_g, L0_attn_ln_b,
           L0_ffn_w_gate_up, L0_ffn_w_down):
    TOWN = 1024
    TH = TOWN + HALO
    SW = TH // 2

    x = np.asarray(L0_attn_x, dtype=np.float32)[0]
    T = x.shape[0]
    assert T == NCORES * TOWN

    wk = _swizzle_w(np.ascontiguousarray(np.asarray(L0_attn_Wk, np.float32).T), 256)
    wv = _swizzle_w(np.ascontiguousarray(np.asarray(L0_attn_Wv, np.float32).T), 256)
    wr = _swizzle_w(np.ascontiguousarray(np.asarray(L0_attn_Wr, np.float32).T), 256)
    wg = _swizzle_w(np.ascontiguousarray(np.asarray(L0_attn_Wg, np.float32).T), 256)
    wo = _swizzle_w(np.ascontiguousarray(np.asarray(L0_attn_Wo, np.float32).T), 256)
    wguT = np.ascontiguousarray(np.asarray(L0_ffn_w_gate_up, np.float32).T)
    wgu_g = _swizzle_w(np.ascontiguousarray(wguT[:, 0:INTER]), 256)
    wgu_u = _swizzle_w(np.ascontiguousarray(wguT[:, INTER:]), 256)
    wdT = np.ascontiguousarray(np.asarray(L0_ffn_w_down, np.float32).T)
    wd = np.ascontiguousarray(
        wdT.reshape(8, 8, P, CC, P).transpose(0, 3, 2, 1, 4))

    vec_common = {
        "tmk": _vec_fm(np.asarray(L0_attn_tm_k, np.float32)),
        "tmv": _vec_fm(np.asarray(L0_attn_tm_v, np.float32)),
        "tmr": _vec_fm(np.asarray(L0_attn_tm_r, np.float32)),
        "tmg": _vec_fm(np.asarray(L0_attn_tm_g, np.float32)),
        "td": _vec_fm(np.asarray(L0_attn_time_decay, np.float32)),
        "tf": _vec_fm(np.asarray(L0_attn_time_first, np.float32)),
        "lng": _vec_fm(np.asarray(L0_attn_ln_g, np.float32)),
        "lnb": _vec_fm(np.asarray(L0_attn_ln_b, np.float32)),
    }
    w_common = {"wk": wk, "wv": wv, "wr": wr, "wg": wg, "wo": wo,
                "wgu_g": wgu_g, "wgu_u": wgu_u, "wd": wd}

    in_maps = []
    for c in range(NCORES):
        base = c * TOWN
        if c == 0:
            halo_rows = np.concatenate([x[0:HALO - 1], x[1:2]], axis=0)
            mask = np.zeros((P, 1), np.float32)
        else:
            halo_rows = x[base - HALO: base]
            mask = np.ones((P, 1), np.float32)
        xa = np.concatenate([halo_rows, x[base: base + TOWN]], axis=0)
        m = {
            "x_h0": _x_fm(xa[0:SW]),
            "x_h1": _x_fm(xa[SW - 1: TH]),
            "x_res": _x_fm(xa[HALO: TH]),
            "mask": mask,
        }
        m.update(vec_common)
        m.update(w_common)
        in_maps.append(m)

    nc = build_program(TOWN)
    r = run_bass_kernel_spmd(nc, in_maps, core_ids=list(range(NCORES)))
    outs = []
    for c in range(NCORES):
        o = np.asarray(r.results[c]["out"])
        outs.append(o.transpose(1, 0, 2).reshape(C, TOWN).T)
    full = np.concatenate(outs, axis=0)[None]
    return full.astype(np.float32)


# version 22
# speedup vs baseline: 1.0986x; 1.0167x over previous
import numpy as np
import ml_dtypes
import concourse.bass as bass
import concourse.mybir as mybir
from concourse.alu_op_type import AluOpType as OP
from concourse.bass_utils import run_bass_kernel_spmd

F32 = mybir.dt.float32
BF16 = mybir.dt.bfloat16
AF = mybir.ActivationFunctionType

P = 128
CC = 16
C = 2048
NCORES = 8
HALO = 128
EPS = 1e-5
INTER = 8192

ENGINES = ("tensor", "vector", "scalar", "gpsimd", "sync")
OWN_SEM = {"tensor": "P", "vector": "V", "scalar": "A", "gpsimd": "G", "sync": None}


def segs(n, cap=512):
    out = []
    o = 0
    while o < n:
        s = min(cap, n - o)
        out.append((o, s))
        o += s
    return out


class Res:

    def __init__(self, name):
        self.name = name
        self.writer = None
        self.readers = []


class Sched:

    def __init__(self, nc):
        self.nc = nc
        self.ops = []
        self.last_eng = {}
        self.barrier_idx = None

    def barrier(self):
        deps = [i for i in self.last_eng.values()]
        idx = self.op("vector", lambda e: e.engine_nop(), waits_ops=deps)
        self.barrier_idx = idx
        return idx

    def op(self, eng, fns, reads=(), writes=(), waits_ops=(), dma=False,
           dma_sem=None):
        if not isinstance(fns, (list, tuple)):
            fns = [fns]
        idx = len(self.ops)
        deps = set(waits_ops)
        rnames = [r.name for r in reads]
        wnames = [r.name for r in writes]
        for r in reads:
            if r.writer is not None:
                deps.add(r.writer)
            r.readers.append(idx)
        for r in writes:
            if r.writer is not None:
                deps.add(r.writer)
            deps.update(r.readers)
            r.writer = idx
            r.readers = []
        if dma:
            sem = dma_sem or ("dG" if eng == "gpsimd" else "dS")
        else:
            sem = OWN_SEM[eng]
        self.ops.append(dict(eng=eng, fns=list(fns), sem=sem, dma=dma,
                             deps=deps, idx=idx, rnames=rnames, wnames=wnames))
        self.last_eng[eng] = idx
        return idx

    def emit(self, sems):
        ops = self.ops
        needed = set()
        for o in ops:
            own = OWN_SEM[o["eng"]]
            for d in o["deps"]:
                dop = ops[d]
                if (not dop["dma"]) and dop["sem"] is not None and dop["sem"] == own:
                    continue
                needed.add(d)
        inc_count = {}
        op_val = {}
        for o in ops:
            s = o["sem"]
            if o["dma"]:
                inc_count[s] = inc_count.get(s, 0) + 16
                op_val[o["idx"]] = (s, inc_count[s])
            elif o["idx"] in needed and s is not None:
                inc_count[s] = inc_count.get(s, 0) + 1
                op_val[o["idx"]] = (s, inc_count[s])
            else:
                op_val[o["idx"]] = None
        per_eng = {e: [] for e in ENGINES}
        hw = {e: {} for e in ENGINES}
        for o in ops:
            own = OWN_SEM[o["eng"]]
            wmap = {}
            for d in sorted(o["deps"]):
                dop = ops[d]
                if (not dop["dma"]) and dop["sem"] is not None and dop["sem"] == own:
                    continue
                v = op_val[d]
                if v is None:
                    raise RuntimeError("dep without inc")
                s, val = v
                if wmap.get(s, 0) < val:
                    wmap[s] = val
            waits = []
            for s, val in sorted(wmap.items()):
                if hw[o["eng"]].get(s, 0) >= val:
                    continue
                hw[o["eng"]][s] = val
                waits.append((s, val))
            inc = op_val[o["idx"]]
            myinc = None
            if o["dma"]:
                myinc = (inc[0], 16)
            elif inc is not None:
                myinc = (inc[0], 1)
            per_eng[o["eng"]].append((waits, o["fns"], myinc, o))
        self._final_counts = dict(inc_count)

        audit = getattr(self, "audit", None)

        def runner(items):
            def go(e):
                for waits, fns, myinc, o in items:
                    insts = []
                    for s, val in waits:
                        e.wait_ge(sems[s], val)
                    last = None
                    for fn in fns:
                        last = fn(e)
                        insts.append(last)
                    if myinc is not None:
                        last.then_inc(sems[myinc[0]], myinc[1])
                    if audit is not None:
                        o["_insts"] = insts
            return go

        with self.nc.Block() as blk:
            for eng in ENGINES:
                getattr(blk, eng)(runner(per_eng[eng]))


def build_program(TOWN=1024, dbg=False):
    TH = TOWN + HALO
    SW = TH // 2
    BW = SW + 1
    assert TH % 2 == 0 and SW > HALO
    OW0 = SW - HALO

    nc = bass.Bass("TRN2", target_bir_lowering=False,
                   detect_race_conditions=False)

    d_xh0 = nc.dram_tensor("x_h0", [P, CC, SW], F32, kind="ExternalInput")
    d_xh1 = nc.dram_tensor("x_h1", [P, CC, SW + 1], F32, kind="ExternalInput")
    d_xres = nc.dram_tensor("x_res", [P, CC, TOWN], F32, kind="ExternalInput")
    d_w = {}
    for nm in ("wk", "wv", "wr", "wg", "wo"):
        d_w[nm] = nc.dram_tensor(nm, [8, P, CC, 256], F32, kind="ExternalInput")
    d_wgu_g = nc.dram_tensor("wgu_g", [32, P, CC, 256], F32, kind="ExternalInput")
    d_wgu_u = nc.dram_tensor("wgu_u", [32, P, CC, 256], F32, kind="ExternalInput")
    d_wd = nc.dram_tensor("wd", [8, CC, P, 8, P], F32, kind="ExternalInput")
    d_vec = {}
    for nm in ("tmk", "tmv", "tmr", "tmg", "td", "tf", "lng", "lnb"):
        d_vec[nm] = nc.dram_tensor(nm, [P, CC], F32, kind="ExternalInput")
    d_mask = nc.dram_tensor("mask", [P, 1], F32, kind="ExternalInput")
    d_out = nc.dram_tensor("out", [P, CC, TOWN], F32, kind="ExternalOutput")

    from contextlib import ExitStack
    stack = ExitStack()
    sems = {}
    for s in ("dG", "dS", "dW", "P", "V", "A", "G"):
        sems[s] = stack.enter_context(nc.semaphore(s))

    sc = Sched(nc)

    def sb(name, shape, dt):
        return stack.enter_context(nc.sbuf_tensor(name, shape, dt))

    vecs = {nm: sb("sb_" + nm, [P, CC], F32) for nm in d_vec}
    maskt = sb("maskt", [P, 1], F32)
    wfm = sb("wfm", [P, CC], F32)
    euf = sb("euf", [P, CC], F32)
    s2f = sb("s2f", [P, CC], F32)
    e1f = sb("e1f", [P, CC], F32)
    onesm = sb("onesm", [P, 1], BF16)
    onesr = sb("onesr", [1, P], F32)
    stA = sb("stA", [P, CC], F32)
    stB = sb("stB", [P, CC], F32)
    ag = sb("ag", [P, CC * TOWN], BF16)
    SL = [sb("SL0", [P, CC * 256], BF16), sb("SL1", [P, CC * 256], BF16)]

    R = {}

    def res(name):
        if name not in R:
            R[name] = Res(name)
            R[name].writer = sc.barrier_idx
        return R[name]

    def fence():
        idx = sc.barrier()
        for r in R.values():
            r.writer = idx
            r.readers = []
        return idx

    def cres(buf, cc):
        return res("%s_%d" % (buf, cc))

    def all_res(buf):
        return [cres(buf, i) for i in range(CC)]

    def dma(eng, dst_ap, src_ap, reads=(), writes=(), waits=(), dma_sem=None):
        return sc.op(eng, lambda e, d=dst_ap, s=src_ap: e.dma_start(d, s),
                     reads=reads, writes=writes, waits_ops=waits, dma=True,
                     dma_sem=dma_sem)

    def act(out_ap, in_ap, func, bias=0.0, scale=1.0, reads=(), writes=(), waits=()):
        return sc.op("scalar",
                     lambda e, o=out_ap, i=in_ap, f=func, b=bias, s=scale:
                     e.activation(o, i, f, bias=b, scale=s),
                     reads=reads, writes=writes, waits_ops=waits)

    def vop(fn, reads=(), writes=(), waits=()):
        return sc.op("vector", fn, reads=reads, writes=writes, waits_ops=waits)

    slab_seq = [0]
    slab_pos = {}

    def load_slab_one(dram, key=None):
        k = key if key is not None else id(dram)
        s = slab_pos.get(k, 0)
        slab_pos[k] = s + 1
        par = slab_seq[0] % 2
        slab_seq[0] += 1
        t = dma("gpsimd",
                SL[par][:, :].rearrange("p (c w) -> p c w", w=256),
                dram.ap()[s],
                writes=[res("SL%d" % par)])
        return t, par

    for nm in d_vec:
        dma("sync", vecs[nm][:, :], d_vec[nm].ap(), writes=[res("vec_" + nm)])
    dma("sync", maskt[:, :], d_mask.ap(), writes=[res("maskt")])
    vop(lambda e: e.memset(onesm[:, :], 1.0 / 2048.0), writes=[res("onesm")])
    vop(lambda e: e.memset(onesr[:, :], 1.0), writes=[res("onesr")])
    act(e1f[:, :], vecs["td"][:, :], AF.Exp,
        reads=[res("vec_td")], writes=[res("e1f")])
    act(euf[:, :], vecs["tf"][:, :], AF.Exp,
        reads=[res("vec_tf")], writes=[res("euf")])
    vop(lambda e: e.memset(wfm[:, 0:1], 1.0), writes=[res("wfm")])
    for fn in (AF.Square, AF.Sigmoid, AF.Silu, AF.Ln, AF.Exp):
        act(wfm[:, 0:1], wfm[:, 0:1], fn, writes=[res("wfm")])
    act(wfm[:, :], e1f[:, :], AF.Exp, scale=-1.0,
        reads=[res("e1f")], writes=[res("wfm")])
    vop(lambda e: e.tensor_mul(s2f[:, :], euf[:, :], wfm[:, :]),
        reads=[res("euf"), res("wfm")], writes=[res("s2f")])
    vop(lambda e: e.tensor_scalar(s2f[:, :], s2f[:, :], -1.0, 1.0,
                                  OP.mult, OP.add), writes=[res("s2f")])
    fence()

    with ExitStack() as hstack:
        def hsb(name, shape, dt):
            return hstack.enter_context(nc.sbuf_tensor(name, shape, dt))

        def hps(name, shape, dt=F32):
            return hstack.enter_context(nc.psum_tensor(name, shape, dt))

        Lb = hsb("Lb", [P, CC * BW], BF16)
        Eb = hsb("Eb", [P, CC * BW], BF16)
        Vb = hsb("Vb", [P, CC * BW], BF16)
        M1 = hsb("M1", [P, CC * BW], BF16)
        M2 = hsb("M2", [P, CC * BW], BF16)
        R1 = hsb("R1", [P, CC * BW], BF16)
        R3 = hsb("R3", [P, CC * BW], BF16)
        tmix = hsb("tmix", [P, BW], BF16)
        tf1 = [hsb("tf1a", [P, BW], F32), hsb("tf1b", [P, BW], F32)]
        tf2 = [hsb("tf2a", [P, BW], F32), hsb("tf2b", [P, BW], F32)]
        tf3 = hsb("tf3a", [P, BW], F32)
        mub_h = hsb("mub_h", [P, BW], F32)
        rsb_h = hsb("rsb_h", [P, BW], F32)
        hmu_row = hsb("hmu_row", [1, BW], F32)
        hs2_row = hsb("hs2_row", [1, BW], F32)
        hrs_row = hsb("hrs_row", [1, BW], F32)
        pp = [hps("pp0", [P, BW], F32), hps("pp1", [P, BW], F32),
              hps("pp2", [P, BW], F32)]
        ps_mu = pp[0]
        ps_s2 = pp[1]

        def chunk(buf, cc, lo, n, stride=BW):
            return buf[:, cc * stride + lo: cc * stride + lo + n]

        def halo_view(buf):
            return buf[:, :].rearrange("p (c w) -> p c w", w=BW)[:, :, 0:HALO]

        def col_view(buf, col):
            return buf[:, :].rearrange("p (c w) -> p c w", w=BW)[:, :, col:col + 1]

        if dbg:
            for buf, bres in ((Lb, "Lb"), (Eb, "Eb"), (Vb, "Vb"),
                              (M1, "M1"), (M2, "M2"), (R1, "R1"), (R3, "R3")):
                vop(lambda e, buf=buf: e.memset(buf[:, :], 0.0),
                    writes=[res(bres)] + all_res(bres))

        def ln_stats_apply(ln_w, xbf_res, sq_res, xbuf, sqbuf, outbuf):
            xv3 = xbuf[:, :].rearrange("p (c w) -> p c w", w=BW)[:, :, 0:ln_w]
            sv3 = sqbuf[:, :].rearrange("p (c w) -> p c w", w=BW)[:, :, 0:ln_w]
            act(sv3, xv3, AF.Square, reads=all_res(xbf_res),
                writes=all_res(sq_res))
            for stat, (srcb, sres, psum, pres) in enumerate(
                    ((xbuf, xbf_res, ps_mu, "ps_mu"),
                     (sqbuf, sq_res, ps_s2, "ps_s2"))):
                for (o, n) in segs(ln_w, 512):
                    fns = []
                    for cci in range(CC):
                        fns.append(lambda e, srcb=srcb, psum=psum, o=o, n=n, cci=cci:
                                   e.matmul(psum[0:1, o:o + n], onesm[:, 0:1],
                                            chunk(srcb, cci, o, n),
                                            start=(cci == 0), stop=(cci == CC - 1)))
                    sc.op("tensor", fns, reads=all_res(sres) + [res("onesm")],
                          writes=[res("pp0" if pres == "ps_mu" else "pp1")])
            vop(lambda e: e.tensor_copy(hmu_row[0:1, 0:ln_w], ps_mu[0:1, 0:ln_w]),
                reads=[res("pp0")], writes=[res("hmu_row")])
            vop(lambda e: e.tensor_copy(hs2_row[0:1, 0:ln_w], ps_s2[0:1, 0:ln_w]),
                reads=[res("pp1")], writes=[res("hs2_row")])
            vop(lambda e: e.tensor_mul(hrs_row[0:1, 0:ln_w], hmu_row[0:1, 0:ln_w],
                                       hmu_row[0:1, 0:ln_w]),
                reads=[res("hmu_row")], writes=[res("hrs_row")])
            vop(lambda e: e.tensor_sub(hs2_row[0:1, 0:ln_w], hs2_row[0:1, 0:ln_w],
                                       hrs_row[0:1, 0:ln_w]),
                reads=[res("hrs_row")], writes=[res("hs2_row")])
            vop(lambda e: e.tensor_single_scalar(
                hs2_row[0:1, 0:ln_w], hs2_row[0:1, 0:ln_w], EPS, OP.add),
                writes=[res("hs2_row")])
            act(hs2_row[0:1, 0:ln_w], hs2_row[0:1, 0:ln_w], AF.Ln,
                writes=[res("hs2_row")])
            act(hrs_row[0:1, 0:ln_w], hs2_row[0:1, 0:ln_w], AF.Exp, scale=-0.5,
                reads=[res("hs2_row")], writes=[res("hrs_row")])
            for row, rres, dstb, dres, psel in (
                    (hmu_row, "hmu_row", mub_h, "mub_h", 0),
                    (hrs_row, "hrs_row", rsb_h, "rsb_h", 1)):
                fns = []
                for (o, n) in segs(ln_w, 512):
                    fns.append(lambda e, row=row, o=o, n=n, psel=psel:
                               e.matmul(pp[psel][0:128, o:o + n], onesr[0:1, 0:P],
                                        row[0:1, o:o + n], start=True, stop=True))
                sc.op("tensor", fns, reads=[res(rres), res("onesr")],
                      writes=[res("pp%d" % psel)])
                vop(lambda e, dstb=dstb, psel=psel, ln_w=ln_w:
                    e.tensor_copy(dstb[:, 0:ln_w], pp[psel][:, 0:ln_w]),
                    reads=[res("pp%d" % psel)], writes=[res(dres)])
            for cci in range(CC):
                vop(lambda e, cci=cci, ln_w=ln_w: e.tensor_sub(
                    tf1[0][:, 0:ln_w], chunk(xbuf, cci, 0, ln_w),
                    mub_h[:, 0:ln_w]),
                    reads=[cres(xbf_res, cci), res("mub_h")],
                    writes=[res("tf1_0")])
                vop(lambda e, cci=cci, ln_w=ln_w: e.tensor_mul(
                    chunk(outbuf, cci, 0, ln_w), tf1[0][:, 0:ln_w],
                    rsb_h[:, 0:ln_w]),
                    reads=[res("tf1_0"), res("rsb_h")],
                    writes=[cres("Lb", cci)])

        def mix_into(dstbuf, dres, tmname, lnoff, lo, n, h):
            for cci in range(CC):
                if h == 0 and lo == 0:
                    vop(lambda e: e.memset(tmix[:, 0:1], 0.0),
                        writes=[res("tmix")])
                    vop(lambda e, cci=cci, n=n: e.tensor_sub(
                        tmix[:, 1:n], chunk(Lb, cci, 0, n - 1),
                        chunk(Lb, cci, 1, n - 1)),
                        reads=[res("Lb")], writes=[res("tmix")])
                else:
                    off = lnoff + lo
                    vop(lambda e, cci=cci, off=off, n=n: e.tensor_sub(
                        tmix[:, 0:n], chunk(Lb, cci, off - 1, n),
                        chunk(Lb, cci, off, n)),
                        reads=[res("Lb")], writes=[res("tmix")])
                vop(lambda e, cci=cci, tmname=tmname, dstbuf=dstbuf,
                    off=lnoff + lo, n=n:
                    e.scalar_tensor_tensor(
                        chunk(dstbuf, cci, 0, n), tmix[:, 0:n],
                        vecs[tmname][:, cci:cci + 1],
                        chunk(Lb, cci, off, n), OP.mult, OP.add),
                    reads=[res("tmix"), res("vec_" + tmname), cres("Lb", cci)],
                    writes=[cres(dres, cci)])

        def projection(wname, mixbuf, mres, width, evac_fn):
            for s in range(8):
                t_slab, par = load_slab_one(d_w[wname], key=wname)
                for osub in range(2):
                    oc = s * 2 + osub
                    psel = oc % 3
                    fns = []
                    for cci in range(CC):
                        for (o, n) in segs(width, 512):
                            fns.append(lambda e, par=par, cci=cci, osub=osub,
                                       o=o, n=n, psel=psel, mixbuf=mixbuf:
                                       e.matmul(pp[psel][:, o:o + n],
                                                SL[par][:, cci * 256 + osub * P:
                                                        cci * 256 + osub * P + P],
                                                chunk(mixbuf, cci, o, n),
                                                start=(cci == 0),
                                                stop=(cci == CC - 1)))
                    sc.op("tensor", fns,
                          reads=[res("SL%d" % par)] + all_res(mres),
                          writes=[res("pp%d" % psel)], waits_ops=(t_slab,))
                    evac_fn(oc, psel)

        for h in (0, 1):
            for wnm in ("wk", "wv", "wr", "wg"):
                slab_pos.pop(wnm, None)
            ln_w = SW if h == 0 else SW + 1
            lnoff = 0 if h == 0 else 1
            lo = HALO if h == 0 else 0
            now = SW - lo
            gcol0 = 0 if h == 0 else OW0
            d_x = d_xh0 if h == 0 else d_xh1

            t_xbf = dma("gpsimd",
                        Eb[:, :].rearrange("p (c w) -> p c w", w=BW)[:, :, 0:ln_w],
                        d_x.ap(), writes=all_res("Eb"))
            ln_stats_apply(ln_w, "Eb", "Vb", Eb, Vb, Lb)

            mix_into(M1, "M1", "tmk", lnoff, 0, SW, h)
            mix_into(R1, "R1", "tmr", lnoff, lo, now, h if lo == 0 else 1)
            mix_into(M2, "M2", "tmv", lnoff, 0, SW, h)

            projection("wk", M1, "M1", SW,
                       lambda oc, psel: act(chunk(Eb, oc, 0, SW),
                                            pp[psel][:, 0:SW], AF.Exp,
                                            reads=[res("pp%d" % psel)],
                                            writes=[cres("Eb", oc)]))
            projection("wr", R1, "R1", now,
                       lambda oc, psel: act(chunk(R3, oc, 0, now),
                                            pp[psel][:, 0:now], AF.Sigmoid,
                                            reads=[res("pp%d" % psel)],
                                            writes=[cres("R3", oc)]))

            if h == 0:
                for cci in range(CC):
                    vop(lambda e, cci=cci: e.tensor_single_scalar(
                        Eb[:, cci * BW: cci * BW + HALO],
                        Eb[:, cci * BW: cci * BW + HALO],
                        maskt[:, 0:1], OP.mult),
                        reads=[res("maskt")], writes=[cres("Eb", cci)])

            gmix_pend = []

            def gmix_one(cci):
                if h == 0 and lo == 0:
                    pass
                n = now
                off = lnoff + lo
                vop(lambda e, cci=cci, off=off, n=n: e.tensor_sub(
                    tmix[:, 0:n], chunk(Lb, cci, off - 1, n),
                    chunk(Lb, cci, off, n)),
                    reads=[cres("Lb", cci)], writes=[res("tmix")])
                vop(lambda e, cci=cci, off=off, n=n:
                    e.scalar_tensor_tensor(
                        chunk(R1, cci, 0, n), tmix[:, 0:n],
                        vecs["tmg"][:, cci:cci + 1],
                        chunk(Lb, cci, off, n), OP.mult, OP.add),
                    reads=[res("tmix"), res("vec_tmg"), cres("Lb", cci)],
                    writes=[cres("R1", cci)])

            def v_evac(oc, psel):
                sc.op("scalar", lambda e, oc=oc, psel=psel:
                      e.copy(chunk(Vb, oc, 0, SW), pp[psel][:, 0:SW]),
                      reads=[res("pp%d" % psel)], writes=[cres("Vb", oc)])
                cci = oc
                vop(lambda e, cci=cci: e.tensor_mul(
                    chunk(M1, cci, 0, SW), chunk(Eb, cci, 0, SW),
                    chunk(Vb, cci, 0, SW)),
                    reads=[cres("Eb", cci), cres("Vb", cci)],
                    writes=[cres("M1", cci)])
                init_a = 0.0 if h == 0 else stA[:, cci:cci + 1]
                init_b = 0.0 if h == 0 else stB[:, cci:cci + 1]
                wb = wfm[:, cci:cci + 1].broadcast_to([P, SW])
                vop(lambda e, cci=cci, init_a=init_a, wb=wb:
                    e.tensor_tensor_scan(
                    chunk(Vb, cci, 0, SW), wb,
                    chunk(M1, cci, 0, SW), init_a, OP.mult, OP.add),
                    reads=[res("wfm"), cres("M1", cci), res("stA")],
                    writes=[cres("Vb", cci)])
                gmix_one(cci)
                vop(lambda e, cci=cci, init_b=init_b, wb=wb:
                    e.tensor_tensor_scan(
                    chunk(Lb, cci, 0, SW), wb,
                    chunk(Eb, cci, 0, SW), init_b, OP.mult, OP.add),
                    reads=[res("wfm"), cres("Eb", cci), res("stB")],
                    writes=[cres("Lb", cci)])

            projection("wv", M2, "M2", SW, v_evac)

            def epilogue(cci):
                par = cci % 2
                if lo == 0:
                    vop(lambda e, cci=cci, par=par: e.scalar_tensor_tensor(
                        tf1[par][:, 0:1], M1[:, cci * BW: cci * BW + 1],
                        euf[:, cci:cci + 1], stA[:, cci:cci + 1],
                        OP.mult, OP.add),
                        reads=[cres("M1", cci), res("euf"), res("stA")],
                        writes=[res("tf1_%d" % par)])
                    vop(lambda e, cci=cci, par=par: e.scalar_tensor_tensor(
                        tf2[par][:, 0:1], Eb[:, cci * BW: cci * BW + 1],
                        euf[:, cci:cci + 1], stB[:, cci:cci + 1],
                        OP.mult, OP.add),
                        reads=[cres("Eb", cci), res("euf"), res("stB")],
                        writes=[res("tf2_%d" % par)])
                    o1, n1 = 1, now - 1
                else:
                    o1, n1 = 0, now
                vop(lambda e, cci=cci, par=par, lo=lo, o1=o1, n1=n1:
                    e.scalar_tensor_tensor(
                    tf1[par][:, o1:o1 + n1], chunk(M1, cci, lo + o1, n1),
                    euf[:, cci:cci + 1], chunk(Vb, cci, lo + o1 - 1, n1),
                    OP.mult, OP.add),
                    reads=[cres("M1", cci), cres("Vb", cci), res("euf")],
                    writes=[res("tf1_%d" % par)])
                vop(lambda e, cci=cci, par=par, lo=lo, o1=o1, n1=n1:
                    e.scalar_tensor_tensor(
                    tf2[par][:, o1:o1 + n1], chunk(Eb, cci, lo + o1, n1),
                    euf[:, cci:cci + 1], chunk(Lb, cci, lo + o1 - 1, n1),
                    OP.mult, OP.add),
                    reads=[cres("Eb", cci), cres("Lb", cci), res("euf")],
                    writes=[res("tf2_%d" % par)])
                act(tf3[:, 0:now], tf2[par][:, 0:now], AF.Ln,
                    reads=[res("tf2_%d" % par)], writes=[res("tf3")])
                act(tf2[par][:, 0:now], tf3[:, 0:now], AF.Exp, scale=-1.0,
                    reads=[res("tf3")], writes=[res("tf2_%d" % par)])

            def wkv_mul(cci):
                par = cci % 2
                vop(lambda e, cci=cci, par=par, lo=lo, now=now: e.tensor_mul(
                    chunk(M1, cci, lo, now), tf1[par][:, 0:now],
                    tf2[par][:, 0:now]),
                    reads=[res("tf1_%d" % par), res("tf2_%d" % par)],
                    writes=[cres("M1", cci)])

            if h == 0:
                vop(lambda e: e.tensor_copy(
                    stA[:, :],
                    Vb[:, :].rearrange("p (c w) -> p c w", w=BW)[:, :, SW - 1:SW]),
                    reads=all_res("Vb"), writes=[res("stA")])
                vop(lambda e: e.tensor_copy(
                    stB[:, :],
                    Lb[:, :].rearrange("p (c w) -> p c w", w=BW)[:, :, SW - 1:SW]),
                    reads=all_res("Lb"), writes=[res("stB")])
            pend = []
            t_slab = par = None
            for oc in range(CC):
                if oc % 2 == 0:
                    t_slab, par = load_slab_one(d_w["wg"], key="wg")
                osub = oc % 2
                psel = oc % 3
                fns = []
                for cci in range(CC):
                    for (o, n) in segs(now, 512):
                        fns.append(lambda e, par=par, cci=cci, osub=osub,
                                   o=o, n=n, psel=psel:
                                   e.matmul(pp[psel][:, o:o + n],
                                            SL[par][:, cci * 256 + osub * P:
                                                    cci * 256 + osub * P + P],
                                            chunk(R1, cci, o, n),
                                            start=(cci == 0),
                                            stop=(cci == CC - 1)))
                sc.op("tensor", fns,
                      reads=[res("SL%d" % par)] + all_res("R1"),
                      writes=[res("pp%d" % psel)], waits_ops=(t_slab,))
                epilogue(oc)
                sc.op("scalar", lambda e, oc=oc, psel=psel, now=now:
                      e.copy(chunk(M2, oc, 0, now), pp[psel][:, 0:now]),
                      reads=[res("pp%d" % psel)], writes=[cres("M2", oc)])
                for f in pend:
                    f()
                pend = [lambda cci=oc: wkv_mul(cci)]
            for f in pend:
                f()

            for cci in range(CC):
                tgb, tgr = (tmix, "tmix") if cci % 2 == 0 else (tf3, "tf3")
                act(tgb[:, 0:now], chunk(M2, cci, 0, now), AF.Sigmoid,
                    reads=[cres("M2", cci)], writes=[res(tgr)])
                tpar = cci % 2
                vop(lambda e, cci=cci, tgb=tgb, lo=lo, now=now, tpar=tpar:
                    e.tensor_mul(
                    tf1[tpar][:, 0:now], tgb[:, 0:now],
                    chunk(M1, cci, lo, now)),
                    reads=[res(tgr), cres("M1", cci)],
                    writes=[res("tf1_%d" % tpar)])
                vop(lambda e, cci=cci, tpar=tpar, gcol0=gcol0, now=now:
                    e.tensor_mul(
                    ag[:, cci * TOWN + gcol0: cci * TOWN + gcol0 + now],
                    tf1[tpar][:, 0:now], chunk(R3, cci, 0, now)),
                    reads=[res("tf1_%d" % tpar), cres("R3", cci)],
                    writes=[res("ag")])

        if dbg:
            for nm, buf, wdt in (("dbg_ag", ag, CC * TOWN),
                                 ("dbg_M1", M1, CC * BW),
                                 ("dbg_R3", R3, CC * BW),
                                 ("dbg_Lb", Lb, CC * BW),
                                 ("dbg_R1", R1, CC * BW),
                                 ("dbg_M2", M2, CC * BW)):
                dd = nc.dram_tensor(nm, [P, wdt], BF16, kind="ExternalOutput")
                bn = nm.replace("dbg_", "") if nm != "dbg_ag" else "ag"
                dma("sync", dd.ap(), buf[:, :],
                    reads=([res("ag")] if bn == "ag" else all_res(bn)))
    fence()

    with ExitStack() as pstack:
        def psb(name, shape, dt):
            return pstack.enter_context(nc.sbuf_tensor(name, shape, dt))

        acc = psb("acc", [P, CC * TOWN], F32)
        tp1 = psb("tp1", [P, TOWN], F32)
        tbf = psb("tbf", [P, TOWN], BF16)
        mu_b = psb("mu_b", [P, TOWN], F32)
        rs_b = psb("rs_b", [P, TOWN], F32)
        mu_row = psb("mu_row2", [1, TOWN], F32)
        s2_row = psb("s2_row2", [1, TOWN], F32)
        rs_row = psb("rs_row2", [1, TOWN], F32)
        wstack = ExitStack()
        attn_o = wstack.enter_context(nc.sbuf_tensor("attn_o", [P, CC * TOWN], BF16))
        qpsum = ExitStack()
        po = [qpsum.enter_context(nc.psum_tensor("po0", [P, TOWN], F32)),
              qpsum.enter_context(nc.psum_tensor("po1", [P, TOWN], F32))]
        ps2_mu = qpsum.enter_context(nc.psum_tensor("ps2_mu", [1, TOWN], F32))
        ps2_s2 = qpsum.enter_context(nc.psum_tensor("ps2_s2", [1, TOWN], F32))

        dma("sync", acc[:, :].rearrange("p (c w) -> p c w", w=TOWN),
            d_xres.ap(), writes=[res("acc")])

        def stream_stats(oc, srcbuf, srcres, start):
            vop(lambda e, oc=oc, srcbuf=srcbuf: e.tensor_copy(
                tbf[:, :], srcbuf[:, oc * TOWN:(oc + 1) * TOWN]),
                reads=[res(srcres)], writes=[res("tbf")])
            for stepi in range(2):
                psum, pres = ((ps2_mu, "ps2_mu"), (ps2_s2, "ps2_s2"))[stepi]
                if stepi == 1:
                    act(tbf[:, :], tbf[:, :], AF.Square, writes=[res("tbf")])
                fns = []
                for (o, n) in segs(TOWN, 512):
                    fns.append(lambda e, psum=psum, o=o, n=n:
                               e.matmul(psum[0:1, o:o + n], onesm[:, 0:1],
                                        tbf[:, o:o + n],
                                        start=start, stop=(oc == CC - 1)))
                sc.op("tensor", fns, reads=[res("tbf"), res("onesm")],
                      writes=[] if not start else [res(pres)],
                      waits_ops=(R[pres].writer,) if (start and R[pres].writer
                                                      is not None) else ())
                if not start:
                    R[pres].writer = sc.ops[-1]["idx"]

        def finalize_rows_and_bcast():
            vop(lambda e: e.tensor_copy(mu_row[0:1, 0:TOWN], ps2_mu[0:1, 0:TOWN]),
                reads=[res("ps2_mu")], writes=[res("mu_row")])
            vop(lambda e: e.tensor_copy(s2_row[0:1, 0:TOWN], ps2_s2[0:1, 0:TOWN]),
                reads=[res("ps2_s2")], writes=[res("s2_row")])
            vop(lambda e: e.tensor_mul(rs_row[0:1, 0:TOWN], mu_row[0:1, 0:TOWN],
                                       mu_row[0:1, 0:TOWN]),
                reads=[res("mu_row")], writes=[res("rs_row")])
            vop(lambda e: e.tensor_sub(s2_row[0:1, 0:TOWN], s2_row[0:1, 0:TOWN],
                                       rs_row[0:1, 0:TOWN]),
                reads=[res("rs_row")], writes=[res("s2_row")])
            vop(lambda e: e.tensor_single_scalar(
                s2_row[0:1, 0:TOWN], s2_row[0:1, 0:TOWN], EPS, OP.add),
                writes=[res("s2_row")])
            act(s2_row[0:1, 0:TOWN], s2_row[0:1, 0:TOWN], AF.Ln,
                writes=[res("s2_row")])
            act(rs_row[0:1, 0:TOWN], s2_row[0:1, 0:TOWN], AF.Exp, scale=-0.5,
                reads=[res("s2_row")], writes=[res("rs_row")])
            for row, rres, dstb, dres, psel in (
                    (mu_row, "mu_row", mu_b, "mu_b", 0),
                    (rs_row, "rs_row", rs_b, "rs_b", 1)):
                fns = []
                for (o, n) in segs(TOWN, 512):
                    fns.append(lambda e, row=row, o=o, n=n, psel=psel:
                               e.matmul(po[psel][0:128, o:o + n], onesr[0:1, 0:P],
                                        row[0:1, o:o + n], start=True, stop=True))
                sc.op("tensor", fns, reads=[res(rres), res("onesr")],
                      writes=[res("po%d" % psel)])
                vop(lambda e, dstb=dstb, psel=psel:
                    e.tensor_copy(dstb[:, 0:TOWN], po[psel][:, 0:TOWN]),
                    reads=[res("po%d" % psel)], writes=[res(dres)])

        pend_stats = []
        for s in range(8):
            t_slab, par = load_slab_one(d_w["wo"], key="wo")
            for osub in range(2):
                oc = s * 2 + osub
                psel = oc % 2
                fns = []
                for cci in range(CC):
                    for (o, n) in segs(TOWN, 512):
                        fns.append(lambda e, par=par, cci=cci, osub=osub, o=o,
                                   n=n, psel=psel:
                                   e.matmul(po[psel][:, o:o + n],
                                            SL[par][:, cci * 256 + osub * P:
                                                    cci * 256 + osub * P + P],
                                            ag[:, cci * TOWN + o:
                                               cci * TOWN + o + n],
                                            start=(cci == 0),
                                            stop=(cci == CC - 1)))
                sc.op("tensor", fns, reads=[res("SL%d" % par), res("ag")],
                      writes=[res("po%d" % psel)], waits_ops=(t_slab,))
                sc.op("scalar", lambda e, oc=oc, psel=psel:
                      e.copy(attn_o[:, oc * TOWN:(oc + 1) * TOWN],
                             po[psel][:, 0:TOWN]),
                      reads=[res("po%d" % psel)], writes=[res("attn_o")])
                if len(pend_stats) >= 2:
                    pend_stats.pop(0)()
                pend_stats.append(lambda oc=oc: stream_stats(
                    oc, attn_o, "attn_o", start=(oc == 0)))
        for f in pend_stats:
            f()
        finalize_rows_and_bcast()
        for cci in range(CC):
            sc.op("gpsimd", lambda e, cci=cci: e.tensor_sub(
                tp1[:, :], attn_o[:, cci * TOWN:(cci + 1) * TOWN], mu_b[:, :]),
                reads=[res("attn_o"), res("mu_b")], writes=[res("tp1")])
            vop(lambda e: e.tensor_mul(tp1[:, :], tp1[:, :], rs_b[:, :]),
                reads=[res("rs_b")], writes=[res("tp1")])
            vop(lambda e, cci=cci: e.tensor_scalar(
                tp1[:, :], tp1[:, :], vecs["lng"][:, cci:cci + 1],
                vecs["lnb"][:, cci:cci + 1], OP.mult, OP.add),
                reads=[res("vec_lng"), res("vec_lnb")], writes=[res("tp1")])
            vop(lambda e, cci=cci: e.tensor_add(
                acc[:, cci * TOWN:(cci + 1) * TOWN],
                acc[:, cci * TOWN:(cci + 1) * TOWN], tp1[:, :]),
                reads=[res("tp1")], writes=[res("acc")])
            stream_stats(cci, acc, "acc", start=(cci == 0))

        if dbg:
            dd = nc.dram_tensor("dbg_attn_o", [P, CC * TOWN], BF16,
                                kind="ExternalOutput")
            dma("sync", dd.ap(), attn_o[:, :], reads=[res("attn_o")])
        wstack.close()
        fence()
        ln2 = psb("ln2", [P, CC * TOWN], BF16)
        finalize_rows_and_bcast()
        for so, sn in segs(TOWN, 512):
            for cci in range(CC):
                sc.op("gpsimd", lambda e, cci=cci, so=so, sn=sn: e.tensor_sub(
                    tp1[:, so:so + sn],
                    acc[:, cci * TOWN + so: cci * TOWN + so + sn],
                    mu_b[:, so:so + sn]),
                    reads=[res("acc"), res("mu_b")], writes=[res("tp1")])
                vop(lambda e, cci=cci, so=so, sn=sn: e.tensor_mul(
                    ln2[:, cci * TOWN + so: cci * TOWN + so + sn],
                    tp1[:, so:so + sn], rs_b[:, so:so + sn]),
                    reads=[res("tp1"), res("rs_b")],
                    writes=[res("ln2_s%d" % (so // 512))])
        if dbg:
            dd = nc.dram_tensor("dbg_ln2", [P, CC * TOWN], BF16,
                                kind="ExternalOutput")
            dma("sync", dd.ap(), ln2[:, :], reads=[res("ln2")])
        qpsum.close()

        with ExitStack() as fstack:
            def fsb(name, shape, dt):
                return fstack.enter_context(nc.sbuf_tensor(name, shape, dt))

            def fps(name, shape, dt=F32):
                return fstack.enter_context(nc.psum_tensor(name, shape, dt))

            hs = [ag[:, 0:8 * TOWN], ag[:, 8 * TOWN:16 * TOWN]]
            wdb = [fsb("wdb0", [P, 8 * P], BF16), fsb("wdb1", [P, 8 * P], BF16)]
            th2 = [fsb("th0", [P, TOWN], BF16), fsb("th1", [P, TOWN], BF16)]
            pg = fps("pg", [P, TOWN], F32)
            pu = fps("pu", [P, TOWN], F32)
            pd = [fps("pd0", [P, TOWN], F32), fps("pd1", [P, TOWN], F32)]
            def emit_down(s, hc):
                hpar = s % 2
                wpar = hc % 2
                dma("gpsimd",
                    wdb[wpar][:, :].rearrange("p (i w) -> p i w", w=P),
                    d_wd.ap()[s][hc], writes=[res("wdb%d" % wpar)])
                fns = []
                for ic in range(8):
                    for (o, n) in segs(TOWN, 512):
                        fns.append(lambda e, wpar=wpar, ic=ic, o=o, n=n,
                                   hpar=hpar:
                                   e.matmul(pd[wpar][:, o:o + n],
                                            wdb[wpar][:, ic * P:(ic + 1) * P],
                                            hs[hpar][:, ic * TOWN + o:
                                                     ic * TOWN + o + n],
                                            start=(ic == 0), stop=(ic == 7)))
                sc.op("tensor", fns,
                      reads=[res("wdb%d" % wpar), res("hs%d" % hpar)],
                      writes=[res("pd%d" % wpar)])
                vop(lambda e, hc=hc, wpar=wpar: e.tensor_add(
                    acc[:, hc * TOWN:(hc + 1) * TOWN],
                    acc[:, hc * TOWN:(hc + 1) * TOWN], pd[wpar][:, :]),
                    reads=[res("pd%d" % wpar)], writes=[res("acc")])

            down_pend = []
            for s in range(8):
                hpar = s % 2
                for ic in range(8):
                    gic = 8 * s + ic
                    slab = gic // 2
                    within = gic % 2
                    if within == 0:
                        tg_, parg = load_slab_one(d_wgu_g, key="g")
                        tu_, paru = load_slab_one(d_wgu_u, key="u")
                        sc._ffn_cur = {"g": (tg_, parg), "u": (tu_, paru)}
                    cur = sc._ffn_cur
                    for which in ("g", "u"):
                        t_slab, par = cur[which]
                        psum, pres = (pg, "pg") if which == "g" else (pu, "pu")
                        segsplit = (s == 0 and ic < 4)
                        seg_groups = ([[sg] for sg in segs(TOWN, 512)]
                                      if segsplit else [segs(TOWN, 512)])
                        for grp in seg_groups:
                            fns = []
                            for cci in range(CC):
                                for (o, n) in grp:
                                    fns.append(lambda e, par=par, cci=cci,
                                               within=within, o=o, n=n, psum=psum:
                                               e.matmul(psum[:, o:o + n],
                                                        SL[par][:, cci * 256
                                                                + within * P:
                                                                cci * 256
                                                                + within * P + P],
                                                        ln2[:, cci * TOWN + o:
                                                            cci * TOWN + o + n],
                                                        start=(cci == 0),
                                                        stop=(cci == CC - 1)))
                            rds = [res("SL%d" % par)]
                            rds += [res("ln2_s%d" % (o // 512)) for (o, n) in grp]
                            sc.op("tensor", fns, reads=rds,
                                  writes=[res(pres)], waits_ops=(t_slab,))
                    tpar = ic % 2
                    act(th2[tpar][:, :], pg[:, :], AF.Silu,
                        reads=[res("pg")], writes=[res("th_%d" % tpar)])
                    vop(lambda e, hpar=hpar, ic=ic, tpar=tpar: e.tensor_mul(
                        hs[hpar][:, ic * TOWN:(ic + 1) * TOWN],
                        th2[tpar][:, :], pu[:, :]),
                        reads=[res("th_%d" % tpar), res("pu")],
                        writes=[res("hs%d" % hpar)])
                    for _ in range(2):
                        if down_pend:
                            emit_down(*down_pend.pop(0))
                down_pend.extend((s, hc) for hc in range(CC))
            for s_hc in down_pend:
                emit_down(*s_hc)

        t_out = dma("sync", d_out.ap(),
                    acc[:, :].rearrange("p (c w) -> p c w", w=TOWN),
                    reads=[res("acc")])
        sc.op("sync", lambda e: e.nop(), waits_ops=(t_out,))

    sc.emit(sems)
    nc.finalize()
    stack.close()
    return nc



def _swizzle_w(wT, oslab):
    cin, o = wT.shape
    return np.ascontiguousarray(
        wT.reshape(CC, P, o // oslab, oslab).transpose(2, 1, 0, 3))


def _vec_fm(v):
    return np.ascontiguousarray(v.reshape(CC, P).T)


def _x_fm(xs):
    t = xs.shape[0]
    return np.ascontiguousarray(xs.T.reshape(CC, P, t).transpose(1, 0, 2))


def kernel(L0_attn_x, L0_attn_time_decay, L0_attn_time_first, L0_attn_tm_k,
           L0_attn_tm_v, L0_attn_tm_r, L0_attn_tm_g, L0_attn_Wr, L0_attn_Wk,
           L0_attn_Wv, L0_attn_Wg, L0_attn_Wo, L0_attn_ln_g, L0_attn_ln_b,
           L0_ffn_w_gate_up, L0_ffn_w_down):
    TOWN = 1024
    TH = TOWN + HALO
    SW = TH // 2

    x = np.asarray(L0_attn_x, dtype=np.float32)[0]
    T = x.shape[0]
    assert T == NCORES * TOWN

    wk = _swizzle_w(np.ascontiguousarray(np.asarray(L0_attn_Wk, np.float32).T), 256)
    wv = _swizzle_w(np.ascontiguousarray(np.asarray(L0_attn_Wv, np.float32).T), 256)
    wr = _swizzle_w(np.ascontiguousarray(np.asarray(L0_attn_Wr, np.float32).T), 256)
    wg = _swizzle_w(np.ascontiguousarray(np.asarray(L0_attn_Wg, np.float32).T), 256)
    wo = _swizzle_w(np.ascontiguousarray(np.asarray(L0_attn_Wo, np.float32).T), 256)
    wguT = np.ascontiguousarray(np.asarray(L0_ffn_w_gate_up, np.float32).T)
    wgu_g = _swizzle_w(np.ascontiguousarray(wguT[:, 0:INTER]), 256)
    wgu_u = _swizzle_w(np.ascontiguousarray(wguT[:, INTER:]), 256)
    wdT = np.ascontiguousarray(np.asarray(L0_ffn_w_down, np.float32).T)
    wd = np.ascontiguousarray(
        wdT.reshape(8, 8, P, CC, P).transpose(0, 3, 2, 1, 4))

    vec_common = {
        "tmk": _vec_fm(np.asarray(L0_attn_tm_k, np.float32)),
        "tmv": _vec_fm(np.asarray(L0_attn_tm_v, np.float32)),
        "tmr": _vec_fm(np.asarray(L0_attn_tm_r, np.float32)),
        "tmg": _vec_fm(np.asarray(L0_attn_tm_g, np.float32)),
        "td": _vec_fm(np.asarray(L0_attn_time_decay, np.float32)),
        "tf": _vec_fm(np.asarray(L0_attn_time_first, np.float32)),
        "lng": _vec_fm(np.asarray(L0_attn_ln_g, np.float32)),
        "lnb": _vec_fm(np.asarray(L0_attn_ln_b, np.float32)),
    }
    w_common = {"wk": wk, "wv": wv, "wr": wr, "wg": wg, "wo": wo,
                "wgu_g": wgu_g, "wgu_u": wgu_u, "wd": wd}

    in_maps = []
    for c in range(NCORES):
        base = c * TOWN
        if c == 0:
            halo_rows = np.concatenate([x[0:HALO - 1], x[1:2]], axis=0)
            mask = np.zeros((P, 1), np.float32)
        else:
            halo_rows = x[base - HALO: base]
            mask = np.ones((P, 1), np.float32)
        xa = np.concatenate([halo_rows, x[base: base + TOWN]], axis=0)
        m = {
            "x_h0": _x_fm(xa[0:SW]),
            "x_h1": _x_fm(xa[SW - 1: TH]),
            "x_res": _x_fm(xa[HALO: TH]),
            "mask": mask,
        }
        m.update(vec_common)
        m.update(w_common)
        in_maps.append(m)

    nc = build_program(TOWN)
    r = run_bass_kernel_spmd(nc, in_maps, core_ids=list(range(NCORES)))
    outs = []
    for c in range(NCORES):
        o = np.asarray(r.results[c]["out"])
        outs.append(o.transpose(1, 0, 2).reshape(C, TOWN).T)
    full = np.concatenate(outs, axis=0)[None]
    return full.astype(np.float32)


# version 26
# speedup vs baseline: 1.1228x; 1.0220x over previous
import numpy as np
import ml_dtypes
import concourse.bass as bass
import concourse.mybir as mybir
from concourse.alu_op_type import AluOpType as OP
from concourse.bass_utils import run_bass_kernel_spmd

F32 = mybir.dt.float32
BF16 = mybir.dt.bfloat16
AF = mybir.ActivationFunctionType

P = 128
CC = 16
C = 2048
NCORES = 8
HALO = 128
EPS = 1e-5
INTER = 8192

ENGINES = ("tensor", "vector", "scalar", "gpsimd", "sync")
OWN_SEM = {"tensor": "P", "vector": "V", "scalar": "A", "gpsimd": "G", "sync": None}


def segs(n, cap=512):
    out = []
    o = 0
    while o < n:
        s = min(cap, n - o)
        out.append((o, s))
        o += s
    return out


class Res:

    def __init__(self, name):
        self.name = name
        self.writer = None
        self.readers = []


class Sched:

    def __init__(self, nc):
        self.nc = nc
        self.ops = []
        self.last_eng = {}
        self.barrier_idx = None

    def barrier(self):
        deps = [i for i in self.last_eng.values()]
        idx = self.op("vector", lambda e: e.engine_nop(), waits_ops=deps)
        self.barrier_idx = idx
        return idx

    def op(self, eng, fns, reads=(), writes=(), waits_ops=(), dma=False,
           dma_sem=None):
        if not isinstance(fns, (list, tuple)):
            fns = [fns]
        idx = len(self.ops)
        deps = set(waits_ops)
        rnames = [r.name for r in reads]
        wnames = [r.name for r in writes]
        for r in reads:
            if r.writer is not None:
                deps.add(r.writer)
            r.readers.append(idx)
        for r in writes:
            if r.writer is not None:
                deps.add(r.writer)
            deps.update(r.readers)
            r.writer = idx
            r.readers = []
        if dma:
            sem = dma_sem or ("dG" if eng == "gpsimd" else "dS")
        else:
            sem = OWN_SEM[eng]
        self.ops.append(dict(eng=eng, fns=list(fns), sem=sem, dma=dma,
                             deps=deps, idx=idx, rnames=rnames, wnames=wnames))
        self.last_eng[eng] = idx
        return idx

    def emit(self, sems):
        ops = self.ops
        needed = set()
        for o in ops:
            own = OWN_SEM[o["eng"]]
            for d in o["deps"]:
                dop = ops[d]
                if (not dop["dma"]) and dop["sem"] is not None and dop["sem"] == own:
                    continue
                needed.add(d)
        inc_count = {}
        op_val = {}
        for o in ops:
            s = o["sem"]
            if o["dma"]:
                inc_count[s] = inc_count.get(s, 0) + 16
                op_val[o["idx"]] = (s, inc_count[s])
            elif o["idx"] in needed and s is not None:
                inc_count[s] = inc_count.get(s, 0) + 1
                op_val[o["idx"]] = (s, inc_count[s])
            else:
                op_val[o["idx"]] = None
        per_eng = {e: [] for e in ENGINES}
        hw = {e: {} for e in ENGINES}
        for o in ops:
            own = OWN_SEM[o["eng"]]
            wmap = {}
            for d in sorted(o["deps"]):
                dop = ops[d]
                if (not dop["dma"]) and dop["sem"] is not None and dop["sem"] == own:
                    continue
                v = op_val[d]
                if v is None:
                    raise RuntimeError("dep without inc")
                s, val = v
                if wmap.get(s, 0) < val:
                    wmap[s] = val
            waits = []
            for s, val in sorted(wmap.items()):
                if hw[o["eng"]].get(s, 0) >= val:
                    continue
                hw[o["eng"]][s] = val
                waits.append((s, val))
            inc = op_val[o["idx"]]
            myinc = None
            if o["dma"]:
                myinc = (inc[0], 16)
            elif inc is not None:
                myinc = (inc[0], 1)
            per_eng[o["eng"]].append((waits, o["fns"], myinc, o))
        self._final_counts = dict(inc_count)

        audit = getattr(self, "audit", None)

        def runner(items):
            def go(e):
                for waits, fns, myinc, o in items:
                    insts = []
                    for s, val in waits:
                        e.wait_ge(sems[s], val)
                    last = None
                    for fn in fns:
                        last = fn(e)
                        insts.append(last)
                    if myinc is not None:
                        last.then_inc(sems[myinc[0]], myinc[1])
                    if audit is not None:
                        o["_insts"] = insts
            return go

        with self.nc.Block() as blk:
            for eng in ENGINES:
                getattr(blk, eng)(runner(per_eng[eng]))


def build_program(TOWN=1024, dbg=False):
    TH = TOWN + HALO
    SW = TH // 2
    BW = SW + 1
    assert TH % 2 == 0 and SW > HALO
    OW0 = SW - HALO

    nc = bass.Bass("TRN2", target_bir_lowering=False,
                   detect_race_conditions=False)

    d_xh0 = nc.dram_tensor("x_h0", [P, CC, SW], F32, kind="ExternalInput")
    d_xh1 = nc.dram_tensor("x_h1", [P, CC, SW + 1], F32, kind="ExternalInput")
    d_xres = nc.dram_tensor("x_res", [P, CC, TOWN], F32, kind="ExternalInput")
    d_w = {}
    for nm in ("wk", "wv", "wr", "wg", "wo"):
        d_w[nm] = nc.dram_tensor(nm, [8, P, CC, 256], F32, kind="ExternalInput")
    d_wgu_g = nc.dram_tensor("wgu_g", [32, P, CC, 256], F32, kind="ExternalInput")
    d_wgu_u = nc.dram_tensor("wgu_u", [32, P, CC, 256], F32, kind="ExternalInput")
    d_wd = nc.dram_tensor("wd", [8, CC, P, 8, P], F32, kind="ExternalInput")
    d_vec = {}
    for nm in ("tmk", "tmv", "tmr", "tmg", "td", "tf", "lng", "lnb"):
        d_vec[nm] = nc.dram_tensor(nm, [P, CC], F32, kind="ExternalInput")
    d_mask = nc.dram_tensor("mask", [P, 1], F32, kind="ExternalInput")
    d_out = nc.dram_tensor("out", [P, CC, TOWN], F32, kind="ExternalOutput")

    from contextlib import ExitStack
    stack = ExitStack()
    sems = {}
    for s in ("dG", "dS", "dW", "P", "V", "A", "G"):
        sems[s] = stack.enter_context(nc.semaphore(s))

    sc = Sched(nc)

    def sb(name, shape, dt):
        return stack.enter_context(nc.sbuf_tensor(name, shape, dt))

    vecs = {nm: sb("sb_" + nm, [P, CC], F32) for nm in d_vec}
    maskt = sb("maskt", [P, 1], F32)
    wfm = sb("wfm", [P, CC], F32)
    euf = sb("euf", [P, CC], F32)
    e1f = sb("e1f", [P, CC], F32)
    onesm = sb("onesm", [P, 1], BF16)
    onesr = sb("onesr", [1, P], F32)
    stA = sb("stA", [P, CC], F32)
    stB = sb("stB", [P, CC], F32)
    ag = sb("ag", [P, CC * TOWN], BF16)
    SL = [sb("SL0", [P, CC * 256], BF16), sb("SL1", [P, CC * 256], BF16),
          sb("SL2", [P, CC * 256], BF16)]

    R = {}

    def res(name):
        if name not in R:
            R[name] = Res(name)
            R[name].writer = sc.barrier_idx
        return R[name]

    def fence():
        idx = sc.barrier()
        for r in R.values():
            r.writer = idx
            r.readers = []
        return idx

    def cres(buf, cc):
        return res("%s_%d" % (buf, cc))

    def all_res(buf):
        return [cres(buf, i) for i in range(CC)]

    def dma(eng, dst_ap, src_ap, reads=(), writes=(), waits=(), dma_sem=None):
        return sc.op(eng, lambda e, d=dst_ap, s=src_ap: e.dma_start(d, s),
                     reads=reads, writes=writes, waits_ops=waits, dma=True,
                     dma_sem=dma_sem)

    def act(out_ap, in_ap, func, bias=0.0, scale=1.0, reads=(), writes=(), waits=()):
        return sc.op("scalar",
                     lambda e, o=out_ap, i=in_ap, f=func, b=bias, s=scale:
                     e.activation(o, i, f, bias=b, scale=s),
                     reads=reads, writes=writes, waits_ops=waits)

    def vop(fn, reads=(), writes=(), waits=()):
        return sc.op("vector", fn, reads=reads, writes=writes, waits_ops=waits)

    slab_seq = [0]
    slab_pos = {}

    def load_slab_one(dram, key=None):
        k = key if key is not None else id(dram)
        s = slab_pos.get(k, 0)
        slab_pos[k] = s + 1
        par = slab_seq[0] % 3
        slab_seq[0] += 1
        t = dma("gpsimd",
                SL[par][:, :].rearrange("p (c w) -> p c w", w=256),
                dram.ap()[s],
                writes=[res("SL%d" % par)])
        return t, par

    for nm in d_vec:
        dma("sync", vecs[nm][:, :], d_vec[nm].ap(), writes=[res("vec_" + nm)])
    dma("sync", maskt[:, :], d_mask.ap(), writes=[res("maskt")])
    vop(lambda e: e.memset(onesm[:, :], 1.0 / 2048.0), writes=[res("onesm")])
    vop(lambda e: e.memset(onesr[:, :], 1.0), writes=[res("onesr")])
    act(e1f[:, :], vecs["td"][:, :], AF.Exp,
        reads=[res("vec_td")], writes=[res("e1f")])
    act(euf[:, :], vecs["tf"][:, :], AF.Exp,
        reads=[res("vec_tf")], writes=[res("euf")])
    vop(lambda e: e.memset(wfm[:, 0:1], 1.0), writes=[res("wfm")])
    for fn in (AF.Square, AF.Sigmoid, AF.Silu, AF.Ln, AF.Exp):
        act(wfm[:, 0:1], wfm[:, 0:1], fn, writes=[res("wfm")])
    act(wfm[:, :], e1f[:, :], AF.Exp, scale=-1.0,
        reads=[res("e1f")], writes=[res("wfm")])
    fence()

    with ExitStack() as hstack:
        def hsb(name, shape, dt):
            return hstack.enter_context(nc.sbuf_tensor(name, shape, dt))

        def hps(name, shape, dt=F32):
            return hstack.enter_context(nc.psum_tensor(name, shape, dt))

        Lb = hsb("Lb", [P, CC * BW], BF16)
        Eb = hsb("Eb", [P, CC * BW], BF16)
        Vb = hsb("Vb", [P, CC * BW], BF16)
        M1 = hsb("M1", [P, CC * BW], BF16)
        M2 = hsb("M2", [P, CC * BW], BF16)
        R1 = hsb("R1", [P, CC * BW], BF16)
        R3 = hsb("R3", [P, CC * BW], BF16)
        tmix = hsb("tmix", [P, BW], BF16)
        tf1 = [hsb("tf1a", [P, BW], F32), hsb("tf1b", [P, BW], F32)]
        tf2 = [hsb("tf2a", [P, BW], F32), hsb("tf2b", [P, BW], F32)]
        tf3 = hsb("tf3a", [P, BW], F32)
        mub_h = hsb("mub_h", [P, BW], F32)
        rsb_h = hsb("rsb_h", [P, BW], F32)
        hmu_row = hsb("hmu_row", [1, BW], F32)
        hs2_row = hsb("hs2_row", [1, BW], F32)
        hrs_row = hsb("hrs_row", [1, BW], F32)
        pp = [hps("pp0", [P, BW], F32), hps("pp1", [P, BW], F32)]
        ps_mu = hps("ps_mu", [1, BW], F32)
        ps_s2 = hps("ps_s2", [1, BW], F32)

        def chunk(buf, cc, lo, n, stride=BW):
            return buf[:, cc * stride + lo: cc * stride + lo + n]

        def halo_view(buf):
            return buf[:, :].rearrange("p (c w) -> p c w", w=BW)[:, :, 0:HALO]

        def col_view(buf, col):
            return buf[:, :].rearrange("p (c w) -> p c w", w=BW)[:, :, col:col + 1]

        if dbg:
            for buf, bres in ((Lb, "Lb"), (Eb, "Eb"), (Vb, "Vb"),
                              (M1, "M1"), (M2, "M2"), (R1, "R1"), (R3, "R3")):
                vop(lambda e, buf=buf: e.memset(buf[:, :], 0.0),
                    writes=[res(bres)] + all_res(bres))

        def ln_stats_apply(ln_w, xbf_res, sq_res, xbuf, sqbuf, outbuf,
                           t_seg0=None):
            s0w = min(512, ln_w)
            xv0 = xbuf[:, :].rearrange("p (c w) -> p c w", w=BW)[:, :, 0:s0w]
            sv0 = sqbuf[:, :].rearrange("p (c w) -> p c w", w=BW)[:, :, 0:s0w]
            t_sq0 = act(sv0, xv0, AF.Square, writes=all_res(sq_res),
                        waits=(t_seg0,) if t_seg0 is not None else ())
            if ln_w > s0w:
                xv1 = xbuf[:, :].rearrange("p (c w) -> p c w",
                                           w=BW)[:, :, s0w:ln_w]
                sv1 = sqbuf[:, :].rearrange("p (c w) -> p c w",
                                            w=BW)[:, :, s0w:ln_w]
                act(sv1, xv1, AF.Square, reads=all_res(xbf_res),
                    writes=all_res(sq_res))
            for stat, (srcb, sres, psum, pres) in enumerate(
                    ((xbuf, xbf_res, ps_mu, "ps_mu"),
                     (sqbuf, sq_res, ps_s2, "ps_s2"))):
                for (o, n) in segs(ln_w, 512):
                    fns = []
                    for cci in range(CC):
                        fns.append(lambda e, srcb=srcb, psum=psum, o=o, n=n, cci=cci:
                                   e.matmul(psum[0:1, o:o + n], onesm[:, 0:1],
                                            chunk(srcb, cci, o, n),
                                            start=(cci == 0), stop=(cci == CC - 1)))
                    if o == 0 and stat == 0 and t_seg0 is not None:
                        rds, wts = [res("onesm")], (t_seg0,)
                    elif o == 0 and stat == 1:
                        rds, wts = [res("onesm")], (t_sq0,)
                    else:
                        rds, wts = all_res(sres) + [res("onesm")], ()
                    sc.op("tensor", fns, reads=rds, writes=[res(pres)],
                          waits_ops=wts)
            vop(lambda e: e.tensor_copy(hmu_row[0:1, 0:ln_w], ps_mu[0:1, 0:ln_w]),
                reads=[res("ps_mu")], writes=[res("hmu_row")])
            vop(lambda e: e.tensor_copy(hs2_row[0:1, 0:ln_w], ps_s2[0:1, 0:ln_w]),
                reads=[res("ps_s2")], writes=[res("hs2_row")])
            vop(lambda e: e.tensor_mul(hrs_row[0:1, 0:ln_w], hmu_row[0:1, 0:ln_w],
                                       hmu_row[0:1, 0:ln_w]),
                reads=[res("hmu_row")], writes=[res("hrs_row")])
            vop(lambda e: e.tensor_sub(hs2_row[0:1, 0:ln_w], hs2_row[0:1, 0:ln_w],
                                       hrs_row[0:1, 0:ln_w]),
                reads=[res("hrs_row")], writes=[res("hs2_row")])
            vop(lambda e: e.tensor_single_scalar(
                hs2_row[0:1, 0:ln_w], hs2_row[0:1, 0:ln_w], EPS, OP.add),
                writes=[res("hs2_row")])
            act(hs2_row[0:1, 0:ln_w], hs2_row[0:1, 0:ln_w], AF.Ln,
                writes=[res("hs2_row")])
            act(hrs_row[0:1, 0:ln_w], hs2_row[0:1, 0:ln_w], AF.Exp, scale=-0.5,
                reads=[res("hs2_row")], writes=[res("hrs_row")])
            for row, rres, dstb, dres, psel in (
                    (hmu_row, "hmu_row", mub_h, "mub_h", 0),
                    (hrs_row, "hrs_row", rsb_h, "rsb_h", 1)):
                fns = []
                for (o, n) in segs(ln_w, 512):
                    fns.append(lambda e, row=row, o=o, n=n, psel=psel:
                               e.matmul(pp[psel][0:128, o:o + n], onesr[0:1, 0:P],
                                        row[0:1, o:o + n], start=True, stop=True))
                sc.op("tensor", fns, reads=[res(rres), res("onesr")],
                      writes=[res("pp%d" % psel)])
                vop(lambda e, dstb=dstb, psel=psel, ln_w=ln_w:
                    e.tensor_copy(dstb[:, 0:ln_w], pp[psel][:, 0:ln_w]),
                    reads=[res("pp%d" % psel)], writes=[res(dres)])
            for cci in range(CC):
                vop(lambda e, cci=cci, ln_w=ln_w: e.tensor_sub(
                    tf1[0][:, 0:ln_w], chunk(xbuf, cci, 0, ln_w),
                    mub_h[:, 0:ln_w]),
                    reads=[cres(xbf_res, cci), res("mub_h")],
                    writes=[res("tf1_0")])
                vop(lambda e, cci=cci, ln_w=ln_w: e.tensor_mul(
                    chunk(outbuf, cci, 0, ln_w), tf1[0][:, 0:ln_w],
                    rsb_h[:, 0:ln_w]),
                    reads=[res("tf1_0"), res("rsb_h")], writes=[res("Lb")])

        def mix_into(dstbuf, dres, tmname, lnoff, lo, n, h):
            for cci in range(CC):
                if h == 0 and lo == 0:
                    vop(lambda e: e.memset(tmix[:, 0:1], 0.0),
                        writes=[res("tmix")])
                    vop(lambda e, cci=cci, n=n: e.tensor_sub(
                        tmix[:, 1:n], chunk(Lb, cci, 0, n - 1),
                        chunk(Lb, cci, 1, n - 1)),
                        reads=[res("Lb")], writes=[res("tmix")])
                else:
                    off = lnoff + lo
                    vop(lambda e, cci=cci, off=off, n=n: e.tensor_sub(
                        tmix[:, 0:n], chunk(Lb, cci, off - 1, n),
                        chunk(Lb, cci, off, n)),
                        reads=[res("Lb")], writes=[res("tmix")])
                vop(lambda e, cci=cci, tmname=tmname, dstbuf=dstbuf,
                    off=lnoff + lo, n=n:
                    e.scalar_tensor_tensor(
                        chunk(dstbuf, cci, 0, n), tmix[:, 0:n],
                        vecs[tmname][:, cci:cci + 1],
                        chunk(Lb, cci, off, n), OP.mult, OP.add),
                    reads=[res("tmix"), res("vec_" + tmname), res("Lb")],
                    writes=[cres(dres, cci)])

        def projection(wname, mixbuf, mres, width, evac_fn):
            for s in range(8):
                t_slab, par = load_slab_one(d_w[wname], key=wname)
                for osub in range(2):
                    oc = s * 2 + osub
                    psel = oc % 2
                    fns = []
                    for cci in range(CC):
                        for (o, n) in segs(width, 512):
                            fns.append(lambda e, par=par, cci=cci, osub=osub,
                                       o=o, n=n, psel=psel, mixbuf=mixbuf:
                                       e.matmul(pp[psel][:, o:o + n],
                                                SL[par][:, cci * 256 + osub * P:
                                                        cci * 256 + osub * P + P],
                                                chunk(mixbuf, cci, o, n),
                                                start=(cci == 0),
                                                stop=(cci == CC - 1)))
                    sc.op("tensor", fns,
                          reads=[res("SL%d" % par)] + all_res(mres),
                          writes=[res("pp%d" % psel)], waits_ops=(t_slab,))
                    evac_fn(oc, psel)

        for h in (0, 1):
            for wnm in ("wk", "wv", "wr", "wg"):
                slab_pos.pop(wnm, None)
            ln_w = SW if h == 0 else SW + 1
            lnoff = 0 if h == 0 else 1
            lo = HALO if h == 0 else 0
            now = SW - lo
            gcol0 = 0 if h == 0 else OW0
            d_x = d_xh0 if h == 0 else d_xh1

            s0w = min(512, ln_w)
            t_x1 = dma("gpsimd",
                       Eb[:, :].rearrange("p (c w) -> p c w",
                                          w=BW)[:, :, 0:s0w],
                       d_x.ap()[:, :, 0:s0w], writes=all_res("Eb"))
            if ln_w > s0w:
                dma("gpsimd",
                    Eb[:, :].rearrange("p (c w) -> p c w",
                                       w=BW)[:, :, s0w:ln_w],
                    d_x.ap()[:, :, s0w:ln_w], writes=all_res("Eb"))
            ln_stats_apply(ln_w, "Eb", "Vb", Eb, Vb, Lb, t_x1)

            mix_into(M1, "M1", "tmk", lnoff, 0, SW, h)
            mix_into(M2, "M2", "tmv", lnoff, 0, SW, h)
            mix_into(R1, "R1", "tmr", lnoff, lo, now, h if lo == 0 else 1)

            projection("wk", M1, "M1", SW,
                       lambda oc, psel: act(chunk(Eb, oc, 0, SW),
                                            pp[psel][:, 0:SW], AF.Exp,
                                            reads=[res("pp%d" % psel)],
                                            writes=[cres("Eb", oc)]))
            projection("wv", M2, "M2", SW,
                       lambda oc, psel: sc.op(
                           "scalar",
                           lambda e, oc=oc, psel=psel:
                           e.copy(chunk(Vb, oc, 0, SW), pp[psel][:, 0:SW]),
                           reads=[res("pp%d" % psel)], writes=[cres("Vb", oc)]))

            projection("wr", R1, "R1", now,
                       lambda oc, psel: act(chunk(R3, oc, 0, now),
                                            pp[psel][:, 0:now], AF.Sigmoid,
                                            reads=[res("pp%d" % psel)],
                                            writes=[cres("R3", oc)]))

            if h == 0:
                for cci in range(CC):
                    vop(lambda e, cci=cci: e.tensor_single_scalar(
                        Eb[:, cci * BW: cci * BW + HALO],
                        Eb[:, cci * BW: cci * BW + HALO],
                        maskt[:, 0:1], OP.mult),
                        reads=[res("maskt")], writes=[cres("Eb", cci)])

            pend_qa = []
            for cci in range(CC):
                vop(lambda e, cci=cci: e.tensor_mul(
                    chunk(M1, cci, 0, SW), chunk(Eb, cci, 0, SW),
                    chunk(Vb, cci, 0, SW)),
                    reads=[cres("Eb", cci), cres("Vb", cci)],
                    writes=[cres("M1", cci)])
                init_a = 0.0 if h == 0 else stA[:, cci:cci + 1]
                wb = wfm[:, cci:cci + 1].broadcast_to([P, SW])
                pend_qa.append(lambda e=None, cci=cci, init_a=init_a, wb=wb:
                               vop(lambda e, cci=cci, init_a=init_a, wb=wb:
                                   e.tensor_tensor_scan(
                                   chunk(Vb, cci, 0, SW), wb,
                                   chunk(M1, cci, 0, SW), init_a,
                                   OP.mult, OP.add),
                                   reads=[res("wfm"), cres("M1", cci),
                                          res("stA")],
                                   writes=[cres("Vb", cci)]))
                if len(pend_qa) >= 2:
                    pend_qa.pop(0)()

            def scan_pair(cci):
                init_b = 0.0 if h == 0 else stB[:, cci:cci + 1]
                wb = wfm[:, cci:cci + 1].broadcast_to([P, SW])
                vop(lambda e, cci=cci, init_b=init_b, wb=wb:
                    e.tensor_tensor_scan(
                    chunk(M2, cci, 0, SW), wb,
                    chunk(Eb, cci, 0, SW), init_b, OP.mult, OP.add),
                    reads=[res("wfm"), cres("Eb", cci), res("stB")],
                    writes=[cres("M2", cci)])

            def epilogue(cci):
                par = cci % 2
                if lo == 0:
                    vop(lambda e, cci=cci, par=par: e.scalar_tensor_tensor(
                        tf1[par][:, 0:1], M1[:, cci * BW: cci * BW + 1],
                        euf[:, cci:cci + 1], stA[:, cci:cci + 1],
                        OP.mult, OP.add),
                        reads=[cres("M1", cci), res("euf"), res("stA")],
                        writes=[res("tf1_%d" % par)])
                    vop(lambda e, cci=cci, par=par: e.scalar_tensor_tensor(
                        tf2[par][:, 0:1], Eb[:, cci * BW: cci * BW + 1],
                        euf[:, cci:cci + 1], stB[:, cci:cci + 1],
                        OP.mult, OP.add),
                        reads=[cres("Eb", cci), res("euf"), res("stB")],
                        writes=[res("tf2_%d" % par)])
                    o1, n1 = 1, now - 1
                else:
                    o1, n1 = 0, now
                vop(lambda e, cci=cci, par=par, lo=lo, o1=o1, n1=n1:
                    e.scalar_tensor_tensor(
                    tf1[par][:, o1:o1 + n1], chunk(M1, cci, lo + o1, n1),
                    euf[:, cci:cci + 1], chunk(Vb, cci, lo + o1 - 1, n1),
                    OP.mult, OP.add),
                    reads=[cres("M1", cci), cres("Vb", cci), res("euf")],
                    writes=[res("tf1_%d" % par)])
                vop(lambda e, cci=cci, par=par, lo=lo, o1=o1, n1=n1:
                    e.scalar_tensor_tensor(
                    tf2[par][:, o1:o1 + n1], chunk(Eb, cci, lo + o1, n1),
                    euf[:, cci:cci + 1], chunk(M2, cci, lo + o1 - 1, n1),
                    OP.mult, OP.add),
                    reads=[cres("Eb", cci), cres("M2", cci), res("euf")],
                    writes=[res("tf2_%d" % par)])
                act(tf3[:, 0:now], tf2[par][:, 0:now], AF.Ln,
                    reads=[res("tf2_%d" % par)], writes=[res("tf3")])
                act(tf2[par][:, 0:now], tf3[:, 0:now], AF.Exp, scale=-1.0,
                    reads=[res("tf3")], writes=[res("tf2_%d" % par)])

            def wkv_mul(cci):
                par = cci % 2
                vop(lambda e, cci=cci, par=par, lo=lo, now=now: e.tensor_mul(
                    chunk(M1, cci, lo, now), tf1[par][:, 0:now],
                    tf2[par][:, 0:now]),
                    reads=[res("tf1_%d" % par), res("tf2_%d" % par)],
                    writes=[cres("M1", cci)])

            for f in pend_qa:
                f()
            pend_qa = []
            for cci in range(CC):
                scan_pair(cci)
            if h == 0:
                vop(lambda e: e.tensor_copy(
                    stA[:, :],
                    Vb[:, :].rearrange("p (c w) -> p c w", w=BW)[:, :, SW - 1:SW]),
                    reads=all_res("Vb"), writes=[res("stA")])
                vop(lambda e: e.tensor_copy(
                    stB[:, :],
                    M2[:, :].rearrange("p (c w) -> p c w", w=BW)[:, :, SW - 1:SW]),
                    reads=all_res("M2"), writes=[res("stB")])
            pend = []
            for cci in range(CC):
                epilogue(cci)
                for f in pend:
                    f()
                pend = [lambda cci=cci: wkv_mul(cci)]
            for f in pend:
                f()
            mix_into(R1, "R1", "tmg", lnoff, lo, now, h if lo == 0 else 1)

            def g_evac(oc, psel, lo=lo, now=now, gcol0=gcol0):
                tpar = 0
                act(tmix[:, 0:now], pp[psel][:, 0:now], AF.Sigmoid,
                    reads=[res("pp%d" % psel)], writes=[res("tmix")])
                vop(lambda e, oc=oc, tpar=tpar, lo=lo, now=now: e.tensor_mul(
                    tf1[tpar][:, 0:now], tmix[:, 0:now],
                    chunk(M1, oc, lo, now)),
                    reads=[res("tmix"), cres("M1", oc)],
                    writes=[res("tf1_%d" % tpar)])
                vop(lambda e, oc=oc, tpar=tpar, gcol0=gcol0, now=now:
                    e.tensor_mul(
                    ag[:, oc * TOWN + gcol0: oc * TOWN + gcol0 + now],
                    tf1[tpar][:, 0:now], chunk(R3, oc, 0, now)),
                    reads=[res("tf1_%d" % tpar), cres("R3", oc)],
                    writes=[res("ag")])
            projection("wg", R1, "R1", now, g_evac)

        if dbg:
            for nm, buf, wdt in (("dbg_ag", ag, CC * TOWN),
                                 ("dbg_M1", M1, CC * BW),
                                 ("dbg_R3", R3, CC * BW),
                                 ("dbg_Lb", Lb, CC * BW),
                                 ("dbg_R1", R1, CC * BW),
                                 ("dbg_M2", M2, CC * BW)):
                dd = nc.dram_tensor(nm, [P, wdt], BF16, kind="ExternalOutput")
                bn = nm.replace("dbg_", "") if nm != "dbg_ag" else "ag"
                dma("sync", dd.ap(), buf[:, :],
                    reads=([res("ag")] if bn == "ag" else all_res(bn)))
    fence()

    with ExitStack() as pstack:
        def psb(name, shape, dt):
            return pstack.enter_context(nc.sbuf_tensor(name, shape, dt))

        acc = psb("acc", [P, CC * TOWN], F32)
        tp1 = psb("tp1", [P, TOWN], F32)
        tbf = psb("tbf", [P, TOWN], BF16)
        mu_b = psb("mu_b", [P, TOWN], F32)
        rs_b = psb("rs_b", [P, TOWN], F32)
        mu_row = psb("mu_row2", [1, TOWN], F32)
        s2_row = psb("s2_row2", [1, TOWN], F32)
        rs_row = psb("rs_row2", [1, TOWN], F32)
        wstack = ExitStack()
        attn_o = wstack.enter_context(nc.sbuf_tensor("attn_o", [P, CC * TOWN], BF16))
        qpsum = ExitStack()
        po = [qpsum.enter_context(nc.psum_tensor("po0", [P, TOWN], F32)),
              qpsum.enter_context(nc.psum_tensor("po1", [P, TOWN], F32))]
        ps2_mu = qpsum.enter_context(nc.psum_tensor("ps2_mu", [1, TOWN], F32))
        ps2_s2 = qpsum.enter_context(nc.psum_tensor("ps2_s2", [1, TOWN], F32))

        dma("sync", acc[:, :].rearrange("p (c w) -> p c w", w=TOWN),
            d_xres.ap(), writes=[res("acc")])

        def stream_stats(oc, srcbuf, srcres, start):
            vop(lambda e, oc=oc, srcbuf=srcbuf: e.tensor_copy(
                tbf[:, :], srcbuf[:, oc * TOWN:(oc + 1) * TOWN]),
                reads=[res(srcres)], writes=[res("tbf")])
            for stepi in range(2):
                psum, pres = ((ps2_mu, "ps2_mu"), (ps2_s2, "ps2_s2"))[stepi]
                if stepi == 1:
                    act(tbf[:, :], tbf[:, :], AF.Square, writes=[res("tbf")])
                fns = []
                for (o, n) in segs(TOWN, 512):
                    fns.append(lambda e, psum=psum, o=o, n=n:
                               e.matmul(psum[0:1, o:o + n], onesm[:, 0:1],
                                        tbf[:, o:o + n],
                                        start=start, stop=(oc == CC - 1)))
                sc.op("tensor", fns, reads=[res("tbf"), res("onesm")],
                      writes=[] if not start else [res(pres)],
                      waits_ops=(R[pres].writer,) if (start and R[pres].writer
                                                      is not None) else ())
                if not start:
                    R[pres].writer = sc.ops[-1]["idx"]

        def finalize_rows_and_bcast():
            vop(lambda e: e.tensor_copy(mu_row[0:1, 0:TOWN], ps2_mu[0:1, 0:TOWN]),
                reads=[res("ps2_mu")], writes=[res("mu_row")])
            vop(lambda e: e.tensor_copy(s2_row[0:1, 0:TOWN], ps2_s2[0:1, 0:TOWN]),
                reads=[res("ps2_s2")], writes=[res("s2_row")])
            vop(lambda e: e.tensor_mul(rs_row[0:1, 0:TOWN], mu_row[0:1, 0:TOWN],
                                       mu_row[0:1, 0:TOWN]),
                reads=[res("mu_row")], writes=[res("rs_row")])
            vop(lambda e: e.tensor_sub(s2_row[0:1, 0:TOWN], s2_row[0:1, 0:TOWN],
                                       rs_row[0:1, 0:TOWN]),
                reads=[res("rs_row")], writes=[res("s2_row")])
            vop(lambda e: e.tensor_single_scalar(
                s2_row[0:1, 0:TOWN], s2_row[0:1, 0:TOWN], EPS, OP.add),
                writes=[res("s2_row")])
            act(s2_row[0:1, 0:TOWN], s2_row[0:1, 0:TOWN], AF.Ln,
                writes=[res("s2_row")])
            act(rs_row[0:1, 0:TOWN], s2_row[0:1, 0:TOWN], AF.Exp, scale=-0.5,
                reads=[res("s2_row")], writes=[res("rs_row")])
            for row, rres, dstb, dres, psel in (
                    (mu_row, "mu_row", mu_b, "mu_b", 0),
                    (rs_row, "rs_row", rs_b, "rs_b", 1)):
                fns = []
                for (o, n) in segs(TOWN, 512):
                    fns.append(lambda e, row=row, o=o, n=n, psel=psel:
                               e.matmul(po[psel][0:128, o:o + n], onesr[0:1, 0:P],
                                        row[0:1, o:o + n], start=True, stop=True))
                sc.op("tensor", fns, reads=[res(rres), res("onesr")],
                      writes=[res("po%d" % psel)])
                vop(lambda e, dstb=dstb, psel=psel:
                    e.tensor_copy(dstb[:, 0:TOWN], po[psel][:, 0:TOWN]),
                    reads=[res("po%d" % psel)], writes=[res(dres)])

        pend_stats = []
        for s in range(8):
            t_slab, par = load_slab_one(d_w["wo"], key="wo")
            for osub in range(2):
                oc = s * 2 + osub
                psel = oc % 2
                fns = []
                for cci in range(CC):
                    for (o, n) in segs(TOWN, 512):
                        fns.append(lambda e, par=par, cci=cci, osub=osub, o=o,
                                   n=n, psel=psel:
                                   e.matmul(po[psel][:, o:o + n],
                                            SL[par][:, cci * 256 + osub * P:
                                                    cci * 256 + osub * P + P],
                                            ag[:, cci * TOWN + o:
                                               cci * TOWN + o + n],
                                            start=(cci == 0),
                                            stop=(cci == CC - 1)))
                sc.op("tensor", fns, reads=[res("SL%d" % par), res("ag")],
                      writes=[res("po%d" % psel)], waits_ops=(t_slab,))
                sc.op("scalar", lambda e, oc=oc, psel=psel:
                      e.copy(attn_o[:, oc * TOWN:(oc + 1) * TOWN],
                             po[psel][:, 0:TOWN]),
                      reads=[res("po%d" % psel)], writes=[res("attn_o")])
                if len(pend_stats) >= 2:
                    pend_stats.pop(0)()
                pend_stats.append(lambda oc=oc: stream_stats(
                    oc, attn_o, "attn_o", start=(oc == 0)))
        for f in pend_stats:
            f()
        finalize_rows_and_bcast()
        for cci in range(CC):
            sc.op("gpsimd", lambda e, cci=cci: e.tensor_sub(
                tp1[:, :], attn_o[:, cci * TOWN:(cci + 1) * TOWN], mu_b[:, :]),
                reads=[res("attn_o"), res("mu_b")], writes=[res("tp1")])
            vop(lambda e: e.tensor_mul(tp1[:, :], tp1[:, :], rs_b[:, :]),
                reads=[res("rs_b")], writes=[res("tp1")])
            vop(lambda e, cci=cci: e.tensor_scalar(
                tp1[:, :], tp1[:, :], vecs["lng"][:, cci:cci + 1],
                vecs["lnb"][:, cci:cci + 1], OP.mult, OP.add),
                reads=[res("vec_lng"), res("vec_lnb")], writes=[res("tp1")])
            vop(lambda e, cci=cci: e.tensor_add(
                acc[:, cci * TOWN:(cci + 1) * TOWN],
                acc[:, cci * TOWN:(cci + 1) * TOWN], tp1[:, :]),
                reads=[res("tp1")], writes=[res("acc")])
            stream_stats(cci, acc, "acc", start=(cci == 0))

        if dbg:
            dd = nc.dram_tensor("dbg_attn_o", [P, CC * TOWN], BF16,
                                kind="ExternalOutput")
            dma("sync", dd.ap(), attn_o[:, :], reads=[res("attn_o")])
        wstack.close()
        fence()
        ln2 = psb("ln2", [P, CC * TOWN], BF16)
        finalize_rows_and_bcast()
        for cci in range(CC):
            sc.op("gpsimd", lambda e, cci=cci: e.tensor_sub(
                tp1[:, :], acc[:, cci * TOWN:(cci + 1) * TOWN], mu_b[:, :]),
                reads=[res("acc"), res("mu_b")], writes=[res("tp1")])
            vop(lambda e, cci=cci: e.tensor_mul(
                ln2[:, cci * TOWN:(cci + 1) * TOWN], tp1[:, :], rs_b[:, :]),
                reads=[res("tp1"), res("rs_b")], writes=[cres("ln2", cci)])
        if dbg:
            dd = nc.dram_tensor("dbg_ln2", [P, CC * TOWN], BF16,
                                kind="ExternalOutput")
            dma("sync", dd.ap(), ln2[:, :], reads=[res("ln2")])
        qpsum.close()

        with ExitStack() as fstack:
            def fsb(name, shape, dt):
                return fstack.enter_context(nc.sbuf_tensor(name, shape, dt))

            def fps(name, shape, dt=F32):
                return fstack.enter_context(nc.psum_tensor(name, shape, dt))

            hs = [ag[:, 0:8 * TOWN], ag[:, 8 * TOWN:16 * TOWN]]
            wdb = [fsb("wdb0", [P, 8 * P], BF16), fsb("wdb1", [P, 8 * P], BF16)]
            th2 = [fsb("th0", [P, TOWN], BF16), fsb("th1", [P, TOWN], BF16)]
            pg = fps("pg", [P, TOWN], F32)
            pu = fps("pu", [P, TOWN], F32)
            pd = [fps("pd0", [P, TOWN], F32), fps("pd1", [P, TOWN], F32)]
            def emit_down(s, hc):
                hpar = s % 2
                wpar = hc % 2
                dma("gpsimd",
                    wdb[wpar][:, :].rearrange("p (i w) -> p i w", w=P),
                    d_wd.ap()[s][hc], writes=[res("wdb%d" % wpar)])
                fns = []
                for ic in range(8):
                    for (o, n) in segs(TOWN, 512):
                        fns.append(lambda e, wpar=wpar, ic=ic, o=o, n=n,
                                   hpar=hpar:
                                   e.matmul(pd[wpar][:, o:o + n],
                                            wdb[wpar][:, ic * P:(ic + 1) * P],
                                            hs[hpar][:, ic * TOWN + o:
                                                     ic * TOWN + o + n],
                                            start=(ic == 0), stop=(ic == 7)))
                sc.op("tensor", fns,
                      reads=[res("wdb%d" % wpar), res("hs%d" % hpar)],
                      writes=[res("pd%d" % wpar)])
                vop(lambda e, hc=hc, wpar=wpar: e.tensor_add(
                    acc[:, hc * TOWN:(hc + 1) * TOWN],
                    acc[:, hc * TOWN:(hc + 1) * TOWN], pd[wpar][:, :]),
                    reads=[res("pd%d" % wpar)], writes=[res("acc")])

            down_pend = []
            for s in range(8):
                hpar = s % 2
                for ic in range(8):
                    gic = 8 * s + ic
                    slab = gic // 2
                    within = gic % 2
                    if within == 0:
                        tg_, parg = load_slab_one(d_wgu_g, key="g")
                        tu_, paru = load_slab_one(d_wgu_u, key="u")
                        sc._ffn_cur = {"g": (tg_, parg), "u": (tu_, paru)}
                    cur = sc._ffn_cur
                    for which in ("g", "u"):
                        t_slab, par = cur[which]
                        psum, pres = (pg, "pg") if which == "g" else (pu, "pu")

                        def mk_fns(cclist, par=par, within=within, psum=psum):
                            fns = []
                            for cci in cclist:
                                for (o, n) in segs(TOWN, 512):
                                    fns.append(
                                        lambda e, par=par, cci=cci,
                                        within=within, o=o, n=n, psum=psum:
                                        e.matmul(psum[:, o:o + n],
                                                 SL[par][:, cci * 256
                                                         + within * P:
                                                         cci * 256
                                                         + within * P + P],
                                                 ln2[:, cci * TOWN + o:
                                                     cci * TOWN + o + n],
                                                 start=(cci == 0),
                                                 stop=(cci == CC - 1)))
                            return fns
                        if gic == 0:
                            for cci in range(CC):
                                sc.op("tensor", mk_fns([cci]),
                                      reads=[res("SL%d" % par),
                                             cres("ln2", cci)],
                                      writes=[res(pres)],
                                      waits_ops=(t_slab,) if cci == 0 else ())
                        else:
                            sc.op("tensor", mk_fns(range(CC)),
                                  reads=[res("SL%d" % par)] + all_res("ln2"),
                                  writes=[res(pres)], waits_ops=(t_slab,))
                    tpar = ic % 2
                    act(th2[tpar][:, :], pg[:, :], AF.Silu,
                        reads=[res("pg")], writes=[res("th_%d" % tpar)])
                    vop(lambda e, hpar=hpar, ic=ic, tpar=tpar: e.tensor_mul(
                        hs[hpar][:, ic * TOWN:(ic + 1) * TOWN],
                        th2[tpar][:, :], pu[:, :]),
                        reads=[res("th_%d" % tpar), res("pu")],
                        writes=[res("hs%d" % hpar)])
                    for _ in range(2):
                        if down_pend:
                            emit_down(*down_pend.pop(0))
                down_pend.extend((s, hc) for hc in range(CC))
            for s_hc in down_pend:
                emit_down(*s_hc)

        t_out = dma("sync", d_out.ap(),
                    acc[:, :].rearrange("p (c w) -> p c w", w=TOWN),
                    reads=[res("acc")])
        sc.op("sync", lambda e: e.nop(), waits_ops=(t_out,))

    sc.emit(sems)
    nc.finalize()
    stack.close()
    return nc



def _swizzle_w(wT, oslab):
    cin, o = wT.shape
    return np.ascontiguousarray(
        wT.reshape(CC, P, o // oslab, oslab).transpose(2, 1, 0, 3))


def _vec_fm(v):
    return np.ascontiguousarray(v.reshape(CC, P).T)


def _x_fm(xs):
    t = xs.shape[0]
    return np.ascontiguousarray(xs.T.reshape(CC, P, t).transpose(1, 0, 2))


def kernel(L0_attn_x, L0_attn_time_decay, L0_attn_time_first, L0_attn_tm_k,
           L0_attn_tm_v, L0_attn_tm_r, L0_attn_tm_g, L0_attn_Wr, L0_attn_Wk,
           L0_attn_Wv, L0_attn_Wg, L0_attn_Wo, L0_attn_ln_g, L0_attn_ln_b,
           L0_ffn_w_gate_up, L0_ffn_w_down):
    TOWN = 1024
    TH = TOWN + HALO
    SW = TH // 2

    x = np.asarray(L0_attn_x, dtype=np.float32)[0]
    T = x.shape[0]
    assert T == NCORES * TOWN

    wk = _swizzle_w(np.ascontiguousarray(np.asarray(L0_attn_Wk, np.float32).T), 256)
    wv = _swizzle_w(np.ascontiguousarray(np.asarray(L0_attn_Wv, np.float32).T), 256)
    wr = _swizzle_w(np.ascontiguousarray(np.asarray(L0_attn_Wr, np.float32).T), 256)
    wg = _swizzle_w(np.ascontiguousarray(np.asarray(L0_attn_Wg, np.float32).T), 256)
    wo = _swizzle_w(np.ascontiguousarray(np.asarray(L0_attn_Wo, np.float32).T), 256)
    wguT = np.ascontiguousarray(np.asarray(L0_ffn_w_gate_up, np.float32).T)
    wgu_g = _swizzle_w(np.ascontiguousarray(wguT[:, 0:INTER]), 256)
    wgu_u = _swizzle_w(np.ascontiguousarray(wguT[:, INTER:]), 256)
    wdT = np.ascontiguousarray(np.asarray(L0_ffn_w_down, np.float32).T)
    wd = np.ascontiguousarray(
        wdT.reshape(8, 8, P, CC, P).transpose(0, 3, 2, 1, 4))

    vec_common = {
        "tmk": _vec_fm(np.asarray(L0_attn_tm_k, np.float32)),
        "tmv": _vec_fm(np.asarray(L0_attn_tm_v, np.float32)),
        "tmr": _vec_fm(np.asarray(L0_attn_tm_r, np.float32)),
        "tmg": _vec_fm(np.asarray(L0_attn_tm_g, np.float32)),
        "td": _vec_fm(np.asarray(L0_attn_time_decay, np.float32)),
        "tf": _vec_fm(np.asarray(L0_attn_time_first, np.float32)),
        "lng": _vec_fm(np.asarray(L0_attn_ln_g, np.float32)),
        "lnb": _vec_fm(np.asarray(L0_attn_ln_b, np.float32)),
    }
    w_common = {"wk": wk, "wv": wv, "wr": wr, "wg": wg, "wo": wo,
                "wgu_g": wgu_g, "wgu_u": wgu_u, "wd": wd}

    in_maps = []
    for c in range(NCORES):
        base = c * TOWN
        if c == 0:
            halo_rows = np.concatenate([x[0:HALO - 1], x[1:2]], axis=0)
            mask = np.zeros((P, 1), np.float32)
        else:
            halo_rows = x[base - HALO: base]
            mask = np.ones((P, 1), np.float32)
        xa = np.concatenate([halo_rows, x[base: base + TOWN]], axis=0)
        m = {
            "x_h0": _x_fm(xa[0:SW]),
            "x_h1": _x_fm(xa[SW - 1: TH]),
            "x_res": _x_fm(xa[HALO: TH]),
            "mask": mask,
        }
        m.update(vec_common)
        m.update(w_common)
        in_maps.append(m)

    nc = build_program(TOWN)
    r = run_bass_kernel_spmd(nc, in_maps, core_ids=list(range(NCORES)))
    outs = []
    for c in range(NCORES):
        o = np.asarray(r.results[c]["out"])
        outs.append(o.transpose(1, 0, 2).reshape(C, TOWN).T)
    full = np.concatenate(outs, axis=0)[None]
    return full.astype(np.float32)


# version 27
# speedup vs baseline: 1.1376x; 1.0132x over previous
import numpy as np
import ml_dtypes
import concourse.bass as bass
import concourse.mybir as mybir
from concourse.alu_op_type import AluOpType as OP
from concourse.bass_utils import run_bass_kernel_spmd

F32 = mybir.dt.float32
BF16 = mybir.dt.bfloat16
AF = mybir.ActivationFunctionType

P = 128
CC = 16
C = 2048
NCORES = 8
HALO = 128
EPS = 1e-5
INTER = 8192

ENGINES = ("tensor", "vector", "scalar", "gpsimd", "sync")
OWN_SEM = {"tensor": "P", "vector": "V", "scalar": "A", "gpsimd": "G", "sync": None}


def segs(n, cap=512):
    out = []
    o = 0
    while o < n:
        s = min(cap, n - o)
        out.append((o, s))
        o += s
    return out


class Res:

    def __init__(self, name):
        self.name = name
        self.writer = None
        self.readers = []


class Sched:

    def __init__(self, nc):
        self.nc = nc
        self.ops = []
        self.last_eng = {}
        self.barrier_idx = None

    def barrier(self):
        deps = [i for i in self.last_eng.values()]
        idx = self.op("vector", lambda e: e.engine_nop(), waits_ops=deps)
        self.barrier_idx = idx
        return idx

    def op(self, eng, fns, reads=(), writes=(), waits_ops=(), dma=False,
           dma_sem=None):
        if not isinstance(fns, (list, tuple)):
            fns = [fns]
        idx = len(self.ops)
        deps = set(waits_ops)
        rnames = [r.name for r in reads]
        wnames = [r.name for r in writes]
        for r in reads:
            if r.writer is not None:
                deps.add(r.writer)
            r.readers.append(idx)
        for r in writes:
            if r.writer is not None:
                deps.add(r.writer)
            deps.update(r.readers)
            r.writer = idx
            r.readers = []
        if dma:
            sem = dma_sem or ("dG" if eng == "gpsimd" else "dS")
        else:
            sem = OWN_SEM[eng]
        self.ops.append(dict(eng=eng, fns=list(fns), sem=sem, dma=dma,
                             deps=deps, idx=idx, rnames=rnames, wnames=wnames))
        self.last_eng[eng] = idx
        return idx

    def emit(self, sems):
        ops = self.ops
        needed = set()
        for o in ops:
            own = OWN_SEM[o["eng"]]
            for d in o["deps"]:
                dop = ops[d]
                if (not dop["dma"]) and dop["sem"] is not None and dop["sem"] == own:
                    continue
                needed.add(d)
        inc_count = {}
        op_val = {}
        for o in ops:
            s = o["sem"]
            if o["dma"]:
                inc_count[s] = inc_count.get(s, 0) + 16
                op_val[o["idx"]] = (s, inc_count[s])
            elif o["idx"] in needed and s is not None:
                inc_count[s] = inc_count.get(s, 0) + 1
                op_val[o["idx"]] = (s, inc_count[s])
            else:
                op_val[o["idx"]] = None
        per_eng = {e: [] for e in ENGINES}
        hw = {e: {} for e in ENGINES}
        for o in ops:
            own = OWN_SEM[o["eng"]]
            wmap = {}
            for d in sorted(o["deps"]):
                dop = ops[d]
                if (not dop["dma"]) and dop["sem"] is not None and dop["sem"] == own:
                    continue
                v = op_val[d]
                if v is None:
                    raise RuntimeError("dep without inc")
                s, val = v
                if wmap.get(s, 0) < val:
                    wmap[s] = val
            waits = []
            for s, val in sorted(wmap.items()):
                if hw[o["eng"]].get(s, 0) >= val:
                    continue
                hw[o["eng"]][s] = val
                waits.append((s, val))
            inc = op_val[o["idx"]]
            myinc = None
            if o["dma"]:
                myinc = (inc[0], 16)
            elif inc is not None:
                myinc = (inc[0], 1)
            per_eng[o["eng"]].append((waits, o["fns"], myinc, o))
        self._final_counts = dict(inc_count)

        audit = getattr(self, "audit", None)

        def runner(items):
            def go(e):
                for waits, fns, myinc, o in items:
                    insts = []
                    for s, val in waits:
                        e.wait_ge(sems[s], val)
                    last = None
                    for fn in fns:
                        last = fn(e)
                        insts.append(last)
                    if myinc is not None:
                        last.then_inc(sems[myinc[0]], myinc[1])
                    if audit is not None:
                        o["_insts"] = insts
            return go

        with self.nc.Block() as blk:
            for eng in ENGINES:
                getattr(blk, eng)(runner(per_eng[eng]))


def build_program(TOWN=1024, dbg=False):
    TH = TOWN + HALO
    SW = TH // 2
    BW = SW + 1
    assert TH % 2 == 0 and SW > HALO
    OW0 = SW - HALO

    nc = bass.Bass("TRN2", target_bir_lowering=False,
                   detect_race_conditions=False)

    d_xh0 = nc.dram_tensor("x_h0", [P, CC, SW], F32, kind="ExternalInput")
    d_xh1 = nc.dram_tensor("x_h1", [P, CC, SW + 1], F32, kind="ExternalInput")
    d_xres = nc.dram_tensor("x_res", [P, CC, TOWN], F32, kind="ExternalInput")
    d_w = {}
    for nm in ("wk", "wv", "wr", "wg", "wo"):
        d_w[nm] = nc.dram_tensor(nm, [8, P, CC, 256], F32, kind="ExternalInput")
    d_wgu_g = nc.dram_tensor("wgu_g", [32, P, CC, 256], F32, kind="ExternalInput")
    d_wgu_u = nc.dram_tensor("wgu_u", [32, P, CC, 256], F32, kind="ExternalInput")
    d_wd = nc.dram_tensor("wd", [8, CC, P, 8, P], F32, kind="ExternalInput")
    d_vec = {}
    for nm in ("tmk", "tmv", "tmr", "tmg", "td", "tf", "lng", "lnb"):
        d_vec[nm] = nc.dram_tensor(nm, [P, CC], F32, kind="ExternalInput")
    d_mask = nc.dram_tensor("mask", [P, 1], F32, kind="ExternalInput")
    d_out = nc.dram_tensor("out", [P, CC, TOWN], F32, kind="ExternalOutput")

    from contextlib import ExitStack
    stack = ExitStack()
    sems = {}
    for s in ("dG", "dS", "dW", "P", "V", "A", "G"):
        sems[s] = stack.enter_context(nc.semaphore(s))

    sc = Sched(nc)

    def sb(name, shape, dt):
        return stack.enter_context(nc.sbuf_tensor(name, shape, dt))

    vecs = {nm: sb("sb_" + nm, [P, CC], F32) for nm in d_vec}
    maskt = sb("maskt", [P, 1], F32)
    wfm = sb("wfm", [P, CC], F32)
    euf = sb("euf", [P, CC], F32)
    e1f = sb("e1f", [P, CC], F32)
    onesm = sb("onesm", [P, 1], BF16)
    onesr = sb("onesr", [1, P], F32)
    stA = sb("stA", [P, CC], F32)
    stB = sb("stB", [P, CC], F32)
    ag = sb("ag", [P, CC * TOWN], BF16)
    SL = [sb("SL0", [P, CC * 256], BF16), sb("SL1", [P, CC * 256], BF16),
          sb("SL2", [P, CC * 256], BF16)]

    R = {}

    def res(name):
        if name not in R:
            R[name] = Res(name)
            R[name].writer = sc.barrier_idx
        return R[name]

    def fence():
        idx = sc.barrier()
        for r in R.values():
            r.writer = idx
            r.readers = []
        return idx

    def cres(buf, cc):
        return res("%s_%d" % (buf, cc))

    def all_res(buf):
        return [cres(buf, i) for i in range(CC)]

    def dma(eng, dst_ap, src_ap, reads=(), writes=(), waits=(), dma_sem=None):
        return sc.op(eng, lambda e, d=dst_ap, s=src_ap: e.dma_start(d, s),
                     reads=reads, writes=writes, waits_ops=waits, dma=True,
                     dma_sem=dma_sem)

    def act(out_ap, in_ap, func, bias=0.0, scale=1.0, reads=(), writes=(), waits=()):
        return sc.op("scalar",
                     lambda e, o=out_ap, i=in_ap, f=func, b=bias, s=scale:
                     e.activation(o, i, f, bias=b, scale=s),
                     reads=reads, writes=writes, waits_ops=waits)

    def vop(fn, reads=(), writes=(), waits=()):
        return sc.op("vector", fn, reads=reads, writes=writes, waits_ops=waits)

    slab_seq = [0]
    slab_pos = {}

    def load_slab_one(dram, key=None):
        k = key if key is not None else id(dram)
        s = slab_pos.get(k, 0)
        slab_pos[k] = s + 1
        par = slab_seq[0] % 3
        slab_seq[0] += 1
        t = dma("gpsimd",
                SL[par][:, :].rearrange("p (c w) -> p c w", w=256),
                dram.ap()[s],
                writes=[res("SL%d" % par)])
        return t, par

    for nm in d_vec:
        dma("sync", vecs[nm][:, :], d_vec[nm].ap(), writes=[res("vec_" + nm)])
    dma("sync", maskt[:, :], d_mask.ap(), writes=[res("maskt")])
    vop(lambda e: e.memset(onesm[:, :], 1.0 / 2048.0), writes=[res("onesm")])
    vop(lambda e: e.memset(onesr[:, :], 1.0), writes=[res("onesr")])
    act(e1f[:, :], vecs["td"][:, :], AF.Exp,
        reads=[res("vec_td")], writes=[res("e1f")])
    act(euf[:, :], vecs["tf"][:, :], AF.Exp,
        reads=[res("vec_tf")], writes=[res("euf")])
    vop(lambda e: e.memset(wfm[:, 0:1], 1.0), writes=[res("wfm")])
    for fn in (AF.Square, AF.Sigmoid, AF.Silu, AF.Ln, AF.Exp):
        act(wfm[:, 0:1], wfm[:, 0:1], fn, writes=[res("wfm")])
    act(wfm[:, :], e1f[:, :], AF.Exp, scale=-1.0,
        reads=[res("e1f")], writes=[res("wfm")])
    fence()

    with ExitStack() as hstack:
        def hsb(name, shape, dt):
            return hstack.enter_context(nc.sbuf_tensor(name, shape, dt))

        def hps(name, shape, dt=F32):
            return hstack.enter_context(nc.psum_tensor(name, shape, dt))

        Lb = hsb("Lb", [P, CC * BW], BF16)
        Eb = hsb("Eb", [P, CC * BW], BF16)
        Vb = hsb("Vb", [P, CC * BW], BF16)
        M1 = hsb("M1", [P, CC * BW], BF16)
        M2 = hsb("M2", [P, CC * BW], BF16)
        R1 = hsb("R1", [P, CC * BW], BF16)
        R3 = hsb("R3", [P, CC * BW], BF16)
        tmix = hsb("tmix", [P, BW], BF16)
        tf1 = [hsb("tf1a", [P, BW], F32), hsb("tf1b", [P, BW], F32)]
        tf2 = [hsb("tf2a", [P, BW], F32), hsb("tf2b", [P, BW], F32)]
        tf3 = hsb("tf3a", [P, BW], F32)
        mub_h = hsb("mub_h", [P, BW], F32)
        rsb_h = hsb("rsb_h", [P, BW], F32)
        hmu_row = hsb("hmu_row", [1, BW], F32)
        hs2_row = hsb("hs2_row", [1, BW], F32)
        hrs_row = hsb("hrs_row", [1, BW], F32)
        pp = [hps("pp0", [P, BW], F32), hps("pp1", [P, BW], F32)]
        ps_mu = hps("ps_mu", [1, BW], F32)
        ps_s2 = hps("ps_s2", [1, BW], F32)

        def kwarm(tok):
            sc.op("tensor", lambda e: e.matmul(pp[0][0:1, 0:1], onesm[:, 0:1],
                                               onesm[:, 0:1], start=True,
                                               stop=True), waits_ops=(tok,))

        def chunk(buf, cc, lo, n, stride=BW):
            return buf[:, cc * stride + lo: cc * stride + lo + n]

        def halo_view(buf):
            return buf[:, :].rearrange("p (c w) -> p c w", w=BW)[:, :, 0:HALO]

        def col_view(buf, col):
            return buf[:, :].rearrange("p (c w) -> p c w", w=BW)[:, :, col:col + 1]

        if dbg:
            for buf, bres in ((Lb, "Lb"), (Eb, "Eb"), (Vb, "Vb"),
                              (M1, "M1"), (M2, "M2"), (R1, "R1"), (R3, "R3")):
                vop(lambda e, buf=buf: e.memset(buf[:, :], 0.0),
                    writes=[res(bres)] + all_res(bres))

        def ln_stats_apply(ln_w, xbf_res, sq_res, xbuf, sqbuf, outbuf,
                           t_seg0=None):
            s0w = min(512, ln_w)
            xv0 = xbuf[:, :].rearrange("p (c w) -> p c w", w=BW)[:, :, 0:s0w]
            sv0 = sqbuf[:, :].rearrange("p (c w) -> p c w", w=BW)[:, :, 0:s0w]
            t_sq0 = act(sv0, xv0, AF.Square, writes=all_res(sq_res),
                        waits=(t_seg0,) if t_seg0 is not None else ())
            if ln_w > s0w:
                xv1 = xbuf[:, :].rearrange("p (c w) -> p c w",
                                           w=BW)[:, :, s0w:ln_w]
                sv1 = sqbuf[:, :].rearrange("p (c w) -> p c w",
                                            w=BW)[:, :, s0w:ln_w]
                act(sv1, xv1, AF.Square, reads=all_res(xbf_res),
                    writes=all_res(sq_res))
            for stat, (srcb, sres, psum, pres) in enumerate(
                    ((xbuf, xbf_res, ps_mu, "ps_mu"),
                     (sqbuf, sq_res, ps_s2, "ps_s2"))):
                for (o, n) in segs(ln_w, 512):
                    fns = []
                    for cci in range(CC):
                        fns.append(lambda e, srcb=srcb, psum=psum, o=o, n=n, cci=cci:
                                   e.matmul(psum[0:1, o:o + n], onesm[:, 0:1],
                                            chunk(srcb, cci, o, n),
                                            start=(cci == 0), stop=(cci == CC - 1)))
                    if o == 0 and stat == 0 and t_seg0 is not None:
                        rds, wts = [res("onesm")], (t_seg0,)
                    elif o == 0 and stat == 1:
                        rds, wts = [res("onesm")], (t_sq0,)
                    else:
                        rds, wts = all_res(sres) + [res("onesm")], ()
                    sc.op("tensor", fns, reads=rds, writes=[res(pres)],
                          waits_ops=wts)
            vop(lambda e: e.tensor_copy(hmu_row[0:1, 0:ln_w], ps_mu[0:1, 0:ln_w]),
                reads=[res("ps_mu")], writes=[res("hmu_row")])
            vop(lambda e: e.tensor_copy(hs2_row[0:1, 0:ln_w], ps_s2[0:1, 0:ln_w]),
                reads=[res("ps_s2")], writes=[res("hs2_row")])
            vop(lambda e: e.tensor_mul(hrs_row[0:1, 0:ln_w], hmu_row[0:1, 0:ln_w],
                                       hmu_row[0:1, 0:ln_w]),
                reads=[res("hmu_row")], writes=[res("hrs_row")])
            vop(lambda e: e.tensor_sub(hs2_row[0:1, 0:ln_w], hs2_row[0:1, 0:ln_w],
                                       hrs_row[0:1, 0:ln_w]),
                reads=[res("hrs_row")], writes=[res("hs2_row")])
            vop(lambda e: e.tensor_single_scalar(
                hs2_row[0:1, 0:ln_w], hs2_row[0:1, 0:ln_w], EPS, OP.add),
                writes=[res("hs2_row")])
            act(hs2_row[0:1, 0:ln_w], hs2_row[0:1, 0:ln_w], AF.Ln,
                writes=[res("hs2_row")])
            act(hrs_row[0:1, 0:ln_w], hs2_row[0:1, 0:ln_w], AF.Exp, scale=-0.5,
                reads=[res("hs2_row")], writes=[res("hrs_row")])
            for row, rres, dstb, dres, psel in (
                    (hmu_row, "hmu_row", mub_h, "mub_h", 0),
                    (hrs_row, "hrs_row", rsb_h, "rsb_h", 1)):
                fns = []
                for (o, n) in segs(ln_w, 512):
                    fns.append(lambda e, row=row, o=o, n=n, psel=psel:
                               e.matmul(pp[psel][0:128, o:o + n], onesr[0:1, 0:P],
                                        row[0:1, o:o + n], start=True, stop=True))
                sc.op("tensor", fns, reads=[res(rres), res("onesr")],
                      writes=[res("pp%d" % psel)])
                vop(lambda e, dstb=dstb, psel=psel, ln_w=ln_w:
                    e.tensor_copy(dstb[:, 0:ln_w], pp[psel][:, 0:ln_w]),
                    reads=[res("pp%d" % psel)], writes=[res(dres)])
            for cci in range(CC):
                vop(lambda e, cci=cci, ln_w=ln_w: e.tensor_sub(
                    tf1[0][:, 0:ln_w], chunk(xbuf, cci, 0, ln_w),
                    mub_h[:, 0:ln_w]),
                    reads=[cres(xbf_res, cci), res("mub_h")],
                    writes=[res("tf1_0")])
                t_ap = vop(lambda e, cci=cci, ln_w=ln_w: e.tensor_mul(
                    chunk(outbuf, cci, 0, ln_w), tf1[0][:, 0:ln_w],
                    rsb_h[:, 0:ln_w]),
                    reads=[res("tf1_0"), res("rsb_h")], writes=[res("Lb")])
                kwarm(t_ap)

        def mix_into(dstbuf, dres, tmname, lnoff, lo, n, h, kw=False):
            for cci in range(CC):
                if h == 0 and lo == 0:
                    vop(lambda e: e.memset(tmix[:, 0:1], 0.0),
                        writes=[res("tmix")])
                    vop(lambda e, cci=cci, n=n: e.tensor_sub(
                        tmix[:, 1:n], chunk(Lb, cci, 0, n - 1),
                        chunk(Lb, cci, 1, n - 1)),
                        reads=[res("Lb")], writes=[res("tmix")])
                else:
                    off = lnoff + lo
                    vop(lambda e, cci=cci, off=off, n=n: e.tensor_sub(
                        tmix[:, 0:n], chunk(Lb, cci, off - 1, n),
                        chunk(Lb, cci, off, n)),
                        reads=[res("Lb")], writes=[res("tmix")])
                t_st = vop(lambda e, cci=cci, tmname=tmname, dstbuf=dstbuf,
                           off=lnoff + lo, n=n:
                           e.scalar_tensor_tensor(
                               chunk(dstbuf, cci, 0, n), tmix[:, 0:n],
                               vecs[tmname][:, cci:cci + 1],
                               chunk(Lb, cci, off, n), OP.mult, OP.add),
                           reads=[res("tmix"), res("vec_" + tmname),
                                  res("Lb")],
                           writes=[cres(dres, cci)])
                if kw:
                    kwarm(t_st)

        def projection(wname, mixbuf, mres, width, evac_fn):
            for s in range(8):
                t_slab, par = load_slab_one(d_w[wname], key=wname)
                for osub in range(2):
                    oc = s * 2 + osub
                    psel = oc % 2
                    fns = []
                    for cci in range(CC):
                        for (o, n) in segs(width, 512):
                            fns.append(lambda e, par=par, cci=cci, osub=osub,
                                       o=o, n=n, psel=psel, mixbuf=mixbuf:
                                       e.matmul(pp[psel][:, o:o + n],
                                                SL[par][:, cci * 256 + osub * P:
                                                        cci * 256 + osub * P + P],
                                                chunk(mixbuf, cci, o, n),
                                                start=(cci == 0),
                                                stop=(cci == CC - 1)))
                    sc.op("tensor", fns,
                          reads=[res("SL%d" % par)] + all_res(mres),
                          writes=[res("pp%d" % psel)], waits_ops=(t_slab,))
                    evac_fn(oc, psel)

        for h in (0, 1):
            for wnm in ("wk", "wv", "wr", "wg"):
                slab_pos.pop(wnm, None)
            ln_w = SW if h == 0 else SW + 1
            lnoff = 0 if h == 0 else 1
            lo = HALO if h == 0 else 0
            now = SW - lo
            gcol0 = 0 if h == 0 else OW0
            d_x = d_xh0 if h == 0 else d_xh1

            s0w = min(512, ln_w)
            t_x1 = dma("gpsimd",
                       Eb[:, :].rearrange("p (c w) -> p c w",
                                          w=BW)[:, :, 0:s0w],
                       d_x.ap()[:, :, 0:s0w], writes=all_res("Eb"))
            if ln_w > s0w:
                dma("gpsimd",
                    Eb[:, :].rearrange("p (c w) -> p c w",
                                       w=BW)[:, :, s0w:ln_w],
                    d_x.ap()[:, :, s0w:ln_w], writes=all_res("Eb"))
            ln_stats_apply(ln_w, "Eb", "Vb", Eb, Vb, Lb, t_x1)

            mix_into(M1, "M1", "tmk", lnoff, 0, SW, h, kw=True)
            mix_into(M2, "M2", "tmv", lnoff, 0, SW, h)
            mix_into(R1, "R1", "tmr", lnoff, lo, now, h if lo == 0 else 1)

            projection("wk", M1, "M1", SW,
                       lambda oc, psel: act(chunk(Eb, oc, 0, SW),
                                            pp[psel][:, 0:SW], AF.Exp,
                                            reads=[res("pp%d" % psel)],
                                            writes=[cres("Eb", oc)]))
            projection("wv", M2, "M2", SW,
                       lambda oc, psel: sc.op(
                           "scalar",
                           lambda e, oc=oc, psel=psel:
                           e.copy(chunk(Vb, oc, 0, SW), pp[psel][:, 0:SW]),
                           reads=[res("pp%d" % psel)], writes=[cres("Vb", oc)]))

            projection("wr", R1, "R1", now,
                       lambda oc, psel: act(chunk(R3, oc, 0, now),
                                            pp[psel][:, 0:now], AF.Sigmoid,
                                            reads=[res("pp%d" % psel)],
                                            writes=[cres("R3", oc)]))

            if h == 0:
                for cci in range(CC):
                    vop(lambda e, cci=cci: e.tensor_single_scalar(
                        Eb[:, cci * BW: cci * BW + HALO],
                        Eb[:, cci * BW: cci * BW + HALO],
                        maskt[:, 0:1], OP.mult),
                        reads=[res("maskt")], writes=[cres("Eb", cci)])

            pend_qa = []
            for cci in range(CC):
                vop(lambda e, cci=cci: e.tensor_mul(
                    chunk(M1, cci, 0, SW), chunk(Eb, cci, 0, SW),
                    chunk(Vb, cci, 0, SW)),
                    reads=[cres("Eb", cci), cres("Vb", cci)],
                    writes=[cres("M1", cci)])
                init_a = 0.0 if h == 0 else stA[:, cci:cci + 1]
                wb = wfm[:, cci:cci + 1].broadcast_to([P, SW])
                pend_qa.append(lambda e=None, cci=cci, init_a=init_a, wb=wb:
                               vop(lambda e, cci=cci, init_a=init_a, wb=wb:
                                   e.tensor_tensor_scan(
                                   chunk(Vb, cci, 0, SW), wb,
                                   chunk(M1, cci, 0, SW), init_a,
                                   OP.mult, OP.add),
                                   reads=[res("wfm"), cres("M1", cci),
                                          res("stA")],
                                   writes=[cres("Vb", cci)]))
                if len(pend_qa) >= 2:
                    pend_qa.pop(0)()

            def scan_pair(cci):
                init_b = 0.0 if h == 0 else stB[:, cci:cci + 1]
                wb = wfm[:, cci:cci + 1].broadcast_to([P, SW])
                vop(lambda e, cci=cci, init_b=init_b, wb=wb:
                    e.tensor_tensor_scan(
                    chunk(M2, cci, 0, SW), wb,
                    chunk(Eb, cci, 0, SW), init_b, OP.mult, OP.add),
                    reads=[res("wfm"), cres("Eb", cci), res("stB")],
                    writes=[cres("M2", cci)])

            def epilogue(cci):
                par = cci % 2
                if lo == 0:
                    vop(lambda e, cci=cci, par=par: e.scalar_tensor_tensor(
                        tf1[par][:, 0:1], M1[:, cci * BW: cci * BW + 1],
                        euf[:, cci:cci + 1], stA[:, cci:cci + 1],
                        OP.mult, OP.add),
                        reads=[cres("M1", cci), res("euf"), res("stA")],
                        writes=[res("tf1_%d" % par)])
                    vop(lambda e, cci=cci, par=par: e.scalar_tensor_tensor(
                        tf2[par][:, 0:1], Eb[:, cci * BW: cci * BW + 1],
                        euf[:, cci:cci + 1], stB[:, cci:cci + 1],
                        OP.mult, OP.add),
                        reads=[cres("Eb", cci), res("euf"), res("stB")],
                        writes=[res("tf2_%d" % par)])
                    o1, n1 = 1, now - 1
                else:
                    o1, n1 = 0, now
                vop(lambda e, cci=cci, par=par, lo=lo, o1=o1, n1=n1:
                    e.scalar_tensor_tensor(
                    tf1[par][:, o1:o1 + n1], chunk(M1, cci, lo + o1, n1),
                    euf[:, cci:cci + 1], chunk(Vb, cci, lo + o1 - 1, n1),
                    OP.mult, OP.add),
                    reads=[cres("M1", cci), cres("Vb", cci), res("euf")],
                    writes=[res("tf1_%d" % par)])
                vop(lambda e, cci=cci, par=par, lo=lo, o1=o1, n1=n1:
                    e.scalar_tensor_tensor(
                    tf2[par][:, o1:o1 + n1], chunk(Eb, cci, lo + o1, n1),
                    euf[:, cci:cci + 1], chunk(M2, cci, lo + o1 - 1, n1),
                    OP.mult, OP.add),
                    reads=[cres("Eb", cci), cres("M2", cci), res("euf")],
                    writes=[res("tf2_%d" % par)])
                act(tf3[:, 0:now], tf2[par][:, 0:now], AF.Ln,
                    reads=[res("tf2_%d" % par)], writes=[res("tf3")])
                act(tf2[par][:, 0:now], tf3[:, 0:now], AF.Exp, scale=-1.0,
                    reads=[res("tf3")], writes=[res("tf2_%d" % par)])

            def wkv_mul(cci):
                par = cci % 2
                vop(lambda e, cci=cci, par=par, lo=lo, now=now: e.tensor_mul(
                    chunk(M1, cci, lo, now), tf1[par][:, 0:now],
                    tf2[par][:, 0:now]),
                    reads=[res("tf1_%d" % par), res("tf2_%d" % par)],
                    writes=[cres("M1", cci)])

            for f in pend_qa:
                f()
            pend_qa = []
            for cci in range(CC):
                scan_pair(cci)
            if h == 0:
                vop(lambda e: e.tensor_copy(
                    stA[:, :],
                    Vb[:, :].rearrange("p (c w) -> p c w", w=BW)[:, :, SW - 1:SW]),
                    reads=all_res("Vb"), writes=[res("stA")])
                vop(lambda e: e.tensor_copy(
                    stB[:, :],
                    M2[:, :].rearrange("p (c w) -> p c w", w=BW)[:, :, SW - 1:SW]),
                    reads=all_res("M2"), writes=[res("stB")])
            pend = []
            for cci in range(CC):
                epilogue(cci)
                for f in pend:
                    f()
                pend = [lambda cci=cci: wkv_mul(cci)]
            for f in pend:
                f()
            mix_into(R1, "R1", "tmg", lnoff, lo, now, h if lo == 0 else 1)

            def g_evac(oc, psel, lo=lo, now=now, gcol0=gcol0):
                tpar = 0
                act(tmix[:, 0:now], pp[psel][:, 0:now], AF.Sigmoid,
                    reads=[res("pp%d" % psel)], writes=[res("tmix")])
                vop(lambda e, oc=oc, tpar=tpar, lo=lo, now=now: e.tensor_mul(
                    tf1[tpar][:, 0:now], tmix[:, 0:now],
                    chunk(M1, oc, lo, now)),
                    reads=[res("tmix"), cres("M1", oc)],
                    writes=[res("tf1_%d" % tpar)])
                vop(lambda e, oc=oc, tpar=tpar, gcol0=gcol0, now=now:
                    e.tensor_mul(
                    ag[:, oc * TOWN + gcol0: oc * TOWN + gcol0 + now],
                    tf1[tpar][:, 0:now], chunk(R3, oc, 0, now)),
                    reads=[res("tf1_%d" % tpar), cres("R3", oc)],
                    writes=[res("ag")])
            projection("wg", R1, "R1", now, g_evac)

        if dbg:
            for nm, buf, wdt in (("dbg_ag", ag, CC * TOWN),
                                 ("dbg_M1", M1, CC * BW),
                                 ("dbg_R3", R3, CC * BW),
                                 ("dbg_Lb", Lb, CC * BW),
                                 ("dbg_R1", R1, CC * BW),
                                 ("dbg_M2", M2, CC * BW)):
                dd = nc.dram_tensor(nm, [P, wdt], BF16, kind="ExternalOutput")
                bn = nm.replace("dbg_", "") if nm != "dbg_ag" else "ag"
                dma("sync", dd.ap(), buf[:, :],
                    reads=([res("ag")] if bn == "ag" else all_res(bn)))
        wo_pre = [load_slab_one(d_w["wo"], key="wo") for _ in range(2)]
    fence()

    with ExitStack() as pstack:
        def psb(name, shape, dt):
            return pstack.enter_context(nc.sbuf_tensor(name, shape, dt))

        acc = psb("acc", [P, CC * TOWN], F32)
        tp1 = psb("tp1", [P, TOWN], F32)
        tbf = psb("tbf", [P, TOWN], BF16)
        mu_b = psb("mu_b", [P, TOWN], F32)
        rs_b = psb("rs_b", [P, TOWN], F32)
        mu_row = psb("mu_row2", [1, TOWN], F32)
        s2_row = psb("s2_row2", [1, TOWN], F32)
        rs_row = psb("rs_row2", [1, TOWN], F32)
        wstack = ExitStack()
        attn_o = wstack.enter_context(nc.sbuf_tensor("attn_o", [P, CC * TOWN], BF16))
        qpsum = ExitStack()
        po = [qpsum.enter_context(nc.psum_tensor("po0", [P, TOWN], F32)),
              qpsum.enter_context(nc.psum_tensor("po1", [P, TOWN], F32))]
        ps2_mu = qpsum.enter_context(nc.psum_tensor("ps2_mu", [1, TOWN], F32))
        ps2_s2 = qpsum.enter_context(nc.psum_tensor("ps2_s2", [1, TOWN], F32))

        dma("sync", acc[:, :].rearrange("p (c w) -> p c w", w=TOWN),
            d_xres.ap(), writes=[res("acc")])

        def stream_stats(oc, srcbuf, srcres, start):
            vop(lambda e, oc=oc, srcbuf=srcbuf: e.tensor_copy(
                tbf[:, :], srcbuf[:, oc * TOWN:(oc + 1) * TOWN]),
                reads=[res(srcres)], writes=[res("tbf")])
            for stepi in range(2):
                psum, pres = ((ps2_mu, "ps2_mu"), (ps2_s2, "ps2_s2"))[stepi]
                if stepi == 1:
                    act(tbf[:, :], tbf[:, :], AF.Square, writes=[res("tbf")])
                fns = []
                for (o, n) in segs(TOWN, 512):
                    fns.append(lambda e, psum=psum, o=o, n=n:
                               e.matmul(psum[0:1, o:o + n], onesm[:, 0:1],
                                        tbf[:, o:o + n],
                                        start=start, stop=(oc == CC - 1)))
                sc.op("tensor", fns, reads=[res("tbf"), res("onesm")],
                      writes=[] if not start else [res(pres)],
                      waits_ops=(R[pres].writer,) if (start and R[pres].writer
                                                      is not None) else ())
                if not start:
                    R[pres].writer = sc.ops[-1]["idx"]

        def finalize_rows_and_bcast():
            vop(lambda e: e.tensor_copy(mu_row[0:1, 0:TOWN], ps2_mu[0:1, 0:TOWN]),
                reads=[res("ps2_mu")], writes=[res("mu_row")])
            vop(lambda e: e.tensor_copy(s2_row[0:1, 0:TOWN], ps2_s2[0:1, 0:TOWN]),
                reads=[res("ps2_s2")], writes=[res("s2_row")])
            vop(lambda e: e.tensor_mul(rs_row[0:1, 0:TOWN], mu_row[0:1, 0:TOWN],
                                       mu_row[0:1, 0:TOWN]),
                reads=[res("mu_row")], writes=[res("rs_row")])
            vop(lambda e: e.tensor_sub(s2_row[0:1, 0:TOWN], s2_row[0:1, 0:TOWN],
                                       rs_row[0:1, 0:TOWN]),
                reads=[res("rs_row")], writes=[res("s2_row")])
            vop(lambda e: e.tensor_single_scalar(
                s2_row[0:1, 0:TOWN], s2_row[0:1, 0:TOWN], EPS, OP.add),
                writes=[res("s2_row")])
            act(s2_row[0:1, 0:TOWN], s2_row[0:1, 0:TOWN], AF.Ln,
                writes=[res("s2_row")])
            act(rs_row[0:1, 0:TOWN], s2_row[0:1, 0:TOWN], AF.Exp, scale=-0.5,
                reads=[res("s2_row")], writes=[res("rs_row")])
            for row, rres, dstb, dres, psel in (
                    (mu_row, "mu_row", mu_b, "mu_b", 0),
                    (rs_row, "rs_row", rs_b, "rs_b", 1)):
                fns = []
                for (o, n) in segs(TOWN, 512):
                    fns.append(lambda e, row=row, o=o, n=n, psel=psel:
                               e.matmul(po[psel][0:128, o:o + n], onesr[0:1, 0:P],
                                        row[0:1, o:o + n], start=True, stop=True))
                sc.op("tensor", fns, reads=[res(rres), res("onesr")],
                      writes=[res("po%d" % psel)])
                vop(lambda e, dstb=dstb, psel=psel:
                    e.tensor_copy(dstb[:, 0:TOWN], po[psel][:, 0:TOWN]),
                    reads=[res("po%d" % psel)], writes=[res(dres)])

        pend_stats = []
        for s in range(8):
            t_slab, par = (wo_pre[s] if s < 2 else
                           load_slab_one(d_w["wo"], key="wo"))
            for osub in range(2):
                oc = s * 2 + osub
                psel = oc % 2
                fns = []
                for cci in range(CC):
                    for (o, n) in segs(TOWN, 512):
                        fns.append(lambda e, par=par, cci=cci, osub=osub, o=o,
                                   n=n, psel=psel:
                                   e.matmul(po[psel][:, o:o + n],
                                            SL[par][:, cci * 256 + osub * P:
                                                    cci * 256 + osub * P + P],
                                            ag[:, cci * TOWN + o:
                                               cci * TOWN + o + n],
                                            start=(cci == 0),
                                            stop=(cci == CC - 1)))
                sc.op("tensor", fns, reads=[res("SL%d" % par), res("ag")],
                      writes=[res("po%d" % psel)], waits_ops=(t_slab,))
                sc.op("scalar", lambda e, oc=oc, psel=psel:
                      e.copy(attn_o[:, oc * TOWN:(oc + 1) * TOWN],
                             po[psel][:, 0:TOWN]),
                      reads=[res("po%d" % psel)], writes=[res("attn_o")])
                if len(pend_stats) >= 2:
                    pend_stats.pop(0)()
                pend_stats.append(lambda oc=oc: stream_stats(
                    oc, attn_o, "attn_o", start=(oc == 0)))
        for f in pend_stats:
            f()
        finalize_rows_and_bcast()
        for cci in range(CC):
            sc.op("gpsimd", lambda e, cci=cci: e.tensor_sub(
                tp1[:, :], attn_o[:, cci * TOWN:(cci + 1) * TOWN], mu_b[:, :]),
                reads=[res("attn_o"), res("mu_b")], writes=[res("tp1")])
            vop(lambda e: e.tensor_mul(tp1[:, :], tp1[:, :], rs_b[:, :]),
                reads=[res("rs_b")], writes=[res("tp1")])
            vop(lambda e, cci=cci: e.tensor_scalar(
                tp1[:, :], tp1[:, :], vecs["lng"][:, cci:cci + 1],
                vecs["lnb"][:, cci:cci + 1], OP.mult, OP.add),
                reads=[res("vec_lng"), res("vec_lnb")], writes=[res("tp1")])
            vop(lambda e, cci=cci: e.tensor_add(
                acc[:, cci * TOWN:(cci + 1) * TOWN],
                acc[:, cci * TOWN:(cci + 1) * TOWN], tp1[:, :]),
                reads=[res("tp1")], writes=[res("acc")])
            stream_stats(cci, acc, "acc", start=(cci == 0))

        if dbg:
            dd = nc.dram_tensor("dbg_attn_o", [P, CC * TOWN], BF16,
                                kind="ExternalOutput")
            dma("sync", dd.ap(), attn_o[:, :], reads=[res("attn_o")])
        wstack.close()
        fence()
        ln2 = psb("ln2", [P, CC * TOWN], BF16)
        finalize_rows_and_bcast()
        for cci in range(CC):
            sc.op("gpsimd", lambda e, cci=cci: e.tensor_sub(
                tp1[:, :], acc[:, cci * TOWN:(cci + 1) * TOWN], mu_b[:, :]),
                reads=[res("acc"), res("mu_b")], writes=[res("tp1")])
            t_l2 = vop(lambda e, cci=cci: e.tensor_mul(
                ln2[:, cci * TOWN:(cci + 1) * TOWN], tp1[:, :], rs_b[:, :]),
                reads=[res("tp1"), res("rs_b")], writes=[res("ln2")])
            sc.op("tensor", lambda e: e.matmul(po[0][0:1, 0:1], onesm[:, 0:1],
                                               onesm[:, 0:1], start=True,
                                               stop=True), waits_ops=(t_l2,))
        if dbg:
            dd = nc.dram_tensor("dbg_ln2", [P, CC * TOWN], BF16,
                                kind="ExternalOutput")
            dma("sync", dd.ap(), ln2[:, :], reads=[res("ln2")])
        qpsum.close()

        with ExitStack() as fstack:
            def fsb(name, shape, dt):
                return fstack.enter_context(nc.sbuf_tensor(name, shape, dt))

            def fps(name, shape, dt=F32):
                return fstack.enter_context(nc.psum_tensor(name, shape, dt))

            hs = [ag[:, 0:8 * TOWN], ag[:, 8 * TOWN:16 * TOWN]]
            wdb = [fsb("wdb0", [P, 8 * P], BF16), fsb("wdb1", [P, 8 * P], BF16)]
            th2 = [fsb("th0", [P, TOWN], BF16), fsb("th1", [P, TOWN], BF16)]
            pg = fps("pg", [P, TOWN], F32)
            pu = fps("pu", [P, TOWN], F32)
            pd = [fps("pd0", [P, TOWN], F32), fps("pd1", [P, TOWN], F32)]
            def emit_down(s, hc):
                hpar = s % 2
                wpar = hc % 2
                dma("gpsimd",
                    wdb[wpar][:, :].rearrange("p (i w) -> p i w", w=P),
                    d_wd.ap()[s][hc], writes=[res("wdb%d" % wpar)])
                fns = []
                for ic in range(8):
                    for (o, n) in segs(TOWN, 512):
                        fns.append(lambda e, wpar=wpar, ic=ic, o=o, n=n,
                                   hpar=hpar:
                                   e.matmul(pd[wpar][:, o:o + n],
                                            wdb[wpar][:, ic * P:(ic + 1) * P],
                                            hs[hpar][:, ic * TOWN + o:
                                                     ic * TOWN + o + n],
                                            start=(ic == 0), stop=(ic == 7)))
                sc.op("tensor", fns,
                      reads=[res("wdb%d" % wpar), res("hs%d" % hpar)],
                      writes=[res("pd%d" % wpar)])
                vop(lambda e, hc=hc, wpar=wpar: e.tensor_add(
                    acc[:, hc * TOWN:(hc + 1) * TOWN],
                    acc[:, hc * TOWN:(hc + 1) * TOWN], pd[wpar][:, :]),
                    reads=[res("pd%d" % wpar)], writes=[res("acc")])

            down_pend = []
            for s in range(8):
                hpar = s % 2
                for ic in range(8):
                    gic = 8 * s + ic
                    slab = gic // 2
                    within = gic % 2
                    if within == 0:
                        tg_, parg = load_slab_one(d_wgu_g, key="g")
                        tu_, paru = load_slab_one(d_wgu_u, key="u")
                        sc._ffn_cur = {"g": (tg_, parg), "u": (tu_, paru)}
                    cur = sc._ffn_cur
                    for which in ("g", "u"):
                        t_slab, par = cur[which]
                        psum, pres = (pg, "pg") if which == "g" else (pu, "pu")
                        fns = []
                        for cci in range(CC):
                            for (o, n) in segs(TOWN, 512):
                                fns.append(lambda e, par=par, cci=cci,
                                           within=within, o=o, n=n, psum=psum:
                                           e.matmul(psum[:, o:o + n],
                                                    SL[par][:, cci * 256
                                                            + within * P:
                                                            cci * 256
                                                            + within * P + P],
                                                    ln2[:, cci * TOWN + o:
                                                        cci * TOWN + o + n],
                                                    start=(cci == 0),
                                                    stop=(cci == CC - 1)))
                        sc.op("tensor", fns,
                              reads=[res("SL%d" % par), res("ln2")],
                              writes=[res(pres)], waits_ops=(t_slab,))
                    tpar = ic % 2
                    act(th2[tpar][:, :], pg[:, :], AF.Silu,
                        reads=[res("pg")], writes=[res("th_%d" % tpar)])
                    vop(lambda e, hpar=hpar, ic=ic, tpar=tpar: e.tensor_mul(
                        hs[hpar][:, ic * TOWN:(ic + 1) * TOWN],
                        th2[tpar][:, :], pu[:, :]),
                        reads=[res("th_%d" % tpar), res("pu")],
                        writes=[res("hs%d" % hpar)])
                    for _ in range(2):
                        if down_pend:
                            emit_down(*down_pend.pop(0))
                down_pend.extend((s, hc) for hc in range(CC))
            for s_hc in down_pend:
                emit_down(*s_hc)

        t_out = dma("sync", d_out.ap(),
                    acc[:, :].rearrange("p (c w) -> p c w", w=TOWN),
                    reads=[res("acc")])
        sc.op("sync", lambda e: e.nop(), waits_ops=(t_out,))

    sc.emit(sems)
    nc.finalize()
    stack.close()
    return nc



def _swizzle_w(wT, oslab):
    cin, o = wT.shape
    return np.ascontiguousarray(
        wT.reshape(CC, P, o // oslab, oslab).transpose(2, 1, 0, 3))


def _vec_fm(v):
    return np.ascontiguousarray(v.reshape(CC, P).T)


def _x_fm(xs):
    t = xs.shape[0]
    return np.ascontiguousarray(xs.T.reshape(CC, P, t).transpose(1, 0, 2))


def kernel(L0_attn_x, L0_attn_time_decay, L0_attn_time_first, L0_attn_tm_k,
           L0_attn_tm_v, L0_attn_tm_r, L0_attn_tm_g, L0_attn_Wr, L0_attn_Wk,
           L0_attn_Wv, L0_attn_Wg, L0_attn_Wo, L0_attn_ln_g, L0_attn_ln_b,
           L0_ffn_w_gate_up, L0_ffn_w_down):
    TOWN = 1024
    TH = TOWN + HALO
    SW = TH // 2

    x = np.asarray(L0_attn_x, dtype=np.float32)[0]
    T = x.shape[0]
    assert T == NCORES * TOWN

    wk = _swizzle_w(np.ascontiguousarray(np.asarray(L0_attn_Wk, np.float32).T), 256)
    wv = _swizzle_w(np.ascontiguousarray(np.asarray(L0_attn_Wv, np.float32).T), 256)
    wr = _swizzle_w(np.ascontiguousarray(np.asarray(L0_attn_Wr, np.float32).T), 256)
    wg = _swizzle_w(np.ascontiguousarray(np.asarray(L0_attn_Wg, np.float32).T), 256)
    wo = _swizzle_w(np.ascontiguousarray(np.asarray(L0_attn_Wo, np.float32).T), 256)
    wguT = np.ascontiguousarray(np.asarray(L0_ffn_w_gate_up, np.float32).T)
    wgu_g = _swizzle_w(np.ascontiguousarray(wguT[:, 0:INTER]), 256)
    wgu_u = _swizzle_w(np.ascontiguousarray(wguT[:, INTER:]), 256)
    wdT = np.ascontiguousarray(np.asarray(L0_ffn_w_down, np.float32).T)
    wd = np.ascontiguousarray(
        wdT.reshape(8, 8, P, CC, P).transpose(0, 3, 2, 1, 4))

    vec_common = {
        "tmk": _vec_fm(np.asarray(L0_attn_tm_k, np.float32)),
        "tmv": _vec_fm(np.asarray(L0_attn_tm_v, np.float32)),
        "tmr": _vec_fm(np.asarray(L0_attn_tm_r, np.float32)),
        "tmg": _vec_fm(np.asarray(L0_attn_tm_g, np.float32)),
        "td": _vec_fm(np.asarray(L0_attn_time_decay, np.float32)),
        "tf": _vec_fm(np.asarray(L0_attn_time_first, np.float32)),
        "lng": _vec_fm(np.asarray(L0_attn_ln_g, np.float32)),
        "lnb": _vec_fm(np.asarray(L0_attn_ln_b, np.float32)),
    }
    w_common = {"wk": wk, "wv": wv, "wr": wr, "wg": wg, "wo": wo,
                "wgu_g": wgu_g, "wgu_u": wgu_u, "wd": wd}

    in_maps = []
    for c in range(NCORES):
        base = c * TOWN
        if c == 0:
            halo_rows = np.concatenate([x[0:HALO - 1], x[1:2]], axis=0)
            mask = np.zeros((P, 1), np.float32)
        else:
            halo_rows = x[base - HALO: base]
            mask = np.ones((P, 1), np.float32)
        xa = np.concatenate([halo_rows, x[base: base + TOWN]], axis=0)
        m = {
            "x_h0": _x_fm(xa[0:SW]),
            "x_h1": _x_fm(xa[SW - 1: TH]),
            "x_res": _x_fm(xa[HALO: TH]),
            "mask": mask,
        }
        m.update(vec_common)
        m.update(w_common)
        in_maps.append(m)

    nc = build_program(TOWN)
    r = run_bass_kernel_spmd(nc, in_maps, core_ids=list(range(NCORES)))
    outs = []
    for c in range(NCORES):
        o = np.asarray(r.results[c]["out"])
        outs.append(o.transpose(1, 0, 2).reshape(C, TOWN).T)
    full = np.concatenate(outs, axis=0)[None]
    return full.astype(np.float32)


# version 28
# speedup vs baseline: 1.1394x; 1.0016x over previous
import numpy as np
import ml_dtypes
import concourse.bass as bass
import concourse.mybir as mybir
from concourse.alu_op_type import AluOpType as OP
from concourse.bass_utils import run_bass_kernel_spmd

F32 = mybir.dt.float32
BF16 = mybir.dt.bfloat16
AF = mybir.ActivationFunctionType

P = 128
CC = 16
C = 2048
NCORES = 8
HALO = 128
EPS = 1e-5
INTER = 8192

ENGINES = ("tensor", "vector", "scalar", "gpsimd", "sync")
OWN_SEM = {"tensor": "P", "vector": "V", "scalar": "A", "gpsimd": "G", "sync": None}


def segs(n, cap=512):
    out = []
    o = 0
    while o < n:
        s = min(cap, n - o)
        out.append((o, s))
        o += s
    return out


class Res:

    def __init__(self, name):
        self.name = name
        self.writer = None
        self.readers = []


class Sched:

    def __init__(self, nc):
        self.nc = nc
        self.ops = []
        self.last_eng = {}
        self.barrier_idx = None

    def barrier(self):
        deps = [i for i in self.last_eng.values()]
        idx = self.op("vector", lambda e: e.engine_nop(), waits_ops=deps)
        self.barrier_idx = idx
        return idx

    def op(self, eng, fns, reads=(), writes=(), waits_ops=(), dma=False,
           dma_sem=None):
        if not isinstance(fns, (list, tuple)):
            fns = [fns]
        idx = len(self.ops)
        deps = set(waits_ops)
        rnames = [r.name for r in reads]
        wnames = [r.name for r in writes]
        for r in reads:
            if r.writer is not None:
                deps.add(r.writer)
            r.readers.append(idx)
        for r in writes:
            if r.writer is not None:
                deps.add(r.writer)
            deps.update(r.readers)
            r.writer = idx
            r.readers = []
        if dma:
            sem = dma_sem or ("dG" if eng == "gpsimd" else "dS")
        else:
            sem = OWN_SEM[eng]
        self.ops.append(dict(eng=eng, fns=list(fns), sem=sem, dma=dma,
                             deps=deps, idx=idx, rnames=rnames, wnames=wnames))
        self.last_eng[eng] = idx
        return idx

    def emit(self, sems):
        ops = self.ops
        needed = set()
        for o in ops:
            own = OWN_SEM[o["eng"]]
            for d in o["deps"]:
                dop = ops[d]
                if (not dop["dma"]) and dop["sem"] is not None and dop["sem"] == own:
                    continue
                needed.add(d)
        inc_count = {}
        op_val = {}
        for o in ops:
            s = o["sem"]
            if o["dma"]:
                inc_count[s] = inc_count.get(s, 0) + 16
                op_val[o["idx"]] = (s, inc_count[s])
            elif o["idx"] in needed and s is not None:
                inc_count[s] = inc_count.get(s, 0) + 1
                op_val[o["idx"]] = (s, inc_count[s])
            else:
                op_val[o["idx"]] = None
        per_eng = {e: [] for e in ENGINES}
        hw = {e: {} for e in ENGINES}
        for o in ops:
            own = OWN_SEM[o["eng"]]
            wmap = {}
            for d in sorted(o["deps"]):
                dop = ops[d]
                if (not dop["dma"]) and dop["sem"] is not None and dop["sem"] == own:
                    continue
                v = op_val[d]
                if v is None:
                    raise RuntimeError("dep without inc")
                s, val = v
                if wmap.get(s, 0) < val:
                    wmap[s] = val
            waits = []
            for s, val in sorted(wmap.items()):
                if hw[o["eng"]].get(s, 0) >= val:
                    continue
                hw[o["eng"]][s] = val
                waits.append((s, val))
            inc = op_val[o["idx"]]
            myinc = None
            if o["dma"]:
                myinc = (inc[0], 16)
            elif inc is not None:
                myinc = (inc[0], 1)
            per_eng[o["eng"]].append((waits, o["fns"], myinc, o))
        self._final_counts = dict(inc_count)

        audit = getattr(self, "audit", None)

        def runner(items):
            def go(e):
                for waits, fns, myinc, o in items:
                    insts = []
                    for s, val in waits:
                        e.wait_ge(sems[s], val)
                    last = None
                    for fn in fns:
                        last = fn(e)
                        insts.append(last)
                    if myinc is not None:
                        last.then_inc(sems[myinc[0]], myinc[1])
                    if audit is not None:
                        o["_insts"] = insts
            return go

        with self.nc.Block() as blk:
            for eng in ENGINES:
                getattr(blk, eng)(runner(per_eng[eng]))


def build_program(TOWN=1024, dbg=False):
    TH = TOWN + HALO
    SW = TH // 2
    BW = SW + 1
    assert TH % 2 == 0 and SW > HALO
    OW0 = SW - HALO

    nc = bass.Bass("TRN2", target_bir_lowering=False,
                   detect_race_conditions=False)

    d_xh0 = nc.dram_tensor("x_h0", [P, CC, SW], F32, kind="ExternalInput")
    d_xh1 = nc.dram_tensor("x_h1", [P, CC, SW + 1], F32, kind="ExternalInput")
    d_xres = nc.dram_tensor("x_res", [P, CC, TOWN], F32, kind="ExternalInput")
    d_w = {}
    for nm in ("wk", "wv", "wr", "wg", "wo"):
        d_w[nm] = nc.dram_tensor(nm, [8, P, CC, 256], F32, kind="ExternalInput")
    d_wgu_g = nc.dram_tensor("wgu_g", [32, P, CC, 256], F32, kind="ExternalInput")
    d_wgu_u = nc.dram_tensor("wgu_u", [32, P, CC, 256], F32, kind="ExternalInput")
    d_wd = nc.dram_tensor("wd", [8, CC, P, 8, P], F32, kind="ExternalInput")
    d_vec = {}
    for nm in ("tmk", "tmv", "tmr", "tmg", "td", "tf", "lng", "lnb"):
        d_vec[nm] = nc.dram_tensor(nm, [P, CC], F32, kind="ExternalInput")
    d_mask = nc.dram_tensor("mask", [P, 1], F32, kind="ExternalInput")
    d_out = nc.dram_tensor("out", [P, CC, TOWN], F32, kind="ExternalOutput")

    from contextlib import ExitStack
    stack = ExitStack()
    sems = {}
    for s in ("dG", "dS", "dW", "P", "V", "A", "G"):
        sems[s] = stack.enter_context(nc.semaphore(s))

    sc = Sched(nc)

    def sb(name, shape, dt):
        return stack.enter_context(nc.sbuf_tensor(name, shape, dt))

    vecs = {nm: sb("sb_" + nm, [P, CC], F32) for nm in d_vec}
    maskt = sb("maskt", [P, 1], F32)
    wfm = sb("wfm", [P, CC], F32)
    euf = sb("euf", [P, CC], F32)
    e1f = sb("e1f", [P, CC], F32)
    onesm = sb("onesm", [P, 1], BF16)
    onesr = sb("onesr", [1, P], F32)
    stA = sb("stA", [P, CC], F32)
    stB = sb("stB", [P, CC], F32)
    ag = sb("ag", [P, CC * TOWN], BF16)
    SL = [sb("SL0", [P, CC * 256], BF16), sb("SL1", [P, CC * 256], BF16),
          sb("SL2", [P, CC * 256], BF16)]

    R = {}

    def res(name):
        if name not in R:
            R[name] = Res(name)
            R[name].writer = sc.barrier_idx
        return R[name]

    def fence():
        idx = sc.barrier()
        for r in R.values():
            r.writer = idx
            r.readers = []
        return idx

    def cres(buf, cc):
        return res("%s_%d" % (buf, cc))

    def all_res(buf):
        return [cres(buf, i) for i in range(CC)]

    def dma(eng, dst_ap, src_ap, reads=(), writes=(), waits=(), dma_sem=None):
        return sc.op(eng, lambda e, d=dst_ap, s=src_ap: e.dma_start(d, s),
                     reads=reads, writes=writes, waits_ops=waits, dma=True,
                     dma_sem=dma_sem)

    def act(out_ap, in_ap, func, bias=0.0, scale=1.0, reads=(), writes=(), waits=()):
        return sc.op("scalar",
                     lambda e, o=out_ap, i=in_ap, f=func, b=bias, s=scale:
                     e.activation(o, i, f, bias=b, scale=s),
                     reads=reads, writes=writes, waits_ops=waits)

    def vop(fn, reads=(), writes=(), waits=()):
        return sc.op("vector", fn, reads=reads, writes=writes, waits_ops=waits)

    slab_seq = [0]
    slab_pos = {}

    def load_slab_one(dram, key=None):
        k = key if key is not None else id(dram)
        s = slab_pos.get(k, 0)
        slab_pos[k] = s + 1
        par = slab_seq[0] % 3
        slab_seq[0] += 1
        t = dma("gpsimd",
                SL[par][:, :].rearrange("p (c w) -> p c w", w=256),
                dram.ap()[s],
                writes=[res("SL%d" % par)])
        return t, par

    for nm in d_vec:
        dma("sync", vecs[nm][:, :], d_vec[nm].ap(), writes=[res("vec_" + nm)])
    dma("sync", maskt[:, :], d_mask.ap(), writes=[res("maskt")])
    vop(lambda e: e.memset(onesm[:, :], 1.0 / 2048.0), writes=[res("onesm")])
    vop(lambda e: e.memset(onesr[:, :], 1.0), writes=[res("onesr")])
    act(e1f[:, :], vecs["td"][:, :], AF.Exp,
        reads=[res("vec_td")], writes=[res("e1f")])
    act(euf[:, :], vecs["tf"][:, :], AF.Exp,
        reads=[res("vec_tf")], writes=[res("euf")])
    vop(lambda e: e.memset(wfm[:, 0:1], 1.0), writes=[res("wfm")])
    for fn in (AF.Square, AF.Sigmoid, AF.Silu, AF.Ln, AF.Exp):
        act(wfm[:, 0:1], wfm[:, 0:1], fn, writes=[res("wfm")])
    act(wfm[:, :], e1f[:, :], AF.Exp, scale=-1.0,
        reads=[res("e1f")], writes=[res("wfm")])
    fence()

    with ExitStack() as hstack:
        def hsb(name, shape, dt):
            return hstack.enter_context(nc.sbuf_tensor(name, shape, dt))

        def hps(name, shape, dt=F32):
            return hstack.enter_context(nc.psum_tensor(name, shape, dt))

        Lb = hsb("Lb", [P, CC * BW], BF16)
        Eb = hsb("Eb", [P, CC * BW], BF16)
        Vb = hsb("Vb", [P, CC * BW], BF16)
        M1 = hsb("M1", [P, CC * BW], BF16)
        M2 = hsb("M2", [P, CC * BW], BF16)
        R1 = hsb("R1", [P, CC * BW], BF16)
        R3 = hsb("R3", [P, CC * BW], BF16)
        tmix = hsb("tmix", [P, BW], BF16)
        tf1 = [hsb("tf1a", [P, BW], F32), hsb("tf1b", [P, BW], F32)]
        tf2 = [hsb("tf2a", [P, BW], F32), hsb("tf2b", [P, BW], F32)]
        tf3 = hsb("tf3a", [P, BW], F32)
        mub_h = hsb("mub_h", [P, BW], F32)
        rsb_h = hsb("rsb_h", [P, BW], F32)
        hmu_row = hsb("hmu_row", [1, BW], F32)
        hs2_row = hsb("hs2_row", [1, BW], F32)
        hrs_row = hsb("hrs_row", [1, BW], F32)
        pp = [hps("pp0", [P, BW], F32), hps("pp1", [P, BW], F32)]
        ps_mu = hps("ps_mu", [1, BW], F32)
        ps_s2 = hps("ps_s2", [1, BW], F32)

        def kwarm(tok):
            sc.op("tensor", lambda e: e.matmul(pp[0][0:1, 0:1], onesm[:, 0:1],
                                               onesm[:, 0:1], start=True,
                                               stop=True), waits_ops=(tok,))

        def chunk(buf, cc, lo, n, stride=BW):
            return buf[:, cc * stride + lo: cc * stride + lo + n]

        def halo_view(buf):
            return buf[:, :].rearrange("p (c w) -> p c w", w=BW)[:, :, 0:HALO]

        def col_view(buf, col):
            return buf[:, :].rearrange("p (c w) -> p c w", w=BW)[:, :, col:col + 1]

        if dbg:
            for buf, bres in ((Lb, "Lb"), (Eb, "Eb"), (Vb, "Vb"),
                              (M1, "M1"), (M2, "M2"), (R1, "R1"), (R3, "R3")):
                vop(lambda e, buf=buf: e.memset(buf[:, :], 0.0),
                    writes=[res(bres)] + all_res(bres))

        def ln_stats_apply(ln_w, xbf_res, sq_res, xbuf, sqbuf, outbuf,
                           t_seg0=None):
            s0w = min(512, ln_w)
            xv0 = xbuf[:, :].rearrange("p (c w) -> p c w", w=BW)[:, :, 0:s0w]
            sv0 = sqbuf[:, :].rearrange("p (c w) -> p c w", w=BW)[:, :, 0:s0w]
            t_sq0 = act(sv0, xv0, AF.Square, writes=all_res(sq_res),
                        waits=(t_seg0,) if t_seg0 is not None else ())
            if ln_w > s0w:
                xv1 = xbuf[:, :].rearrange("p (c w) -> p c w",
                                           w=BW)[:, :, s0w:ln_w]
                sv1 = sqbuf[:, :].rearrange("p (c w) -> p c w",
                                            w=BW)[:, :, s0w:ln_w]
                act(sv1, xv1, AF.Square, reads=all_res(xbf_res),
                    writes=all_res(sq_res))
            for stat, (srcb, sres, psum, pres) in enumerate(
                    ((xbuf, xbf_res, ps_mu, "ps_mu"),
                     (sqbuf, sq_res, ps_s2, "ps_s2"))):
                for (o, n) in segs(ln_w, 512):
                    fns = []
                    for cci in range(CC):
                        fns.append(lambda e, srcb=srcb, psum=psum, o=o, n=n, cci=cci:
                                   e.matmul(psum[0:1, o:o + n], onesm[:, 0:1],
                                            chunk(srcb, cci, o, n),
                                            start=(cci == 0), stop=(cci == CC - 1)))
                    if o == 0 and stat == 0 and t_seg0 is not None:
                        rds, wts = [res("onesm")], (t_seg0,)
                    elif o == 0 and stat == 1:
                        rds, wts = [res("onesm")], (t_sq0,)
                    else:
                        rds, wts = all_res(sres) + [res("onesm")], ()
                    sc.op("tensor", fns, reads=rds, writes=[res(pres)],
                          waits_ops=wts)
            vop(lambda e: e.tensor_copy(hmu_row[0:1, 0:ln_w], ps_mu[0:1, 0:ln_w]),
                reads=[res("ps_mu")], writes=[res("hmu_row")])
            vop(lambda e: e.tensor_copy(hs2_row[0:1, 0:ln_w], ps_s2[0:1, 0:ln_w]),
                reads=[res("ps_s2")], writes=[res("hs2_row")])
            vop(lambda e: e.tensor_mul(hrs_row[0:1, 0:ln_w], hmu_row[0:1, 0:ln_w],
                                       hmu_row[0:1, 0:ln_w]),
                reads=[res("hmu_row")], writes=[res("hrs_row")])
            vop(lambda e: e.tensor_sub(hs2_row[0:1, 0:ln_w], hs2_row[0:1, 0:ln_w],
                                       hrs_row[0:1, 0:ln_w]),
                reads=[res("hrs_row")], writes=[res("hs2_row")])
            vop(lambda e: e.tensor_single_scalar(
                hs2_row[0:1, 0:ln_w], hs2_row[0:1, 0:ln_w], EPS, OP.add),
                writes=[res("hs2_row")])
            act(hs2_row[0:1, 0:ln_w], hs2_row[0:1, 0:ln_w], AF.Ln,
                writes=[res("hs2_row")])
            act(hrs_row[0:1, 0:ln_w], hs2_row[0:1, 0:ln_w], AF.Exp, scale=-0.5,
                reads=[res("hs2_row")], writes=[res("hrs_row")])
            for row, rres, dstb, dres, psel in (
                    (hmu_row, "hmu_row", mub_h, "mub_h", 0),
                    (hrs_row, "hrs_row", rsb_h, "rsb_h", 1)):
                fns = []
                for (o, n) in segs(ln_w, 512):
                    fns.append(lambda e, row=row, o=o, n=n, psel=psel:
                               e.matmul(pp[psel][0:128, o:o + n], onesr[0:1, 0:P],
                                        row[0:1, o:o + n], start=True, stop=True))
                sc.op("tensor", fns, reads=[res(rres), res("onesr")],
                      writes=[res("pp%d" % psel)])
                vop(lambda e, dstb=dstb, psel=psel, ln_w=ln_w:
                    e.tensor_copy(dstb[:, 0:ln_w], pp[psel][:, 0:ln_w]),
                    reads=[res("pp%d" % psel)], writes=[res(dres)])
            for cci in range(CC):
                vop(lambda e, cci=cci, ln_w=ln_w: e.tensor_sub(
                    tf1[0][:, 0:ln_w], chunk(xbuf, cci, 0, ln_w),
                    mub_h[:, 0:ln_w]),
                    reads=[cres(xbf_res, cci), res("mub_h")],
                    writes=[res("tf1_0")])
                t_ap = vop(lambda e, cci=cci, ln_w=ln_w: e.tensor_mul(
                    chunk(outbuf, cci, 0, ln_w), tf1[0][:, 0:ln_w],
                    rsb_h[:, 0:ln_w]),
                    reads=[res("tf1_0"), res("rsb_h")], writes=[res("Lb")])
                kwarm(t_ap)

        def mix_into(dstbuf, dres, tmname, lnoff, lo, n, h, kw=False):
            for cci in range(CC):
                if h == 0 and lo == 0:
                    vop(lambda e: e.memset(tmix[:, 0:1], 0.0),
                        writes=[res("tmix")])
                    vop(lambda e, cci=cci, n=n: e.tensor_sub(
                        tmix[:, 1:n], chunk(Lb, cci, 0, n - 1),
                        chunk(Lb, cci, 1, n - 1)),
                        reads=[res("Lb")], writes=[res("tmix")])
                else:
                    off = lnoff + lo
                    vop(lambda e, cci=cci, off=off, n=n: e.tensor_sub(
                        tmix[:, 0:n], chunk(Lb, cci, off - 1, n),
                        chunk(Lb, cci, off, n)),
                        reads=[res("Lb")], writes=[res("tmix")])
                t_st = vop(lambda e, cci=cci, tmname=tmname, dstbuf=dstbuf,
                           off=lnoff + lo, n=n:
                           e.scalar_tensor_tensor(
                               chunk(dstbuf, cci, 0, n), tmix[:, 0:n],
                               vecs[tmname][:, cci:cci + 1],
                               chunk(Lb, cci, off, n), OP.mult, OP.add),
                           reads=[res("tmix"), res("vec_" + tmname),
                                  res("Lb")],
                           writes=[cres(dres, cci)])
                if kw:
                    kwarm(t_st)

        def projection(wname, mixbuf, mres, width, evac_fn):
            for s in range(8):
                t_slab, par = load_slab_one(d_w[wname], key=wname)
                for osub in range(2):
                    oc = s * 2 + osub
                    psel = oc % 2
                    fns = []
                    for cci in range(CC):
                        for (o, n) in segs(width, 512):
                            fns.append(lambda e, par=par, cci=cci, osub=osub,
                                       o=o, n=n, psel=psel, mixbuf=mixbuf:
                                       e.matmul(pp[psel][:, o:o + n],
                                                SL[par][:, cci * 256 + osub * P:
                                                        cci * 256 + osub * P + P],
                                                chunk(mixbuf, cci, o, n),
                                                start=(cci == 0),
                                                stop=(cci == CC - 1)))
                    sc.op("tensor", fns,
                          reads=[res("SL%d" % par)] + all_res(mres),
                          writes=[res("pp%d" % psel)], waits_ops=(t_slab,))
                    evac_fn(oc, psel)

        for h in (0, 1):
            for wnm in ("wk", "wv", "wr", "wg"):
                slab_pos.pop(wnm, None)
            ln_w = SW if h == 0 else SW + 1
            lnoff = 0 if h == 0 else 1
            lo = HALO if h == 0 else 0
            now = SW - lo
            gcol0 = 0 if h == 0 else OW0
            d_x = d_xh0 if h == 0 else d_xh1

            s0w = min(512, ln_w)
            t_x1 = dma("gpsimd",
                       Eb[:, :].rearrange("p (c w) -> p c w",
                                          w=BW)[:, :, 0:s0w],
                       d_x.ap()[:, :, 0:s0w], writes=all_res("Eb"))
            if ln_w > s0w:
                dma("gpsimd",
                    Eb[:, :].rearrange("p (c w) -> p c w",
                                       w=BW)[:, :, s0w:ln_w],
                    d_x.ap()[:, :, s0w:ln_w], writes=all_res("Eb"))
            ln_stats_apply(ln_w, "Eb", "Vb", Eb, Vb, Lb, t_x1)

            mix_into(M1, "M1", "tmk", lnoff, 0, SW, h, kw=True)
            mix_into(M2, "M2", "tmv", lnoff, 0, SW, h)
            mix_into(R1, "R1", "tmr", lnoff, lo, now, h if lo == 0 else 1)

            projection("wk", M1, "M1", SW,
                       lambda oc, psel: act(chunk(Eb, oc, 0, SW),
                                            pp[psel][:, 0:SW], AF.Exp,
                                            reads=[res("pp%d" % psel)],
                                            writes=[cres("Eb", oc)]))
            projection("wv", M2, "M2", SW,
                       lambda oc, psel: sc.op(
                           "scalar",
                           lambda e, oc=oc, psel=psel:
                           e.copy(chunk(Vb, oc, 0, SW), pp[psel][:, 0:SW]),
                           reads=[res("pp%d" % psel)], writes=[cres("Vb", oc)]))

            projection("wr", R1, "R1", now,
                       lambda oc, psel: act(chunk(R3, oc, 0, now),
                                            pp[psel][:, 0:now], AF.Sigmoid,
                                            reads=[res("pp%d" % psel)],
                                            writes=[cres("R3", oc)]))

            if h == 0:
                for cci in range(CC):
                    vop(lambda e, cci=cci: e.tensor_single_scalar(
                        Eb[:, cci * BW: cci * BW + HALO],
                        Eb[:, cci * BW: cci * BW + HALO],
                        maskt[:, 0:1], OP.mult),
                        reads=[res("maskt")], writes=[cres("Eb", cci)])

            pend_qa = []
            for cci in range(CC):
                vop(lambda e, cci=cci: e.tensor_mul(
                    chunk(M1, cci, 0, SW), chunk(Eb, cci, 0, SW),
                    chunk(Vb, cci, 0, SW)),
                    reads=[cres("Eb", cci), cres("Vb", cci)],
                    writes=[cres("M1", cci)])
                init_a = 0.0 if h == 0 else stA[:, cci:cci + 1]
                wb = wfm[:, cci:cci + 1].broadcast_to([P, SW])
                pend_qa.append(lambda e=None, cci=cci, init_a=init_a, wb=wb:
                               vop(lambda e, cci=cci, init_a=init_a, wb=wb:
                                   e.tensor_tensor_scan(
                                   chunk(Vb, cci, 0, SW), wb,
                                   chunk(M1, cci, 0, SW), init_a,
                                   OP.mult, OP.add),
                                   reads=[res("wfm"), cres("M1", cci),
                                          res("stA")],
                                   writes=[cres("Vb", cci)]))
                if len(pend_qa) >= 2:
                    pend_qa.pop(0)()

            def scan_pair(cci):
                init_b = 0.0 if h == 0 else stB[:, cci:cci + 1]
                wb = wfm[:, cci:cci + 1].broadcast_to([P, SW])
                vop(lambda e, cci=cci, init_b=init_b, wb=wb:
                    e.tensor_tensor_scan(
                    chunk(M2, cci, 0, SW), wb,
                    chunk(Eb, cci, 0, SW), init_b, OP.mult, OP.add),
                    reads=[res("wfm"), cres("Eb", cci), res("stB")],
                    writes=[cres("M2", cci)])

            def epilogue(cci):
                par = cci % 2
                if lo == 0:
                    vop(lambda e, cci=cci, par=par: e.scalar_tensor_tensor(
                        tf1[par][:, 0:1], M1[:, cci * BW: cci * BW + 1],
                        euf[:, cci:cci + 1], stA[:, cci:cci + 1],
                        OP.mult, OP.add),
                        reads=[cres("M1", cci), res("euf"), res("stA")],
                        writes=[res("tf1_%d" % par)])
                    vop(lambda e, cci=cci, par=par: e.scalar_tensor_tensor(
                        tf2[par][:, 0:1], Eb[:, cci * BW: cci * BW + 1],
                        euf[:, cci:cci + 1], stB[:, cci:cci + 1],
                        OP.mult, OP.add),
                        reads=[cres("Eb", cci), res("euf"), res("stB")],
                        writes=[res("tf2_%d" % par)])
                    o1, n1 = 1, now - 1
                else:
                    o1, n1 = 0, now
                vop(lambda e, cci=cci, par=par, lo=lo, o1=o1, n1=n1:
                    e.scalar_tensor_tensor(
                    tf1[par][:, o1:o1 + n1], chunk(M1, cci, lo + o1, n1),
                    euf[:, cci:cci + 1], chunk(Vb, cci, lo + o1 - 1, n1),
                    OP.mult, OP.add),
                    reads=[cres("M1", cci), cres("Vb", cci), res("euf")],
                    writes=[res("tf1_%d" % par)])
                vop(lambda e, cci=cci, par=par, lo=lo, o1=o1, n1=n1:
                    e.scalar_tensor_tensor(
                    tf2[par][:, o1:o1 + n1], chunk(Eb, cci, lo + o1, n1),
                    euf[:, cci:cci + 1], chunk(M2, cci, lo + o1 - 1, n1),
                    OP.mult, OP.add),
                    reads=[cres("Eb", cci), cres("M2", cci), res("euf")],
                    writes=[res("tf2_%d" % par)])
                act(tf3[:, 0:now], tf2[par][:, 0:now], AF.Ln,
                    reads=[res("tf2_%d" % par)], writes=[res("tf3")])
                act(tf2[par][:, 0:now], tf3[:, 0:now], AF.Exp, scale=-1.0,
                    reads=[res("tf3")], writes=[res("tf2_%d" % par)])

            def wkv_mul(cci):
                par = cci % 2
                vop(lambda e, cci=cci, par=par, lo=lo, now=now: e.tensor_mul(
                    chunk(M1, cci, lo, now), tf1[par][:, 0:now],
                    tf2[par][:, 0:now]),
                    reads=[res("tf1_%d" % par), res("tf2_%d" % par)],
                    writes=[cres("M1", cci)])

            for f in pend_qa:
                f()
            pend_qa = []
            for cci in range(CC):
                scan_pair(cci)
            if h == 0:
                vop(lambda e: e.tensor_copy(
                    stA[:, :],
                    Vb[:, :].rearrange("p (c w) -> p c w", w=BW)[:, :, SW - 1:SW]),
                    reads=all_res("Vb"), writes=[res("stA")])
                vop(lambda e: e.tensor_copy(
                    stB[:, :],
                    M2[:, :].rearrange("p (c w) -> p c w", w=BW)[:, :, SW - 1:SW]),
                    reads=all_res("M2"), writes=[res("stB")])
            pend = []
            for cci in range(CC):
                epilogue(cci)
                for f in pend:
                    f()
                pend = [lambda cci=cci: wkv_mul(cci)]
            for f in pend:
                f()
            mix_into(R1, "R1", "tmg", lnoff, lo, now, h if lo == 0 else 1,
                     kw=True)

            def g_evac(oc, psel, lo=lo, now=now, gcol0=gcol0):
                tpar = 0
                act(tmix[:, 0:now], pp[psel][:, 0:now], AF.Sigmoid,
                    reads=[res("pp%d" % psel)], writes=[res("tmix")])
                vop(lambda e, oc=oc, tpar=tpar, lo=lo, now=now: e.tensor_mul(
                    tf1[tpar][:, 0:now], tmix[:, 0:now],
                    chunk(M1, oc, lo, now)),
                    reads=[res("tmix"), cres("M1", oc)],
                    writes=[res("tf1_%d" % tpar)])
                vop(lambda e, oc=oc, tpar=tpar, gcol0=gcol0, now=now:
                    e.tensor_mul(
                    ag[:, oc * TOWN + gcol0: oc * TOWN + gcol0 + now],
                    tf1[tpar][:, 0:now], chunk(R3, oc, 0, now)),
                    reads=[res("tf1_%d" % tpar), cres("R3", oc)],
                    writes=[res("ag")])
            projection("wg", R1, "R1", now, g_evac)

        if dbg:
            for nm, buf, wdt in (("dbg_ag", ag, CC * TOWN),
                                 ("dbg_M1", M1, CC * BW),
                                 ("dbg_R3", R3, CC * BW),
                                 ("dbg_Lb", Lb, CC * BW),
                                 ("dbg_R1", R1, CC * BW),
                                 ("dbg_M2", M2, CC * BW)):
                dd = nc.dram_tensor(nm, [P, wdt], BF16, kind="ExternalOutput")
                bn = nm.replace("dbg_", "") if nm != "dbg_ag" else "ag"
                dma("sync", dd.ap(), buf[:, :],
                    reads=([res("ag")] if bn == "ag" else all_res(bn)))
        wo_pre = [load_slab_one(d_w["wo"], key="wo") for _ in range(2)]
    fence()

    with ExitStack() as pstack:
        def psb(name, shape, dt):
            return pstack.enter_context(nc.sbuf_tensor(name, shape, dt))

        acc = psb("acc", [P, CC * TOWN], F32)
        tp1 = psb("tp1", [P, TOWN], F32)
        tbf = psb("tbf", [P, TOWN], BF16)
        mu_b = psb("mu_b", [P, TOWN], F32)
        rs_b = psb("rs_b", [P, TOWN], F32)
        mu_row = psb("mu_row2", [1, TOWN], F32)
        s2_row = psb("s2_row2", [1, TOWN], F32)
        rs_row = psb("rs_row2", [1, TOWN], F32)
        wstack = ExitStack()
        attn_o = wstack.enter_context(nc.sbuf_tensor("attn_o", [P, CC * TOWN], BF16))
        qpsum = ExitStack()
        po = [qpsum.enter_context(nc.psum_tensor("po0", [P, TOWN], F32)),
              qpsum.enter_context(nc.psum_tensor("po1", [P, TOWN], F32))]
        ps2_mu = qpsum.enter_context(nc.psum_tensor("ps2_mu", [1, TOWN], F32))
        ps2_s2 = qpsum.enter_context(nc.psum_tensor("ps2_s2", [1, TOWN], F32))

        dma("sync", acc[:, :].rearrange("p (c w) -> p c w", w=TOWN),
            d_xres.ap(), writes=[res("acc")])

        def stream_stats(oc, srcbuf, srcres, start):
            vop(lambda e, oc=oc, srcbuf=srcbuf: e.tensor_copy(
                tbf[:, :], srcbuf[:, oc * TOWN:(oc + 1) * TOWN]),
                reads=[res(srcres)], writes=[res("tbf")])
            for stepi in range(2):
                psum, pres = ((ps2_mu, "ps2_mu"), (ps2_s2, "ps2_s2"))[stepi]
                if stepi == 1:
                    act(tbf[:, :], tbf[:, :], AF.Square, writes=[res("tbf")])
                fns = []
                for (o, n) in segs(TOWN, 512):
                    fns.append(lambda e, psum=psum, o=o, n=n:
                               e.matmul(psum[0:1, o:o + n], onesm[:, 0:1],
                                        tbf[:, o:o + n],
                                        start=start, stop=(oc == CC - 1)))
                sc.op("tensor", fns, reads=[res("tbf"), res("onesm")],
                      writes=[] if not start else [res(pres)],
                      waits_ops=(R[pres].writer,) if (start and R[pres].writer
                                                      is not None) else ())
                if not start:
                    R[pres].writer = sc.ops[-1]["idx"]

        def finalize_rows_and_bcast():
            vop(lambda e: e.tensor_copy(mu_row[0:1, 0:TOWN], ps2_mu[0:1, 0:TOWN]),
                reads=[res("ps2_mu")], writes=[res("mu_row")])
            vop(lambda e: e.tensor_copy(s2_row[0:1, 0:TOWN], ps2_s2[0:1, 0:TOWN]),
                reads=[res("ps2_s2")], writes=[res("s2_row")])
            vop(lambda e: e.tensor_mul(rs_row[0:1, 0:TOWN], mu_row[0:1, 0:TOWN],
                                       mu_row[0:1, 0:TOWN]),
                reads=[res("mu_row")], writes=[res("rs_row")])
            vop(lambda e: e.tensor_sub(s2_row[0:1, 0:TOWN], s2_row[0:1, 0:TOWN],
                                       rs_row[0:1, 0:TOWN]),
                reads=[res("rs_row")], writes=[res("s2_row")])
            vop(lambda e: e.tensor_single_scalar(
                s2_row[0:1, 0:TOWN], s2_row[0:1, 0:TOWN], EPS, OP.add),
                writes=[res("s2_row")])
            act(s2_row[0:1, 0:TOWN], s2_row[0:1, 0:TOWN], AF.Ln,
                writes=[res("s2_row")])
            act(rs_row[0:1, 0:TOWN], s2_row[0:1, 0:TOWN], AF.Exp, scale=-0.5,
                reads=[res("s2_row")], writes=[res("rs_row")])
            for row, rres, dstb, dres, psel in (
                    (mu_row, "mu_row", mu_b, "mu_b", 0),
                    (rs_row, "rs_row", rs_b, "rs_b", 1)):
                fns = []
                for (o, n) in segs(TOWN, 512):
                    fns.append(lambda e, row=row, o=o, n=n, psel=psel:
                               e.matmul(po[psel][0:128, o:o + n], onesr[0:1, 0:P],
                                        row[0:1, o:o + n], start=True, stop=True))
                sc.op("tensor", fns, reads=[res(rres), res("onesr")],
                      writes=[res("po%d" % psel)])
                vop(lambda e, dstb=dstb, psel=psel:
                    e.tensor_copy(dstb[:, 0:TOWN], po[psel][:, 0:TOWN]),
                    reads=[res("po%d" % psel)], writes=[res(dres)])

        pend_stats = []
        for s in range(8):
            t_slab, par = (wo_pre[s] if s < 2 else
                           load_slab_one(d_w["wo"], key="wo"))
            for osub in range(2):
                oc = s * 2 + osub
                psel = oc % 2
                fns = []
                for cci in range(CC):
                    for (o, n) in segs(TOWN, 512):
                        fns.append(lambda e, par=par, cci=cci, osub=osub, o=o,
                                   n=n, psel=psel:
                                   e.matmul(po[psel][:, o:o + n],
                                            SL[par][:, cci * 256 + osub * P:
                                                    cci * 256 + osub * P + P],
                                            ag[:, cci * TOWN + o:
                                               cci * TOWN + o + n],
                                            start=(cci == 0),
                                            stop=(cci == CC - 1)))
                sc.op("tensor", fns, reads=[res("SL%d" % par), res("ag")],
                      writes=[res("po%d" % psel)], waits_ops=(t_slab,))
                sc.op("scalar", lambda e, oc=oc, psel=psel:
                      e.copy(attn_o[:, oc * TOWN:(oc + 1) * TOWN],
                             po[psel][:, 0:TOWN]),
                      reads=[res("po%d" % psel)], writes=[res("attn_o")])
                if len(pend_stats) >= 2:
                    pend_stats.pop(0)()
                pend_stats.append(lambda oc=oc: stream_stats(
                    oc, attn_o, "attn_o", start=(oc == 0)))
        for f in pend_stats:
            f()
        finalize_rows_and_bcast()
        for cci in range(CC):
            sc.op("gpsimd", lambda e, cci=cci: e.tensor_sub(
                tp1[:, :], attn_o[:, cci * TOWN:(cci + 1) * TOWN], mu_b[:, :]),
                reads=[res("attn_o"), res("mu_b")], writes=[res("tp1")])
            vop(lambda e: e.tensor_mul(tp1[:, :], tp1[:, :], rs_b[:, :]),
                reads=[res("rs_b")], writes=[res("tp1")])
            vop(lambda e, cci=cci: e.tensor_scalar(
                tp1[:, :], tp1[:, :], vecs["lng"][:, cci:cci + 1],
                vecs["lnb"][:, cci:cci + 1], OP.mult, OP.add),
                reads=[res("vec_lng"), res("vec_lnb")], writes=[res("tp1")])
            vop(lambda e, cci=cci: e.tensor_add(
                acc[:, cci * TOWN:(cci + 1) * TOWN],
                acc[:, cci * TOWN:(cci + 1) * TOWN], tp1[:, :]),
                reads=[res("tp1")], writes=[res("acc")])
            stream_stats(cci, acc, "acc", start=(cci == 0))

        if dbg:
            dd = nc.dram_tensor("dbg_attn_o", [P, CC * TOWN], BF16,
                                kind="ExternalOutput")
            dma("sync", dd.ap(), attn_o[:, :], reads=[res("attn_o")])
        wstack.close()
        fence()
        ln2 = psb("ln2", [P, CC * TOWN], BF16)
        ffn_pre = {"g": load_slab_one(d_wgu_g, key="g"),
                   "u": load_slab_one(d_wgu_u, key="u")}
        finalize_rows_and_bcast()
        for cci in range(CC):
            sc.op("gpsimd", lambda e, cci=cci: e.tensor_sub(
                tp1[:, :], acc[:, cci * TOWN:(cci + 1) * TOWN], mu_b[:, :]),
                reads=[res("acc"), res("mu_b")], writes=[res("tp1")])
            t_l2 = vop(lambda e, cci=cci: e.tensor_mul(
                ln2[:, cci * TOWN:(cci + 1) * TOWN], tp1[:, :], rs_b[:, :]),
                reads=[res("tp1"), res("rs_b")], writes=[res("ln2")])
            sc.op("tensor", lambda e: e.matmul(po[0][0:1, 0:1], onesm[:, 0:1],
                                               onesm[:, 0:1], start=True,
                                               stop=True), waits_ops=(t_l2,))
        if dbg:
            dd = nc.dram_tensor("dbg_ln2", [P, CC * TOWN], BF16,
                                kind="ExternalOutput")
            dma("sync", dd.ap(), ln2[:, :], reads=[res("ln2")])
        qpsum.close()

        with ExitStack() as fstack:
            def fsb(name, shape, dt):
                return fstack.enter_context(nc.sbuf_tensor(name, shape, dt))

            def fps(name, shape, dt=F32):
                return fstack.enter_context(nc.psum_tensor(name, shape, dt))

            hs = [ag[:, 0:8 * TOWN], ag[:, 8 * TOWN:16 * TOWN]]
            wdb = [fsb("wdb0", [P, 8 * P], BF16), fsb("wdb1", [P, 8 * P], BF16)]
            th2 = [fsb("th0", [P, TOWN], BF16), fsb("th1", [P, TOWN], BF16)]
            pg = fps("pg", [P, TOWN], F32)
            pu = fps("pu", [P, TOWN], F32)
            pd = [fps("pd0", [P, TOWN], F32), fps("pd1", [P, TOWN], F32)]
            def emit_down(s, hc):
                hpar = s % 2
                wpar = hc % 2
                dma("gpsimd",
                    wdb[wpar][:, :].rearrange("p (i w) -> p i w", w=P),
                    d_wd.ap()[s][hc], writes=[res("wdb%d" % wpar)])
                fns = []
                for ic in range(8):
                    for (o, n) in segs(TOWN, 512):
                        fns.append(lambda e, wpar=wpar, ic=ic, o=o, n=n,
                                   hpar=hpar:
                                   e.matmul(pd[wpar][:, o:o + n],
                                            wdb[wpar][:, ic * P:(ic + 1) * P],
                                            hs[hpar][:, ic * TOWN + o:
                                                     ic * TOWN + o + n],
                                            start=(ic == 0), stop=(ic == 7)))
                sc.op("tensor", fns,
                      reads=[res("wdb%d" % wpar), res("hs%d" % hpar)],
                      writes=[res("pd%d" % wpar)])
                vop(lambda e, hc=hc, wpar=wpar: e.tensor_add(
                    acc[:, hc * TOWN:(hc + 1) * TOWN],
                    acc[:, hc * TOWN:(hc + 1) * TOWN], pd[wpar][:, :]),
                    reads=[res("pd%d" % wpar)], writes=[res("acc")])

            down_pend = []
            for s in range(8):
                hpar = s % 2
                for ic in range(8):
                    gic = 8 * s + ic
                    slab = gic // 2
                    within = gic % 2
                    if within == 0:
                        if slab == 0:
                            sc._ffn_cur = ffn_pre
                        else:
                            tg_, parg = load_slab_one(d_wgu_g, key="g")
                            tu_, paru = load_slab_one(d_wgu_u, key="u")
                            sc._ffn_cur = {"g": (tg_, parg), "u": (tu_, paru)}
                    cur = sc._ffn_cur
                    for which in ("g", "u"):
                        t_slab, par = cur[which]
                        psum, pres = (pg, "pg") if which == "g" else (pu, "pu")
                        fns = []
                        for cci in range(CC):
                            for (o, n) in segs(TOWN, 512):
                                fns.append(lambda e, par=par, cci=cci,
                                           within=within, o=o, n=n, psum=psum:
                                           e.matmul(psum[:, o:o + n],
                                                    SL[par][:, cci * 256
                                                            + within * P:
                                                            cci * 256
                                                            + within * P + P],
                                                    ln2[:, cci * TOWN + o:
                                                        cci * TOWN + o + n],
                                                    start=(cci == 0),
                                                    stop=(cci == CC - 1)))
                        sc.op("tensor", fns,
                              reads=[res("SL%d" % par), res("ln2")],
                              writes=[res(pres)], waits_ops=(t_slab,))
                    tpar = ic % 2
                    act(th2[tpar][:, :], pg[:, :], AF.Silu,
                        reads=[res("pg")], writes=[res("th_%d" % tpar)])
                    vop(lambda e, hpar=hpar, ic=ic, tpar=tpar: e.tensor_mul(
                        hs[hpar][:, ic * TOWN:(ic + 1) * TOWN],
                        th2[tpar][:, :], pu[:, :]),
                        reads=[res("th_%d" % tpar), res("pu")],
                        writes=[res("hs%d" % hpar)])
                    for _ in range(2):
                        if down_pend:
                            emit_down(*down_pend.pop(0))
                down_pend.extend((s, hc) for hc in range(CC))
            for s_hc in down_pend:
                emit_down(*s_hc)

        t_out = dma("sync", d_out.ap(),
                    acc[:, :].rearrange("p (c w) -> p c w", w=TOWN),
                    reads=[res("acc")])
        sc.op("sync", lambda e: e.nop(), waits_ops=(t_out,))

    sc.emit(sems)
    nc.finalize()
    stack.close()
    return nc



def _swizzle_w(wT, oslab):
    cin, o = wT.shape
    return np.ascontiguousarray(
        wT.reshape(CC, P, o // oslab, oslab).transpose(2, 1, 0, 3))


def _vec_fm(v):
    return np.ascontiguousarray(v.reshape(CC, P).T)


def _x_fm(xs):
    t = xs.shape[0]
    return np.ascontiguousarray(xs.T.reshape(CC, P, t).transpose(1, 0, 2))


def kernel(L0_attn_x, L0_attn_time_decay, L0_attn_time_first, L0_attn_tm_k,
           L0_attn_tm_v, L0_attn_tm_r, L0_attn_tm_g, L0_attn_Wr, L0_attn_Wk,
           L0_attn_Wv, L0_attn_Wg, L0_attn_Wo, L0_attn_ln_g, L0_attn_ln_b,
           L0_ffn_w_gate_up, L0_ffn_w_down):
    TOWN = 1024
    TH = TOWN + HALO
    SW = TH // 2

    x = np.asarray(L0_attn_x, dtype=np.float32)[0]
    T = x.shape[0]
    assert T == NCORES * TOWN

    wk = _swizzle_w(np.ascontiguousarray(np.asarray(L0_attn_Wk, np.float32).T), 256)
    wv = _swizzle_w(np.ascontiguousarray(np.asarray(L0_attn_Wv, np.float32).T), 256)
    wr = _swizzle_w(np.ascontiguousarray(np.asarray(L0_attn_Wr, np.float32).T), 256)
    wg = _swizzle_w(np.ascontiguousarray(np.asarray(L0_attn_Wg, np.float32).T), 256)
    wo = _swizzle_w(np.ascontiguousarray(np.asarray(L0_attn_Wo, np.float32).T), 256)
    wguT = np.ascontiguousarray(np.asarray(L0_ffn_w_gate_up, np.float32).T)
    wgu_g = _swizzle_w(np.ascontiguousarray(wguT[:, 0:INTER]), 256)
    wgu_u = _swizzle_w(np.ascontiguousarray(wguT[:, INTER:]), 256)
    wdT = np.ascontiguousarray(np.asarray(L0_ffn_w_down, np.float32).T)
    wd = np.ascontiguousarray(
        wdT.reshape(8, 8, P, CC, P).transpose(0, 3, 2, 1, 4))

    vec_common = {
        "tmk": _vec_fm(np.asarray(L0_attn_tm_k, np.float32)),
        "tmv": _vec_fm(np.asarray(L0_attn_tm_v, np.float32)),
        "tmr": _vec_fm(np.asarray(L0_attn_tm_r, np.float32)),
        "tmg": _vec_fm(np.asarray(L0_attn_tm_g, np.float32)),
        "td": _vec_fm(np.asarray(L0_attn_time_decay, np.float32)),
        "tf": _vec_fm(np.asarray(L0_attn_time_first, np.float32)),
        "lng": _vec_fm(np.asarray(L0_attn_ln_g, np.float32)),
        "lnb": _vec_fm(np.asarray(L0_attn_ln_b, np.float32)),
    }
    w_common = {"wk": wk, "wv": wv, "wr": wr, "wg": wg, "wo": wo,
                "wgu_g": wgu_g, "wgu_u": wgu_u, "wd": wd}

    in_maps = []
    for c in range(NCORES):
        base = c * TOWN
        if c == 0:
            halo_rows = np.concatenate([x[0:HALO - 1], x[1:2]], axis=0)
            mask = np.zeros((P, 1), np.float32)
        else:
            halo_rows = x[base - HALO: base]
            mask = np.ones((P, 1), np.float32)
        xa = np.concatenate([halo_rows, x[base: base + TOWN]], axis=0)
        m = {
            "x_h0": _x_fm(xa[0:SW]),
            "x_h1": _x_fm(xa[SW - 1: TH]),
            "x_res": _x_fm(xa[HALO: TH]),
            "mask": mask,
        }
        m.update(vec_common)
        m.update(w_common)
        in_maps.append(m)

    nc = build_program(TOWN)
    r = run_bass_kernel_spmd(nc, in_maps, core_ids=list(range(NCORES)))
    outs = []
    for c in range(NCORES):
        o = np.asarray(r.results[c]["out"])
        outs.append(o.transpose(1, 0, 2).reshape(C, TOWN).T)
    full = np.concatenate(outs, axis=0)[None]
    return full.astype(np.float32)


# version 30
# speedup vs baseline: 1.1480x; 1.0075x over previous
import numpy as np
import ml_dtypes
import concourse.bass as bass
import concourse.mybir as mybir
from concourse.alu_op_type import AluOpType as OP
from concourse.bass_utils import run_bass_kernel_spmd

F32 = mybir.dt.float32
BF16 = mybir.dt.bfloat16
AF = mybir.ActivationFunctionType

P = 128
CC = 16
C = 2048
NCORES = 8
HALO = 128
EPS = 1e-5
INTER = 8192

ENGINES = ("tensor", "vector", "scalar", "gpsimd", "sync")
OWN_SEM = {"tensor": "P", "vector": "V", "scalar": "A", "gpsimd": "G", "sync": None}


def segs(n, cap=512):
    out = []
    o = 0
    while o < n:
        s = min(cap, n - o)
        out.append((o, s))
        o += s
    return out


class Res:

    def __init__(self, name):
        self.name = name
        self.writer = None
        self.readers = []


class Sched:

    def __init__(self, nc):
        self.nc = nc
        self.ops = []
        self.last_eng = {}
        self.barrier_idx = None

    def barrier(self):
        deps = [i for i in self.last_eng.values()]
        idx = self.op("vector", lambda e: e.engine_nop(), waits_ops=deps)
        self.barrier_idx = idx
        return idx

    def op(self, eng, fns, reads=(), writes=(), waits_ops=(), dma=False,
           dma_sem=None):
        if not isinstance(fns, (list, tuple)):
            fns = [fns]
        idx = len(self.ops)
        deps = set(waits_ops)
        rnames = [r.name for r in reads]
        wnames = [r.name for r in writes]
        for r in reads:
            if r.writer is not None:
                deps.add(r.writer)
            r.readers.append(idx)
        for r in writes:
            if r.writer is not None:
                deps.add(r.writer)
            deps.update(r.readers)
            r.writer = idx
            r.readers = []
        if dma:
            sem = dma_sem or ("dG" if eng == "gpsimd" else "dS")
        else:
            sem = OWN_SEM[eng]
        self.ops.append(dict(eng=eng, fns=list(fns), sem=sem, dma=dma,
                             deps=deps, idx=idx, rnames=rnames, wnames=wnames))
        self.last_eng[eng] = idx
        return idx

    def emit(self, sems):
        ops = self.ops
        needed = set()
        for o in ops:
            own = OWN_SEM[o["eng"]]
            for d in o["deps"]:
                dop = ops[d]
                if (not dop["dma"]) and dop["sem"] is not None and dop["sem"] == own:
                    continue
                needed.add(d)
        inc_count = {}
        op_val = {}
        for o in ops:
            s = o["sem"]
            if o["dma"]:
                inc_count[s] = inc_count.get(s, 0) + 16
                op_val[o["idx"]] = (s, inc_count[s])
            elif o["idx"] in needed and s is not None:
                inc_count[s] = inc_count.get(s, 0) + 1
                op_val[o["idx"]] = (s, inc_count[s])
            else:
                op_val[o["idx"]] = None
        per_eng = {e: [] for e in ENGINES}
        hw = {e: {} for e in ENGINES}
        for o in ops:
            own = OWN_SEM[o["eng"]]
            wmap = {}
            for d in sorted(o["deps"]):
                dop = ops[d]
                if (not dop["dma"]) and dop["sem"] is not None and dop["sem"] == own:
                    continue
                v = op_val[d]
                if v is None:
                    raise RuntimeError("dep without inc")
                s, val = v
                if wmap.get(s, 0) < val:
                    wmap[s] = val
            waits = []
            for s, val in sorted(wmap.items()):
                if hw[o["eng"]].get(s, 0) >= val:
                    continue
                hw[o["eng"]][s] = val
                waits.append((s, val))
            inc = op_val[o["idx"]]
            myinc = None
            if o["dma"]:
                myinc = (inc[0], 16)
            elif inc is not None:
                myinc = (inc[0], 1)
            per_eng[o["eng"]].append((waits, o["fns"], myinc, o))
        self._final_counts = dict(inc_count)

        audit = getattr(self, "audit", None)

        def runner(items):
            def go(e):
                for waits, fns, myinc, o in items:
                    insts = []
                    for s, val in waits:
                        e.wait_ge(sems[s], val)
                    last = None
                    for fn in fns:
                        last = fn(e)
                        insts.append(last)
                    if myinc is not None:
                        last.then_inc(sems[myinc[0]], myinc[1])
                    if audit is not None:
                        o["_insts"] = insts
            return go

        with self.nc.Block() as blk:
            for eng in ENGINES:
                getattr(blk, eng)(runner(per_eng[eng]))


def build_program(TOWN=1024, dbg=False):
    TH = TOWN + HALO
    SW = TH // 2
    BW = SW + 1
    assert TH % 2 == 0 and SW > HALO
    OW0 = SW - HALO

    nc = bass.Bass("TRN2", target_bir_lowering=False,
                   detect_race_conditions=False)

    d_xh0 = nc.dram_tensor("x_h0", [P, CC, SW], F32, kind="ExternalInput")
    d_xh1 = nc.dram_tensor("x_h1", [P, CC, SW + 1], F32, kind="ExternalInput")
    d_xres = nc.dram_tensor("x_res", [P, CC, TOWN], F32, kind="ExternalInput")
    d_w = {}
    for nm in ("wk", "wv", "wr", "wg", "wo"):
        d_w[nm] = nc.dram_tensor(nm, [8, P, CC, 256], F32, kind="ExternalInput")
    d_wgu_g = nc.dram_tensor("wgu_g", [32, P, CC, 256], F32, kind="ExternalInput")
    d_wgu_u = nc.dram_tensor("wgu_u", [32, P, CC, 256], F32, kind="ExternalInput")
    d_wd = nc.dram_tensor("wd", [8, CC, P, 8, P], F32, kind="ExternalInput")
    d_vec = {}
    for nm in ("tmk", "tmv", "tmr", "tmg", "td", "tf", "lng", "lnb"):
        d_vec[nm] = nc.dram_tensor(nm, [P, CC], F32, kind="ExternalInput")
    d_mask = nc.dram_tensor("mask", [P, 1], F32, kind="ExternalInput")
    d_out = nc.dram_tensor("out", [P, CC, TOWN], F32, kind="ExternalOutput")

    from contextlib import ExitStack
    stack = ExitStack()
    sems = {}
    for s in ("dG", "dS", "dW", "P", "V", "A", "G"):
        sems[s] = stack.enter_context(nc.semaphore(s))

    sc = Sched(nc)

    def sb(name, shape, dt):
        return stack.enter_context(nc.sbuf_tensor(name, shape, dt))

    vecs = {nm: sb("sb_" + nm, [P, CC], F32) for nm in d_vec}
    maskt = sb("maskt", [P, 1], F32)
    wfm = sb("wfm", [P, CC], F32)
    euf = sb("euf", [P, CC], F32)
    e1f = sb("e1f", [P, CC], F32)
    onesm = sb("onesm", [P, 1], BF16)
    onesr = sb("onesr", [1, P], F32)
    stA = sb("stA", [P, CC], F32)
    stB = sb("stB", [P, CC], F32)
    ag = sb("ag", [P, CC * TOWN], BF16)
    SL = [sb("SL0", [P, CC * 256], BF16), sb("SL1", [P, CC * 256], BF16),
          sb("SL2", [P, CC * 256], BF16)]

    R = {}

    def res(name):
        if name not in R:
            R[name] = Res(name)
            R[name].writer = sc.barrier_idx
        return R[name]

    def fence():
        idx = sc.barrier()
        for r in R.values():
            r.writer = idx
            r.readers = []
        return idx

    def cres(buf, cc):
        return res("%s_%d" % (buf, cc))

    def all_res(buf):
        return [cres(buf, i) for i in range(CC)]

    def dma(eng, dst_ap, src_ap, reads=(), writes=(), waits=(), dma_sem=None):
        return sc.op(eng, lambda e, d=dst_ap, s=src_ap: e.dma_start(d, s),
                     reads=reads, writes=writes, waits_ops=waits, dma=True,
                     dma_sem=dma_sem)

    def act(out_ap, in_ap, func, bias=0.0, scale=1.0, reads=(), writes=(), waits=()):
        return sc.op("scalar",
                     lambda e, o=out_ap, i=in_ap, f=func, b=bias, s=scale:
                     e.activation(o, i, f, bias=b, scale=s),
                     reads=reads, writes=writes, waits_ops=waits)

    def vop(fn, reads=(), writes=(), waits=()):
        return sc.op("vector", fn, reads=reads, writes=writes, waits_ops=waits)

    slab_seq = [0]
    slab_pos = {}

    def load_slab_one(dram, key=None):
        k = key if key is not None else id(dram)
        s = slab_pos.get(k, 0)
        slab_pos[k] = s + 1
        par = slab_seq[0] % 3
        slab_seq[0] += 1
        t = dma("gpsimd",
                SL[par][:, :].rearrange("p (c w) -> p c w", w=256),
                dram.ap()[s],
                writes=[res("SL%d" % par)])
        return t, par

    for nm in d_vec:
        dma("sync", vecs[nm][:, :], d_vec[nm].ap(), writes=[res("vec_" + nm)])
    dma("sync", maskt[:, :], d_mask.ap(), writes=[res("maskt")])
    vop(lambda e: e.memset(onesm[:, :], 1.0 / 2048.0), writes=[res("onesm")])
    vop(lambda e: e.memset(onesr[:, :], 1.0), writes=[res("onesr")])
    act(e1f[:, :], vecs["td"][:, :], AF.Exp,
        reads=[res("vec_td")], writes=[res("e1f")])
    act(euf[:, :], vecs["tf"][:, :], AF.Exp,
        reads=[res("vec_tf")], writes=[res("euf")])
    vop(lambda e: e.memset(wfm[:, 0:1], 1.0), writes=[res("wfm")])
    for fn in (AF.Square, AF.Sigmoid, AF.Silu, AF.Ln, AF.Exp):
        act(wfm[:, 0:1], wfm[:, 0:1], fn, writes=[res("wfm")])
    act(wfm[:, :], e1f[:, :], AF.Exp, scale=-1.0,
        reads=[res("e1f")], writes=[res("wfm")])
    fence()

    with ExitStack() as hstack:
        def hsb(name, shape, dt):
            return hstack.enter_context(nc.sbuf_tensor(name, shape, dt))

        def hps(name, shape, dt=F32):
            return hstack.enter_context(nc.psum_tensor(name, shape, dt))

        Lb = hsb("Lb", [P, CC * BW], BF16)
        Eb = hsb("Eb", [P, CC * BW], BF16)
        Vb = hsb("Vb", [P, CC * BW], BF16)
        M1 = hsb("M1", [P, CC * BW], BF16)
        M2 = hsb("M2", [P, CC * BW], BF16)
        R1 = hsb("R1", [P, CC * BW], BF16)
        R3 = hsb("R3", [P, CC * BW], BF16)
        tmix = hsb("tmix", [P, BW], BF16)
        tf1 = [hsb("tf1a", [P, BW], F32), hsb("tf1b", [P, BW], F32)]
        tf2 = [hsb("tf2a", [P, BW], F32), hsb("tf2b", [P, BW], F32)]
        tf3 = hsb("tf3a", [P, BW], F32)
        mub_h = hsb("mub_h", [P, BW], F32)
        rsb_h = hsb("rsb_h", [P, BW], F32)
        hmu_row = hsb("hmu_row", [1, BW], F32)
        hs2_row = hsb("hs2_row", [1, BW], F32)
        hrs_row = hsb("hrs_row", [1, BW], F32)
        pp = [hps("pp0", [P, BW], F32), hps("pp1", [P, BW], F32)]
        ps_mu = hps("ps_mu", [1, BW], F32)
        ps_s2 = hps("ps_s2", [1, BW], F32)

        def kwarm(tok):
            sc.op("tensor", lambda e: e.matmul(pp[0][0:1, 0:1], onesm[:, 0:1],
                                               onesm[:, 0:1], start=True,
                                               stop=True), waits_ops=(tok,))

        def chunk(buf, cc, lo, n, stride=BW):
            return buf[:, cc * stride + lo: cc * stride + lo + n]

        def halo_view(buf):
            return buf[:, :].rearrange("p (c w) -> p c w", w=BW)[:, :, 0:HALO]

        def col_view(buf, col):
            return buf[:, :].rearrange("p (c w) -> p c w", w=BW)[:, :, col:col + 1]

        if dbg:
            for buf, bres in ((Lb, "Lb"), (Eb, "Eb"), (Vb, "Vb"),
                              (M1, "M1"), (M2, "M2"), (R1, "R1"), (R3, "R3")):
                vop(lambda e, buf=buf: e.memset(buf[:, :], 0.0),
                    writes=[res(bres)] + all_res(bres))

        def ln_stats_apply(ln_w, xbf_res, sq_res, xbuf, sqbuf, outbuf,
                           t_seg0=None):
            s0w = min(512, ln_w)
            xv0 = xbuf[:, :].rearrange("p (c w) -> p c w", w=BW)[:, :, 0:s0w]
            sv0 = sqbuf[:, :].rearrange("p (c w) -> p c w", w=BW)[:, :, 0:s0w]
            t_sq0 = act(sv0, xv0, AF.Square, writes=all_res(sq_res),
                        waits=(t_seg0,) if t_seg0 is not None else ())
            if ln_w > s0w:
                xv1 = xbuf[:, :].rearrange("p (c w) -> p c w",
                                           w=BW)[:, :, s0w:ln_w]
                sv1 = sqbuf[:, :].rearrange("p (c w) -> p c w",
                                            w=BW)[:, :, s0w:ln_w]
                act(sv1, xv1, AF.Square, reads=all_res(xbf_res),
                    writes=all_res(sq_res))
            for stat, (srcb, sres, psum, pres) in enumerate(
                    ((xbuf, xbf_res, ps_mu, "ps_mu"),
                     (sqbuf, sq_res, ps_s2, "ps_s2"))):
                for (o, n) in segs(ln_w, 512):
                    fns = []
                    for cci in range(CC):
                        fns.append(lambda e, srcb=srcb, psum=psum, o=o, n=n, cci=cci:
                                   e.matmul(psum[0:1, o:o + n], onesm[:, 0:1],
                                            chunk(srcb, cci, o, n),
                                            start=(cci == 0), stop=(cci == CC - 1)))
                    if o == 0 and stat == 0 and t_seg0 is not None:
                        rds, wts = [res("onesm")], (t_seg0,)
                    elif o == 0 and stat == 1:
                        rds, wts = [res("onesm")], (t_sq0,)
                    else:
                        rds, wts = all_res(sres) + [res("onesm")], ()
                    sc.op("tensor", fns, reads=rds, writes=[res(pres)],
                          waits_ops=wts)
            vop(lambda e: e.tensor_copy(hmu_row[0:1, 0:ln_w], ps_mu[0:1, 0:ln_w]),
                reads=[res("ps_mu")], writes=[res("hmu_row")])
            vop(lambda e: e.tensor_copy(hs2_row[0:1, 0:ln_w], ps_s2[0:1, 0:ln_w]),
                reads=[res("ps_s2")], writes=[res("hs2_row")])
            vop(lambda e: e.tensor_mul(hrs_row[0:1, 0:ln_w], hmu_row[0:1, 0:ln_w],
                                       hmu_row[0:1, 0:ln_w]),
                reads=[res("hmu_row")], writes=[res("hrs_row")])
            vop(lambda e: e.tensor_sub(hs2_row[0:1, 0:ln_w], hs2_row[0:1, 0:ln_w],
                                       hrs_row[0:1, 0:ln_w]),
                reads=[res("hrs_row")], writes=[res("hs2_row")])
            vop(lambda e: e.tensor_single_scalar(
                hs2_row[0:1, 0:ln_w], hs2_row[0:1, 0:ln_w], EPS, OP.add),
                writes=[res("hs2_row")])
            act(hs2_row[0:1, 0:ln_w], hs2_row[0:1, 0:ln_w], AF.Ln,
                writes=[res("hs2_row")])
            act(hrs_row[0:1, 0:ln_w], hs2_row[0:1, 0:ln_w], AF.Exp, scale=-0.5,
                reads=[res("hs2_row")], writes=[res("hrs_row")])
            for row, rres, dstb, dres, psel in (
                    (hmu_row, "hmu_row", mub_h, "mub_h", 0),
                    (hrs_row, "hrs_row", rsb_h, "rsb_h", 1)):
                fns = []
                for (o, n) in segs(ln_w, 512):
                    fns.append(lambda e, row=row, o=o, n=n, psel=psel:
                               e.matmul(pp[psel][0:128, o:o + n], onesr[0:1, 0:P],
                                        row[0:1, o:o + n], start=True, stop=True))
                sc.op("tensor", fns, reads=[res(rres), res("onesr")],
                      writes=[res("pp%d" % psel)])
                vop(lambda e, dstb=dstb, psel=psel, ln_w=ln_w:
                    e.tensor_copy(dstb[:, 0:ln_w], pp[psel][:, 0:ln_w]),
                    reads=[res("pp%d" % psel)], writes=[res(dres)])
            for cci in range(CC):
                vop(lambda e, cci=cci, ln_w=ln_w: e.tensor_sub(
                    tf1[0][:, 0:ln_w], chunk(xbuf, cci, 0, ln_w),
                    mub_h[:, 0:ln_w]),
                    reads=[cres(xbf_res, cci), res("mub_h")],
                    writes=[res("tf1_0")])
                t_ap = vop(lambda e, cci=cci, ln_w=ln_w: e.tensor_mul(
                    chunk(outbuf, cci, 0, ln_w), tf1[0][:, 0:ln_w],
                    rsb_h[:, 0:ln_w]),
                    reads=[res("tf1_0"), res("rsb_h")], writes=[res("Lb")])
                kwarm(t_ap)

        def mix_into(dstbuf, dres, tmname, lnoff, lo, n, h, kw=False):
            for cci in range(CC):
                if h == 0 and lo == 0:
                    vop(lambda e: e.memset(tmix[:, 0:1], 0.0),
                        writes=[res("tmix")])
                    vop(lambda e, cci=cci, n=n: e.tensor_sub(
                        tmix[:, 1:n], chunk(Lb, cci, 0, n - 1),
                        chunk(Lb, cci, 1, n - 1)),
                        reads=[res("Lb")], writes=[res("tmix")])
                else:
                    off = lnoff + lo
                    vop(lambda e, cci=cci, off=off, n=n: e.tensor_sub(
                        tmix[:, 0:n], chunk(Lb, cci, off - 1, n),
                        chunk(Lb, cci, off, n)),
                        reads=[res("Lb")], writes=[res("tmix")])
                t_st = vop(lambda e, cci=cci, tmname=tmname, dstbuf=dstbuf,
                           off=lnoff + lo, n=n:
                           e.scalar_tensor_tensor(
                               chunk(dstbuf, cci, 0, n), tmix[:, 0:n],
                               vecs[tmname][:, cci:cci + 1],
                               chunk(Lb, cci, off, n), OP.mult, OP.add),
                           reads=[res("tmix"), res("vec_" + tmname),
                                  res("Lb")],
                           writes=[cres(dres, cci)])
                if kw:
                    kwarm(t_st)

        def projection(wname, mixbuf, mres, width, evac_fn):
            for s in range(8):
                t_slab, par = load_slab_one(d_w[wname], key=wname)
                for osub in range(2):
                    oc = s * 2 + osub
                    psel = oc % 2
                    fns = []
                    for cci in range(CC):
                        for (o, n) in segs(width, 512):
                            fns.append(lambda e, par=par, cci=cci, osub=osub,
                                       o=o, n=n, psel=psel, mixbuf=mixbuf:
                                       e.matmul(pp[psel][:, o:o + n],
                                                SL[par][:, cci * 256 + osub * P:
                                                        cci * 256 + osub * P + P],
                                                chunk(mixbuf, cci, o, n),
                                                start=(cci == 0),
                                                stop=(cci == CC - 1)))
                    sc.op("tensor", fns,
                          reads=[res("SL%d" % par)] + all_res(mres),
                          writes=[res("pp%d" % psel)], waits_ops=(t_slab,))
                    evac_fn(oc, psel)

        for h in (0, 1):
            for wnm in ("wk", "wv", "wr", "wg"):
                slab_pos.pop(wnm, None)
            ln_w = SW if h == 0 else SW + 1
            lnoff = 0 if h == 0 else 1
            lo = HALO if h == 0 else 0
            now = SW - lo
            gcol0 = 0 if h == 0 else OW0
            d_x = d_xh0 if h == 0 else d_xh1

            s0w = min(512, ln_w)
            t_x1 = dma("gpsimd",
                       Eb[:, :].rearrange("p (c w) -> p c w",
                                          w=BW)[:, :, 0:s0w],
                       d_x.ap()[:, :, 0:s0w], writes=all_res("Eb"))
            if ln_w > s0w:
                dma("gpsimd",
                    Eb[:, :].rearrange("p (c w) -> p c w",
                                       w=BW)[:, :, s0w:ln_w],
                    d_x.ap()[:, :, s0w:ln_w], writes=all_res("Eb"))
            ln_stats_apply(ln_w, "Eb", "Vb", Eb, Vb, Lb, t_x1)

            mix_into(M1, "M1", "tmk", lnoff, 0, SW, h, kw=True)
            mix_into(M2, "M2", "tmv", lnoff, 0, SW, h)
            mix_into(R1, "R1", "tmr", lnoff, lo, now, h if lo == 0 else 1)

            projection("wk", M1, "M1", SW,
                       lambda oc, psel: act(chunk(Eb, oc, 0, SW),
                                            pp[psel][:, 0:SW], AF.Exp,
                                            reads=[res("pp%d" % psel)],
                                            writes=[cres("Eb", oc)]))
            projection("wv", M2, "M2", SW,
                       lambda oc, psel: sc.op(
                           "scalar",
                           lambda e, oc=oc, psel=psel:
                           e.copy(chunk(Vb, oc, 0, SW), pp[psel][:, 0:SW]),
                           reads=[res("pp%d" % psel)], writes=[cres("Vb", oc)]))

            projection("wr", R1, "R1", now,
                       lambda oc, psel: act(chunk(R3, oc, 0, now),
                                            pp[psel][:, 0:now], AF.Sigmoid,
                                            reads=[res("pp%d" % psel)],
                                            writes=[cres("R3", oc)]))

            if h == 0:
                for cci in range(CC):
                    vop(lambda e, cci=cci: e.tensor_single_scalar(
                        Eb[:, cci * BW: cci * BW + HALO],
                        Eb[:, cci * BW: cci * BW + HALO],
                        maskt[:, 0:1], OP.mult),
                        reads=[res("maskt")], writes=[cres("Eb", cci)])

            pend_qa = []
            for cci in range(CC):
                vop(lambda e, cci=cci: e.tensor_mul(
                    chunk(M1, cci, 0, SW), chunk(Eb, cci, 0, SW),
                    chunk(Vb, cci, 0, SW)),
                    reads=[cres("Eb", cci), cres("Vb", cci)],
                    writes=[cres("M1", cci)])
                init_a = 0.0 if h == 0 else stA[:, cci:cci + 1]
                wb = wfm[:, cci:cci + 1].broadcast_to([P, SW])
                pend_qa.append(lambda e=None, cci=cci, init_a=init_a, wb=wb:
                               vop(lambda e, cci=cci, init_a=init_a, wb=wb:
                                   e.tensor_tensor_scan(
                                   chunk(Vb, cci, 0, SW), wb,
                                   chunk(M1, cci, 0, SW), init_a,
                                   OP.mult, OP.add),
                                   reads=[res("wfm"), cres("M1", cci),
                                          res("stA")],
                                   writes=[cres("Vb", cci)]))
                if len(pend_qa) >= 2:
                    pend_qa.pop(0)()

            def scan_pair(cci):
                init_b = 0.0 if h == 0 else stB[:, cci:cci + 1]
                wb = wfm[:, cci:cci + 1].broadcast_to([P, SW])
                vop(lambda e, cci=cci, init_b=init_b, wb=wb:
                    e.tensor_tensor_scan(
                    chunk(M2, cci, 0, SW), wb,
                    chunk(Eb, cci, 0, SW), init_b, OP.mult, OP.add),
                    reads=[res("wfm"), cres("Eb", cci), res("stB")],
                    writes=[cres("M2", cci)])

            def epilogue(cci):
                par = cci % 2
                if lo == 0:
                    vop(lambda e, cci=cci, par=par: e.scalar_tensor_tensor(
                        tf1[par][:, 0:1], M1[:, cci * BW: cci * BW + 1],
                        euf[:, cci:cci + 1], stA[:, cci:cci + 1],
                        OP.mult, OP.add),
                        reads=[cres("M1", cci), res("euf"), res("stA")],
                        writes=[res("tf1_%d" % par)])
                    vop(lambda e, cci=cci, par=par: e.scalar_tensor_tensor(
                        tf2[par][:, 0:1], Eb[:, cci * BW: cci * BW + 1],
                        euf[:, cci:cci + 1], stB[:, cci:cci + 1],
                        OP.mult, OP.add),
                        reads=[cres("Eb", cci), res("euf"), res("stB")],
                        writes=[res("tf2_%d" % par)])
                    o1, n1 = 1, now - 1
                else:
                    o1, n1 = 0, now
                vop(lambda e, cci=cci, par=par, lo=lo, o1=o1, n1=n1:
                    e.scalar_tensor_tensor(
                    tf1[par][:, o1:o1 + n1], chunk(M1, cci, lo + o1, n1),
                    euf[:, cci:cci + 1], chunk(Vb, cci, lo + o1 - 1, n1),
                    OP.mult, OP.add),
                    reads=[cres("M1", cci), cres("Vb", cci), res("euf")],
                    writes=[res("tf1_%d" % par)])
                vop(lambda e, cci=cci, par=par, lo=lo, o1=o1, n1=n1:
                    e.scalar_tensor_tensor(
                    tf2[par][:, o1:o1 + n1], chunk(Eb, cci, lo + o1, n1),
                    euf[:, cci:cci + 1], chunk(M2, cci, lo + o1 - 1, n1),
                    OP.mult, OP.add),
                    reads=[cres("Eb", cci), cres("M2", cci), res("euf")],
                    writes=[res("tf2_%d" % par)])
                act(tf3[:, 0:now], tf2[par][:, 0:now], AF.Ln,
                    reads=[res("tf2_%d" % par)], writes=[res("tf3")])
                act(tf2[par][:, 0:now], tf3[:, 0:now], AF.Exp, scale=-1.0,
                    reads=[res("tf3")], writes=[res("tf2_%d" % par)])

            def wkv_mul(cci):
                par = cci % 2
                vop(lambda e, cci=cci, par=par, lo=lo, now=now: e.tensor_mul(
                    chunk(M1, cci, lo, now), tf1[par][:, 0:now],
                    tf2[par][:, 0:now]),
                    reads=[res("tf1_%d" % par), res("tf2_%d" % par)],
                    writes=[cres("M1", cci)])

            for f in pend_qa:
                f()
            pend_qa = []
            for cci in range(CC):
                scan_pair(cci)
            if h == 0:
                vop(lambda e: e.tensor_copy(
                    stA[:, :],
                    Vb[:, :].rearrange("p (c w) -> p c w", w=BW)[:, :, SW - 1:SW]),
                    reads=all_res("Vb"), writes=[res("stA")])
                vop(lambda e: e.tensor_copy(
                    stB[:, :],
                    M2[:, :].rearrange("p (c w) -> p c w", w=BW)[:, :, SW - 1:SW]),
                    reads=all_res("M2"), writes=[res("stB")])
            pend = []
            for cci in range(CC):
                epilogue(cci)
                for f in pend:
                    f()
                pend = [lambda cci=cci: wkv_mul(cci)]
            for f in pend:
                f()
            mix_into(R1, "R1", "tmg", lnoff, lo, now, h if lo == 0 else 1,
                     kw=True)

            def g_evac(oc, psel, lo=lo, now=now, gcol0=gcol0):
                tpar = 0
                act(tmix[:, 0:now], pp[psel][:, 0:now], AF.Sigmoid,
                    reads=[res("pp%d" % psel)], writes=[res("tmix")])
                vop(lambda e, oc=oc, tpar=tpar, lo=lo, now=now: e.tensor_mul(
                    tf1[tpar][:, 0:now], tmix[:, 0:now],
                    chunk(M1, oc, lo, now)),
                    reads=[res("tmix"), cres("M1", oc)],
                    writes=[res("tf1_%d" % tpar)])
                vop(lambda e, oc=oc, tpar=tpar, gcol0=gcol0, now=now:
                    e.tensor_mul(
                    ag[:, oc * TOWN + gcol0: oc * TOWN + gcol0 + now],
                    tf1[tpar][:, 0:now], chunk(R3, oc, 0, now)),
                    reads=[res("tf1_%d" % tpar), cres("R3", oc)],
                    writes=[res("ag")])
            projection("wg", R1, "R1", now, g_evac)

        if dbg:
            for nm, buf, wdt in (("dbg_ag", ag, CC * TOWN),
                                 ("dbg_M1", M1, CC * BW),
                                 ("dbg_R3", R3, CC * BW),
                                 ("dbg_Lb", Lb, CC * BW),
                                 ("dbg_R1", R1, CC * BW),
                                 ("dbg_M2", M2, CC * BW)):
                dd = nc.dram_tensor(nm, [P, wdt], BF16, kind="ExternalOutput")
                bn = nm.replace("dbg_", "") if nm != "dbg_ag" else "ag"
                dma("sync", dd.ap(), buf[:, :],
                    reads=([res("ag")] if bn == "ag" else all_res(bn)))
        wo_pre = [load_slab_one(d_w["wo"], key="wo") for _ in range(2)]
    fence()

    with ExitStack() as pstack:
        def psb(name, shape, dt):
            return pstack.enter_context(nc.sbuf_tensor(name, shape, dt))

        acc = psb("acc", [P, CC * TOWN], F32)
        tp1 = psb("tp1", [P, TOWN], F32)
        tbf = psb("tbf", [P, TOWN], BF16)
        mu_b = psb("mu_b", [P, TOWN], F32)
        rs_b = psb("rs_b", [P, TOWN], F32)
        mu_row = psb("mu_row2", [1, TOWN], F32)
        s2_row = psb("s2_row2", [1, TOWN], F32)
        rs_row = psb("rs_row2", [1, TOWN], F32)
        wstack = ExitStack()
        attn_o = wstack.enter_context(nc.sbuf_tensor("attn_o", [P, CC * TOWN], BF16))
        qpsum = ExitStack()
        po = [qpsum.enter_context(nc.psum_tensor("po0", [P, TOWN], F32)),
              qpsum.enter_context(nc.psum_tensor("po1", [P, TOWN], F32))]
        ps2_mu = qpsum.enter_context(nc.psum_tensor("ps2_mu", [1, TOWN], F32))
        ps2_s2 = qpsum.enter_context(nc.psum_tensor("ps2_s2", [1, TOWN], F32))

        dma("sync", acc[:, :].rearrange("p (c w) -> p c w", w=TOWN),
            d_xres.ap(), writes=[res("acc")])

        def stream_stats(oc, srcbuf, srcres, start):
            vop(lambda e, oc=oc, srcbuf=srcbuf: e.tensor_copy(
                tbf[:, :], srcbuf[:, oc * TOWN:(oc + 1) * TOWN]),
                reads=[res(srcres)], writes=[res("tbf")])
            for stepi in range(2):
                psum, pres = ((ps2_mu, "ps2_mu"), (ps2_s2, "ps2_s2"))[stepi]
                if stepi == 1:
                    act(tbf[:, :], tbf[:, :], AF.Square, writes=[res("tbf")])
                fns = []
                for (o, n) in segs(TOWN, 512):
                    fns.append(lambda e, psum=psum, o=o, n=n:
                               e.matmul(psum[0:1, o:o + n], onesm[:, 0:1],
                                        tbf[:, o:o + n],
                                        start=start, stop=(oc == CC - 1)))
                sc.op("tensor", fns, reads=[res("tbf"), res("onesm")],
                      writes=[] if not start else [res(pres)],
                      waits_ops=(R[pres].writer,) if (start and R[pres].writer
                                                      is not None) else ())
                if not start:
                    R[pres].writer = sc.ops[-1]["idx"]

        def finalize_rows_and_bcast():
            vop(lambda e: e.tensor_copy(mu_row[0:1, 0:TOWN], ps2_mu[0:1, 0:TOWN]),
                reads=[res("ps2_mu")], writes=[res("mu_row")])
            vop(lambda e: e.tensor_copy(s2_row[0:1, 0:TOWN], ps2_s2[0:1, 0:TOWN]),
                reads=[res("ps2_s2")], writes=[res("s2_row")])
            vop(lambda e: e.tensor_mul(rs_row[0:1, 0:TOWN], mu_row[0:1, 0:TOWN],
                                       mu_row[0:1, 0:TOWN]),
                reads=[res("mu_row")], writes=[res("rs_row")])
            vop(lambda e: e.tensor_sub(s2_row[0:1, 0:TOWN], s2_row[0:1, 0:TOWN],
                                       rs_row[0:1, 0:TOWN]),
                reads=[res("rs_row")], writes=[res("s2_row")])
            vop(lambda e: e.tensor_single_scalar(
                s2_row[0:1, 0:TOWN], s2_row[0:1, 0:TOWN], EPS, OP.add),
                writes=[res("s2_row")])
            act(s2_row[0:1, 0:TOWN], s2_row[0:1, 0:TOWN], AF.Ln,
                writes=[res("s2_row")])
            act(rs_row[0:1, 0:TOWN], s2_row[0:1, 0:TOWN], AF.Exp, scale=-0.5,
                reads=[res("s2_row")], writes=[res("rs_row")])
            for row, rres, dstb, dres, psel in (
                    (mu_row, "mu_row", mu_b, "mu_b", 0),
                    (rs_row, "rs_row", rs_b, "rs_b", 1)):
                fns = []
                for (o, n) in segs(TOWN, 512):
                    fns.append(lambda e, row=row, o=o, n=n, psel=psel:
                               e.matmul(po[psel][0:128, o:o + n], onesr[0:1, 0:P],
                                        row[0:1, o:o + n], start=True, stop=True))
                sc.op("tensor", fns, reads=[res(rres), res("onesr")],
                      writes=[res("po%d" % psel)])
                vop(lambda e, dstb=dstb, psel=psel:
                    e.tensor_copy(dstb[:, 0:TOWN], po[psel][:, 0:TOWN]),
                    reads=[res("po%d" % psel)], writes=[res(dres)])

        pend_stats = []
        for s in range(8):
            t_slab, par = (wo_pre[s] if s < 2 else
                           load_slab_one(d_w["wo"], key="wo"))
            for osub in range(2):
                oc = s * 2 + osub
                psel = oc % 2
                fns = []
                for cci in range(CC):
                    for (o, n) in segs(TOWN, 512):
                        fns.append(lambda e, par=par, cci=cci, osub=osub, o=o,
                                   n=n, psel=psel:
                                   e.matmul(po[psel][:, o:o + n],
                                            SL[par][:, cci * 256 + osub * P:
                                                    cci * 256 + osub * P + P],
                                            ag[:, cci * TOWN + o:
                                               cci * TOWN + o + n],
                                            start=(cci == 0),
                                            stop=(cci == CC - 1)))
                sc.op("tensor", fns, reads=[res("SL%d" % par), res("ag")],
                      writes=[res("po%d" % psel)], waits_ops=(t_slab,))
                sc.op("scalar", lambda e, oc=oc, psel=psel:
                      e.copy(attn_o[:, oc * TOWN:(oc + 1) * TOWN],
                             po[psel][:, 0:TOWN]),
                      reads=[res("po%d" % psel)], writes=[res("attn_o")])
                if len(pend_stats) >= 2:
                    pend_stats.pop(0)()
                pend_stats.append(lambda oc=oc: stream_stats(
                    oc, attn_o, "attn_o", start=(oc == 0)))
        for f in pend_stats:
            f()
        finalize_rows_and_bcast()
        for cci in range(CC):
            sc.op("gpsimd", lambda e, cci=cci: e.tensor_sub(
                tp1[:, :], attn_o[:, cci * TOWN:(cci + 1) * TOWN], mu_b[:, :]),
                reads=[res("attn_o"), res("mu_b")], writes=[res("tp1")])
            vop(lambda e: e.tensor_mul(tp1[:, :], tp1[:, :], rs_b[:, :]),
                reads=[res("rs_b")], writes=[res("tp1")])
            vop(lambda e, cci=cci: e.tensor_scalar(
                tp1[:, :], tp1[:, :], vecs["lng"][:, cci:cci + 1],
                vecs["lnb"][:, cci:cci + 1], OP.mult, OP.add),
                reads=[res("vec_lng"), res("vec_lnb")], writes=[res("tp1")])
            vop(lambda e, cci=cci: e.tensor_add(
                acc[:, cci * TOWN:(cci + 1) * TOWN],
                acc[:, cci * TOWN:(cci + 1) * TOWN], tp1[:, :]),
                reads=[res("tp1")], writes=[res("acc")])
            stream_stats(cci, acc, "acc", start=(cci == 0))

        if dbg:
            dd = nc.dram_tensor("dbg_attn_o", [P, CC * TOWN], BF16,
                                kind="ExternalOutput")
            dma("sync", dd.ap(), attn_o[:, :], reads=[res("attn_o")])
        wstack.close()
        fence()
        ln2 = psb("ln2", [P, CC * TOWN], BF16)
        ffn_pre = {"g": load_slab_one(d_wgu_g, key="g"),
                   "u": load_slab_one(d_wgu_u, key="u")}
        finalize_rows_and_bcast()
        for cci in range(CC):
            sc.op("gpsimd", lambda e, cci=cci: e.tensor_sub(
                tp1[:, :], acc[:, cci * TOWN:(cci + 1) * TOWN], mu_b[:, :]),
                reads=[res("acc"), res("mu_b")], writes=[res("tp1")])
            t_l2 = vop(lambda e, cci=cci: e.tensor_mul(
                ln2[:, cci * TOWN:(cci + 1) * TOWN], tp1[:, :], rs_b[:, :]),
                reads=[res("tp1"), res("rs_b")], writes=[res("ln2")])
            sc.op("tensor", lambda e: e.matmul(po[0][0:1, 0:1], onesm[:, 0:1],
                                               onesm[:, 0:1], start=True,
                                               stop=True), waits_ops=(t_l2,))
        if dbg:
            dd = nc.dram_tensor("dbg_ln2", [P, CC * TOWN], BF16,
                                kind="ExternalOutput")
            dma("sync", dd.ap(), ln2[:, :], reads=[res("ln2")])
        qpsum.close()

        with ExitStack() as fstack:
            def fsb(name, shape, dt):
                return fstack.enter_context(nc.sbuf_tensor(name, shape, dt))

            def fps(name, shape, dt=F32):
                return fstack.enter_context(nc.psum_tensor(name, shape, dt))

            hs = [ag[:, 0:8 * TOWN], ag[:, 8 * TOWN:16 * TOWN]]
            wdb = [fsb("wdb0", [P, 8 * P], BF16), fsb("wdb1", [P, 8 * P], BF16)]
            th2 = [fsb("th0", [P, TOWN], BF16), fsb("th1", [P, TOWN], BF16)]
            pg = fps("pg", [P, TOWN], F32)
            pu = fps("pu", [P, TOWN], F32)
            pd = [fps("pd0", [P, TOWN], F32), fps("pd1", [P, TOWN], F32)]
            def emit_down(s, hc):
                hpar = s % 2
                wpar = hc % 2
                dma("gpsimd",
                    wdb[wpar][:, :].rearrange("p (i w) -> p i w", w=P),
                    d_wd.ap()[s][hc], writes=[res("wdb%d" % wpar)])
                fns = []
                for ic in range(8):
                    for (o, n) in segs(TOWN, 512):
                        fns.append(lambda e, wpar=wpar, ic=ic, o=o, n=n,
                                   hpar=hpar:
                                   e.matmul(pd[wpar][:, o:o + n],
                                            wdb[wpar][:, ic * P:(ic + 1) * P],
                                            hs[hpar][:, ic * TOWN + o:
                                                     ic * TOWN + o + n],
                                            start=(ic == 0), stop=(ic == 7)))
                sc.op("tensor", fns,
                      reads=[res("wdb%d" % wpar), res("hs%d" % hpar)],
                      writes=[res("pd%d" % wpar)])
                t_add = vop(lambda e, hc=hc, wpar=wpar: e.tensor_add(
                    acc[:, hc * TOWN:(hc + 1) * TOWN],
                    acc[:, hc * TOWN:(hc + 1) * TOWN], pd[wpar][:, :]),
                    reads=[res("pd%d" % wpar)], writes=[res("acc")])
                if s == 7:
                    acc_final[hc] = t_add

            acc_final = {}
            down_pend = []
            for s in range(8):
                hpar = s % 2
                for ic in range(8):
                    gic = 8 * s + ic
                    slab = gic // 2
                    within = gic % 2
                    if within == 0:
                        if slab == 0:
                            sc._ffn_cur = ffn_pre
                        else:
                            tg_, parg = load_slab_one(d_wgu_g, key="g")
                            tu_, paru = load_slab_one(d_wgu_u, key="u")
                            sc._ffn_cur = {"g": (tg_, parg), "u": (tu_, paru)}
                    cur = sc._ffn_cur
                    for which in ("g", "u"):
                        t_slab, par = cur[which]
                        psum, pres = (pg, "pg") if which == "g" else (pu, "pu")
                        fns = []
                        for cci in range(CC):
                            for (o, n) in segs(TOWN, 512):
                                fns.append(lambda e, par=par, cci=cci,
                                           within=within, o=o, n=n, psum=psum:
                                           e.matmul(psum[:, o:o + n],
                                                    SL[par][:, cci * 256
                                                            + within * P:
                                                            cci * 256
                                                            + within * P + P],
                                                    ln2[:, cci * TOWN + o:
                                                        cci * TOWN + o + n],
                                                    start=(cci == 0),
                                                    stop=(cci == CC - 1)))
                        sc.op("tensor", fns,
                              reads=[res("SL%d" % par), res("ln2")],
                              writes=[res(pres)], waits_ops=(t_slab,))
                    tpar = ic % 2
                    act(th2[tpar][:, :], pg[:, :], AF.Silu,
                        reads=[res("pg")], writes=[res("th_%d" % tpar)])
                    vop(lambda e, hpar=hpar, ic=ic, tpar=tpar: e.tensor_mul(
                        hs[hpar][:, ic * TOWN:(ic + 1) * TOWN],
                        th2[tpar][:, :], pu[:, :]),
                        reads=[res("th_%d" % tpar), res("pu")],
                        writes=[res("hs%d" % hpar)])
                    for _ in range(2):
                        if down_pend:
                            emit_down(*down_pend.pop(0))
                down_pend.extend((s, hc) for hc in range(CC))
            for s_hc in down_pend:
                emit_down(*s_hc)

        t_outs = []
        for hc in range(CC):
            t_outs.append(dma("sync", d_out.ap()[:, hc],
                              acc[:, hc * TOWN:(hc + 1) * TOWN],
                              waits=(acc_final[hc],)))
        sc.op("sync", lambda e: e.nop(), waits_ops=tuple(t_outs))

    sc.emit(sems)
    nc.finalize()
    stack.close()
    return nc



def _swizzle_w(wT, oslab):
    cin, o = wT.shape
    return np.ascontiguousarray(
        wT.reshape(CC, P, o // oslab, oslab).transpose(2, 1, 0, 3))


def _vec_fm(v):
    return np.ascontiguousarray(v.reshape(CC, P).T)


def _x_fm(xs):
    t = xs.shape[0]
    return np.ascontiguousarray(xs.T.reshape(CC, P, t).transpose(1, 0, 2))


def kernel(L0_attn_x, L0_attn_time_decay, L0_attn_time_first, L0_attn_tm_k,
           L0_attn_tm_v, L0_attn_tm_r, L0_attn_tm_g, L0_attn_Wr, L0_attn_Wk,
           L0_attn_Wv, L0_attn_Wg, L0_attn_Wo, L0_attn_ln_g, L0_attn_ln_b,
           L0_ffn_w_gate_up, L0_ffn_w_down):
    TOWN = 1024
    TH = TOWN + HALO
    SW = TH // 2

    x = np.asarray(L0_attn_x, dtype=np.float32)[0]
    T = x.shape[0]
    assert T == NCORES * TOWN

    wk = _swizzle_w(np.ascontiguousarray(np.asarray(L0_attn_Wk, np.float32).T), 256)
    wv = _swizzle_w(np.ascontiguousarray(np.asarray(L0_attn_Wv, np.float32).T), 256)
    wr = _swizzle_w(np.ascontiguousarray(np.asarray(L0_attn_Wr, np.float32).T), 256)
    wg = _swizzle_w(np.ascontiguousarray(np.asarray(L0_attn_Wg, np.float32).T), 256)
    wo = _swizzle_w(np.ascontiguousarray(np.asarray(L0_attn_Wo, np.float32).T), 256)
    wguT = np.ascontiguousarray(np.asarray(L0_ffn_w_gate_up, np.float32).T)
    wgu_g = _swizzle_w(np.ascontiguousarray(wguT[:, 0:INTER]), 256)
    wgu_u = _swizzle_w(np.ascontiguousarray(wguT[:, INTER:]), 256)
    wdT = np.ascontiguousarray(np.asarray(L0_ffn_w_down, np.float32).T)
    wd = np.ascontiguousarray(
        wdT.reshape(8, 8, P, CC, P).transpose(0, 3, 2, 1, 4))

    vec_common = {
        "tmk": _vec_fm(np.asarray(L0_attn_tm_k, np.float32)),
        "tmv": _vec_fm(np.asarray(L0_attn_tm_v, np.float32)),
        "tmr": _vec_fm(np.asarray(L0_attn_tm_r, np.float32)),
        "tmg": _vec_fm(np.asarray(L0_attn_tm_g, np.float32)),
        "td": _vec_fm(np.asarray(L0_attn_time_decay, np.float32)),
        "tf": _vec_fm(np.asarray(L0_attn_time_first, np.float32)),
        "lng": _vec_fm(np.asarray(L0_attn_ln_g, np.float32)),
        "lnb": _vec_fm(np.asarray(L0_attn_ln_b, np.float32)),
    }
    w_common = {"wk": wk, "wv": wv, "wr": wr, "wg": wg, "wo": wo,
                "wgu_g": wgu_g, "wgu_u": wgu_u, "wd": wd}

    in_maps = []
    for c in range(NCORES):
        base = c * TOWN
        if c == 0:
            halo_rows = np.concatenate([x[0:HALO - 1], x[1:2]], axis=0)
            mask = np.zeros((P, 1), np.float32)
        else:
            halo_rows = x[base - HALO: base]
            mask = np.ones((P, 1), np.float32)
        xa = np.concatenate([halo_rows, x[base: base + TOWN]], axis=0)
        m = {
            "x_h0": _x_fm(xa[0:SW]),
            "x_h1": _x_fm(xa[SW - 1: TH]),
            "x_res": _x_fm(xa[HALO: TH]),
            "mask": mask,
        }
        m.update(vec_common)
        m.update(w_common)
        in_maps.append(m)

    nc = build_program(TOWN)
    r = run_bass_kernel_spmd(nc, in_maps, core_ids=list(range(NCORES)))
    outs = []
    for c in range(NCORES):
        o = np.asarray(r.results[c]["out"])
        outs.append(o.transpose(1, 0, 2).reshape(C, TOWN).T)
    full = np.concatenate(outs, axis=0)[None]
    return full.astype(np.float32)
